# version 03
# baseline (speedup 1.0000x reference)
import math
from contextlib import ExitStack

import numpy as np
import ml_dtypes

import concourse.bass as bass
import concourse.mybir as mybir
from concourse.bass_utils import run_bass_kernel_spmd

F32 = mybir.dt.float32
BF16 = mybir.dt.bfloat16
AF = mybir.ActivationFunctionType
ALU = mybir.AluOpType
AX = mybir.AxisListType

ENG_ALL = ("sync", "tensor", "vector", "scalar", "gpsimd")


class P:

    def __init__(self, name="prog"):
        self.nc = bass.Bass("TRN2", target_bir_lowering=False)
        self.ops = {e: [] for e in ENG_ALL}
        self.counts = {}
        self.watermark = {e: {} for e in ENG_ALL}
        self.es = ExitStack()
        self.scopes = [self.es]
        self.pending = {e: [] for e in ENG_ALL}
        self.tensors = {}
        self.n_waits = 0
        self.n_ops = 0

    def push_scope(self):
        self.scopes.append(ExitStack())

    def pop_scope(self):
        self.scopes.pop().close()

    def barrier(self):
        snap = [(sn, v) for sn, v in self.counts.items()]
        for e in ENG_ALL:
            self.pending[e] = list(snap)

    def dram(self, name, shape, dtype=F32, kind="ExternalInput"):
        t = self.nc.dram_tensor(name, list(shape), dtype, kind=kind).ap()
        self.tensors[name] = t
        return t

    sb_bytes = 0

    def sb(self, name, shape, dtype):
        t = self.scopes[-1].enter_context(self.nc.sbuf_tensor(name, list(shape), dtype))
        self.tensors[name] = t
        n = 1
        for s in shape[1:]:
            n *= s
        self.sb_bytes += n * mybir.dt.size(dtype)
        return t

    def ps(self, name, shape, dtype=F32):
        t = self.es.enter_context(self.nc.psum_tensor(name, list(shape), dtype))
        self.tensors[name] = t
        return t

    def _sem(self, name):
        if name not in self.counts:
            self.counts[name] = 0

    def op(self, eng, fn, deps=(), token=False, dma=False, dma_group=""):
        waits = []
        if self.pending[eng]:
            deps = list(deps) + self.pending[eng]
            self.pending[eng] = []
        for d in deps:
            if d is None:
                continue
            sn, v = d
            if sn.startswith("D"):
                v = self.counts[sn]
            wm = self.watermark[eng].get(sn, 0)
            if v > wm:
                waits.append((sn, v))
                self.watermark[eng][sn] = v
        inc = None
        tok = None
        if dma:
            sn = "D" + eng + dma_group
            self._sem(sn)
            self.counts[sn] += 16
            inc = (sn, 16)
            tok = (sn, self.counts[sn])
        elif token:
            sn = "C" + eng
            self._sem(sn)
            self.counts[sn] += 1
            inc = (sn, 1)
            tok = (sn, self.counts[sn])
        self.ops[eng].append((waits, fn, inc))
        self.n_waits += len(waits)
        self.n_ops += 1
        return tok

    def dma(self, eng, out_ap, in_ap, deps=(), group=""):
        return self.op(eng, lambda E: E.dma_start(out=out_ap, in_=in_ap),
                       deps=deps, dma=True, dma_group=group)

    def build(self):
        nc = self.nc
        sems = {}
        for name in self.counts:
            sems[name] = self.es.enter_context(nc.semaphore(name))
        blk = self.es.enter_context(nc.Block())
        for eng in ENG_ALL:
            if not self.ops[eng]:
                continue
            dec = getattr(blk, eng)

            def body(E, eng=eng):
                for waits, fn, inc in self.ops[eng]:
                    for (sn, v) in waits:
                        E.wait_ge(sems[sn], v)
                    instr = fn(E)
                    if inc is not None:
                        instr.then_inc(sems[inc[0]], inc[1])

            dec(body)
        self.es.close()
        return nc



def pretile_lhs(w, transpose=False):
    if transpose:
        w = w.T
    K, N = w.shape
    nk = (K + 127) // 128
    nm = (N + 127) // 128
    wp = np.zeros((nk * 128, nm * 128), np.float32)
    wp[:K, :N] = w
    t = wp.reshape(nk, 128, nm, 128).transpose(2, 0, 1, 3)
    return np.ascontiguousarray(t)


def bf16(x):
    return np.asarray(x).astype(ml_dtypes.bfloat16)



def emit_weight_stream(p, w_name, nm, nk, wq="sync", conv_eng="vector",
                       scale_col=None, scale_tok=None, nbuf=2, tag=None,
                       m_list=None, c=None, w_deps=(), bufset=0, k_off=0):
    nc = p.nc
    tag = tag or w_name
    w_h = p.tensors[w_name]
    ms = list(range(nm)) if m_list is None else list(m_list)
    nm = len(ms)
    nfb = 2 * nbuf
    if c is not None:
        f32b = (c.wf32 if bufset == 0 else c.wf32b)[:nfb]
        bfb = [b[:, 0:nk * 128] for b in (c.wbf if bufset == 0 else c.wbfb)[:nbuf]]
    else:
        f32b = [p.sb(f"{tag}_f{i}", [128, 128], F32) for i in range(nfb)]
        bfb = [p.sb(f"{tag}_b{i}", [128, nk * 128], BF16) for i in range(nbuf)]
    state = {"conv_tok": {}, "conv_m_tok": [None] * nm,
             "use_tok": [None] * nm, "issued": -1}

    def issue(mi):
        i = mi % nbuf
        m = ms[mi]
        for k in range(nk):
            j = (mi * nk + k) % nfb
            deps = list(w_deps) if mi * nk + k < nfb else []
            ct = state["conv_tok"].get(mi * nk + k - nfb)
            if ct is not None:
                deps.append(ct)
            dtok = p.dma(wq, f32b[j][:], w_h[m, k + k_off], deps=deps,
                         group=f"wb{bufset}_{(mi * nk + k) % 4}")
            cdeps = [dtok]
            if k == 0:
                if mi - nbuf >= 0:
                    cdeps.append(state["use_tok"][mi - nbuf])
                elif mi < nbuf:
                    cdeps.extend(w_deps)
                if scale_tok is not None:
                    cdeps.append(scale_tok)
            sc = scale_col(k) if scale_col is not None else None
            dst = bfb[i][:, k * 128:(k + 1) * 128]
            if sc is not None:
                tok = p.op(conv_eng, lambda E, j=j, dst=dst, sc=sc: E.tensor_scalar(
                    dst, f32b[j][:], sc, None, ALU.mult, ALU.bypass),
                    deps=cdeps, token=True)
            else:
                tok = p.op(conv_eng, lambda E, j=j, dst=dst: E.tensor_copy(
                    dst, f32b[j][:]), deps=cdeps, token=True)
            state["conv_tok"][mi * nk + k] = tok
        state["conv_m_tok"][mi] = tok

    def get(mi):
        while state["issued"] < min(mi + nbuf - 1, nm - 1):
            state["issued"] += 1
            issue(state["issued"])
        return bfb[mi % nbuf], state["conv_m_tok"][mi]

    def post_use(mi, tok):
        state["use_tok"][mi] = tok

    return get, post_use


class NS:
    pass


def common_setup(p, n_scr32=2, n_scrb=4, nk_max=16):
    c = NS()
    c.bank2 = [p.ps(f"bank2_{i}", [128, 1024], F32) for i in range(4)]
    c.bank = [c.bank2[i // 2][:, (i % 2) * 512:(i % 2) * 512 + 512] for i in range(8)]
    c.wf32 = [p.sb(f"c_wf32_{i}", [128, 128], F32) for i in range(4)]
    c.wbf = [p.sb(f"c_wbf_{i}", [128, nk_max * 128], BF16) for i in range(2)]
    c.wf32b = [p.sb(f"c_wf32b_{i}", [128, 128], F32) for i in range(4)]
    c.wbfb = [p.sb(f"c_wbfb_{i}", [128, 16 * 128], BF16) for i in range(2)]
    c.rowtmp = p.sb("c_rowtmp", [1, 512], F32)
    c.rowtmp2 = p.sb("c_rowtmp2", [1, 512], F32)
    c.ident = p.sb("c_ident", [128, 128], F32)
    c.ident_bf = p.sb("c_identbf", [128, 128], BF16)
    c.ones_bf = p.sb("c_onesbf", [1, 128], BF16)
    c.ones_col = p.sb("c_onescol", [128, 1], BF16)
    c.scr32 = [p.sb(f"c_scr32_{i}", [128, 1024], F32) for i in range(n_scr32)]
    c.scrb = [p.sb(f"c_scrb_{i}", [128, 2048], BF16) for i in range(n_scrb)]
    c.rowbf = c.scrb[2][0:1, :]
    c._cc = {}

    def cc(v):
        if v not in c._cc:
            i = len(c._cc)
            t = p.sb(f"c_cc{i}", [128, 1], F32)
            tok = p.op("vector", lambda E, t=t, v=v: E.memset(t[:], v), token=True)
            c._cc[v] = (t, tok)
        return c._cc[v]
    c.cc = cc
    cc(1e-5)
    cc(1.0)
    t0 = p.op("gpsimd", lambda E: E.iota(c.ident[:], pattern=[[-1, 128]], base=0,
                                         channel_multiplier=1,
                                         allow_small_or_imprecise_dtypes=True),
              token=True)
    t1 = p.op("vector", lambda E: E.tensor_single_scalar(
        c.ident[:], c.ident[:], 0.0, ALU.is_equal), deps=[t0], token=True)
    t2 = p.op("vector", lambda E: E.tensor_copy(c.ident_bf[:], c.ident[:]),
              deps=[t1], token=True)
    p.op("vector", lambda E: E.memset(c.ones_bf[:], 1.0), token=False)
    t3 = p.op("vector", lambda E: E.memset(c.ones_col[:], 1.0), token=True)
    c.ident_tok = t2
    c.ones_tok = t3
    return c


def emit_bcast_row(p, c, row_ap, out_bf_tiles_and_cols, deps, banks=(0, 1),
                   evac_eng="vector", T=2048):
    ntt = T // 512
    toks = []
    for tt in range(ntt):
        b = c.bank[banks[tt % len(banks)]]
        mt = p.op("tensor", lambda E, tt=tt, b=b: E.matmul(
            b[:], c.ones_bf[:], row_ap[:, tt * 512:(tt + 1) * 512],
            start=True, stop=True),
            deps=list(deps) + [c.ones_tok] + ([toks[tt - len(banks)]] if tt >= len(banks) else []),
            token=True)
        et = p.op(evac_eng, lambda E, tt=tt, b=b: E.tensor_copy(
            out_bf_tiles_and_cols[tt], b[:]), deps=[mt], token=True)
        toks.append(et)
    return toks


def emit_ln_fm(p, c, x_h, xhat_tiles, xq="gpsimd", eps=1e-5, T=2048,
               a_b=None, b_b=None, extra_rms=False, deps=(), tag="ln",
               plain_rows=None):
    nk = 16
    ntt = T // 512
    st = c.scr32
    sq = c.scrb
    r = NS()
    r.abf = c.rowbf[0:1, 0:T]
    r.bbf = c.rowbf[0:1, 0:T]
    if a_b is None:
        a_b = p.sb(f"{tag}_ab", [128, T], BF16)
        b_b = p.sb(f"{tag}_bb", [128, T], BF16)
    cast_toks = {}
    sq_toks = {}
    smm_toks = {}
    tok = None
    for k in range(nk):
        for h in range(2):
            n = 2 * k + h
            i = n % 2
            dd = [cast_toks.get(n - 2), sq_toks.get(n - 2)] + (list(deps) if n == 0 else [])
            xt = p.dma(xq, st[i][:], x_h[k * 128:(k + 1) * 128, h * 1024:(h + 1) * 1024],
                       deps=dd, group=f"ln{i}")
            cast_toks[n] = p.op("vector", lambda E, i=i: E.tensor_copy(
                sq[2][:, i * 1024:(i + 1) * 1024], st[i][:]),
                deps=[xt, smm_toks.get(n - 2)], token=True)
            sq_toks[n] = p.op("scalar", lambda E, i=i: E.activation(
                out=sq[3][:, i * 1024:(i + 1) * 1024], in_=st[i][:], func=AF.Square),
                deps=[xt, smm_toks.get(n - 2)], token=True)
            for u in range(2):
                tt = 2 * h + u
                p.op("tensor", lambda E, k=k, i=i, tt=tt, u=u: E.matmul(
                    c.bank[tt][0:1, :], c.ones_col[:],
                    sq[2][:, i * 1024 + u * 512: i * 1024 + (u + 1) * 512],
                    start=(k == 0), stop=(k == nk - 1)),
                    deps=[cast_toks[n], c.ones_tok] if u == 0 else ())
                tok = p.op("tensor", lambda E, k=k, i=i, tt=tt, u=u: E.matmul(
                    c.bank[4 + tt][0:1, :], c.ones_col[:],
                    sq[3][:, i * 1024 + u * 512: i * 1024 + (u + 1) * 512],
                    start=(k == 0), stop=(k == nk - 1)),
                    deps=[sq_toks[n]] if u == 0 else (),
                    token=(u == 1))
            smm_toks[n] = tok
    mm_tok = tok
    tmp = c.rowtmp[0:1, :]
    tmp2 = c.rowtmp2[0:1, :]
    tmp_free2 = None
    epsc, epst = c.cc(eps)
    n_ = 2048.0
    abf_toks = []
    b_toks = []
    tmp_free = None
    for tt in range(ntt):
        s_pc = c.bank[tt][0:1, :]
        q_pc = c.bank[4 + tt][0:1, :]
        t1 = p.op("scalar", lambda E, s_pc=s_pc: E.mul(s_pc, s_pc, 1.0 / n_),
                  deps=[mm_tok], token=True)
        t2 = p.op("scalar", lambda E, q_pc=q_pc: E.mul(q_pc, q_pc, 1.0 / n_),
                  deps=[mm_tok], token=True)
        t3 = p.op("scalar", lambda E, s_pc=s_pc: E.activation(
            out=tmp, in_=s_pc, func=AF.Square), deps=[t1, tmp_free], token=True)
        t4 = p.op("vector", lambda E, q_pc=q_pc: E.tensor_sub(q_pc, q_pc, tmp),
                  deps=[t2, t3], token=True)
        t5 = p.op("scalar", lambda E, q_pc=q_pc: E.activation(
            out=q_pc, in_=q_pc, func=AF.Ln, bias=epsc[0:1, :]),
            deps=[t4, epst], token=True) if not extra_rms else None
        if extra_rms:
            t4b = p.op("vector", lambda E, q_pc=q_pc: E.tensor_copy(tmp, q_pc),
                       deps=[t4], token=True)
            t5 = p.op("scalar", lambda E, q_pc=q_pc: E.activation(
                out=q_pc, in_=q_pc, func=AF.Ln, bias=epsc[0:1, :]),
                deps=[t4b, epst], token=True)
        t6 = p.op("scalar", lambda E, q_pc=q_pc: E.activation(
            out=q_pc, in_=q_pc, func=AF.Exp, scale=-0.5), deps=[t5], token=True)
        t6p = t6
        if plain_rows is not None:
            a0r, b0r = plain_rows
            pa = p.op("scalar", lambda E, q_pc=q_pc, tt=tt: E.copy(
                a0r[0:1, tt * 512:(tt + 1) * 512], q_pc), deps=[t6p], token=True)
            pb1 = p.op("vector", lambda E, s_pc=s_pc, tt=tt: E.tensor_mul(
                tmp2, s_pc, a0r[0:1, tt * 512:(tt + 1) * 512]), deps=[pa, tmp_free2],
                token=True)
            pb2 = p.op("vector", lambda E: E.tensor_scalar(
                tmp2, tmp2, -1.0, None, ALU.mult, ALU.bypass), deps=[pb1], token=True)
            tmp_free2 = p.op("vector", lambda E, tt=tt: E.tensor_copy(
                b0r[0:1, tt * 512:(tt + 1) * 512], tmp2), deps=[pb2], token=True)
            t6 = p.op("scalar", lambda E, q_pc=q_pc: E.copy(q_pc, q_pc),
                      deps=[pa], token=True) if False else t6
        if extra_rms:
            t7 = p.op("vector", lambda E, q_pc=q_pc: E.tensor_mul(tmp, tmp, q_pc),
                      deps=[t6], token=True)
            t8 = p.op("vector", lambda E, q_pc=q_pc: E.tensor_mul(tmp, tmp, q_pc),
                      deps=[t7], token=True)
            t9 = p.op("scalar", lambda E: E.activation(
                out=tmp, in_=tmp, func=AF.Ln, bias=epsc[0:1, :]), deps=[t8], token=True)
            t10 = p.op("scalar", lambda E: E.activation(
                out=tmp, in_=tmp, func=AF.Exp, scale=-0.5), deps=[t9], token=True)
            t6 = p.op("vector", lambda E, q_pc=q_pc: E.tensor_mul(q_pc, q_pc, tmp),
                      deps=[t10], token=True)
        tmp_free = t6
        ab = p.op("scalar", lambda E, q_pc=q_pc, tt=tt: E.copy(
            r.abf[:, tt * 512:(tt + 1) * 512], q_pc), deps=[t6], token=True)
        abf_toks.append(ab)
        tb1 = p.op("vector", lambda E, s_pc=s_pc, tt=tt: E.tensor_mul(
            s_pc, s_pc, r.abf[:, tt * 512:(tt + 1) * 512]), deps=[ab], token=True)
        tb2 = p.op("vector", lambda E, s_pc=s_pc: E.tensor_scalar(
            s_pc, s_pc, -1.0, None, ALU.mult, ALU.bypass), deps=[tb1], token=True)
        b_toks.append(tb2)
    ta = emit_bcast_row(p, c, r.abf, [a_b[:, tt * 512:(tt + 1) * 512] for tt in range(4)],
                        deps=abf_toks + b_toks, banks=(4, 5))
    cb_toks = []
    for tt in range(ntt):
        cb_toks.append(p.op("scalar", lambda E, tt=tt: E.copy(
            r.bbf[:, tt * 512:(tt + 1) * 512], c.bank[tt][0:1, :]),
            deps=[b_toks[tt], ta[-1]], token=True))
    tb = emit_bcast_row(p, c, r.bbf, [b_b[:, tt * 512:(tt + 1) * 512] for tt in range(4)],
                        deps=cb_toks, banks=(4, 5))
    bc_tok = [ta[-1], tb[-1]]
    ap_toks = {}
    m1_toks = {}
    for k in range(nk):
        for h in range(2):
            n = 2 * k + h
            i = n % 2
            dd2 = [m1_toks.get(n - 2)] if n >= 2 else [
                cast_toks[2 * nk - 2 + n], sq_toks[2 * nk - 2 + n]]
            xt = p.dma(xq, st[i][:], x_h[k * 128:(k + 1) * 128, h * 1024:(h + 1) * 1024],
                       deps=dd2, group=f"ln{i}")
            m1_toks[n] = p.op("vector", lambda E, k=k, h=h, i=i: E.tensor_mul(
                sq[2][:, i * 1024:(i + 1) * 1024], st[i][:],
                a_b[:, h * 1024:(h + 1) * 1024]),
                deps=[xt] + bc_tok + [ap_toks.get(n - 2)], token=True)
            ap_toks[n] = p.op("vector", lambda E, k=k, h=h, i=i: E.tensor_add(
                xhat_tiles[k][:, h * 1024:(h + 1) * 1024],
                sq[2][:, i * 1024:(i + 1) * 1024],
                b_b[:, h * 1024:(h + 1) * 1024]), deps=[m1_toks[n]], token=True)
    r.a_b, r.b_b = a_b, b_b
    r.bank_free_tok = tb[-1]
    return [ap_toks[2 * k + 1] for k in range(nk)], r


def emit_gemm_multi(p, streams, nm, out_cb, T=2048, banks=None, act_tok=None,
                    w_deps=(), m_list=None, c=None, wq="sync", conv_eng="vector"):
    ntt = T // 512
    pst = banks
    gets = []
    for si, s in enumerate(streams):
        g, pu = emit_weight_stream(p, s["name"], nm, s["nk"], wq=wq,
                                   conv_eng=conv_eng,
                                   scale_col=s.get("scale_col"),
                                   scale_tok=s.get("scale_tok"),
                                   tag=s.get("tag", s["name"]),
                                   m_list=m_list, c=c, w_deps=w_deps,
                                   bufset=s.get("bufset", si),
                                   k_off=s.get("k_off", 0))
        gets.append((g, pu, s))
    evac_toks = {}
    all_toks = []
    ms = list(range(nm)) if m_list is None else m_list
    for mi, m in enumerate(ms):
        wts = []
        deps0 = []
        for g, pu, s in gets:
            wt, wtok = g(mi)
            wts.append(wt)
            deps0.append(wtok)
        if act_tok is not None:
            deps0.extend(act_tok if isinstance(act_tok, list) else [act_tok])
        pb = (mi % 2) * ntt
        for tt in range(ntt):
            e = evac_toks.get((mi - 2, tt))
            if e is not None:
                deps0.extend(e if isinstance(e, list) else [e])
        nktot = sum(s["nk"] for _, _, s in gets)
        kk = 0
        for si, (g, pu, s) in enumerate(gets):
            act = s["act"]
            wt = wts[si]
            for k in range(s["nk"]):
                for tt in range(ntt):
                    tok = p.op("tensor", lambda E, k=k, tt=tt, wt=wt, pb=pb, act=act,
                               first=(kk == 0), last=(kk == nktot - 1): E.matmul(
                        pst[pb + tt][:], wt[:, k * 128:(k + 1) * 128],
                        act(k)[:, tt * 512:(tt + 1) * 512],
                        start=first, stop=last),
                        deps=deps0 if (kk == 0 and tt == 0) else (),
                        token=(kk == nktot - 1 and tt == ntt - 1))
                kk += 1
        mm_tok = tok
        for tt in range(ntt):
            etok = out_cb(m, tt, pst[pb + tt][:], mm_tok)
            evac_toks[(mi, tt)] = etok
            all_toks.append(etok)
        for si, (g, pu, s) in enumerate(gets):
            pu(mi, mm_tok)
    return all_toks


def emit_gemm_fm(p, act, w_name, nm, nk, out_cb, T=2048, wq="sync",
                 conv_eng="vector", scale_col=None, scale_tok=None, tag=None,
                 act_tok=None, banks=None, m_list=None, c=None, w_deps=(),
                 bufset=0, k_off=0):
    return emit_gemm_multi(
        p, [dict(name=w_name, nk=nk, act=act, scale_col=scale_col,
                 scale_tok=scale_tok, tag=tag or w_name, bufset=bufset,
                 k_off=k_off)],
        nm, out_cb, T=T, banks=banks, act_tok=act_tok, w_deps=w_deps,
        m_list=m_list, c=c, wq=wq, conv_eng=conv_eng)



def l0ffn_consts():
    G16 = np.zeros((8, 128, 64), np.float32)
    Exp16 = np.zeros((8, 64, 128), np.float32)
    for m in range(8):
        for row in range(128):
            G16[m, row, 8 * m + row // 16] = 1.0
            Exp16[m, 8 * m + row // 16, row] = 1.0
    Sum16 = np.zeros((128, 16), np.float32)
    Tile16 = np.zeros((16, 128), np.float32)
    for row in range(128):
        Sum16[row, row % 16] = 0.25
        Tile16[row % 16, row] = 0.25
    return dict(G16=bf16(G16), Exp16=bf16(Exp16), Sum16=bf16(Sum16),
                Tile16=bf16(Tile16))


def l0ffn_host(inp):
    A = inp["L0_ffn_A"].reshape(1024, 2048)
    B_r = inp["L0_ffn_B"].transpose(0, 2, 1).reshape(1024, 2048)
    W = inp["L0_ffn_W"]
    d = dict(l0f_A=pretile_lhs(A, transpose=True),
             l0f_B=pretile_lhs(B_r),
             l0f_W=pretile_lhs(W, transpose=True),
             l0f_bias=np.ascontiguousarray(inp["L0_ffn_bias"].reshape(16, 128).T))
    d.update({"l0f_" + k: v for k, v in l0ffn_consts().items()})
    return d


def emit_slot_l0ffn(p, c, x_h, xout_h, deps=(), stages=6):
    T = 2048
    ntt = 4

    g16 = p.sb("g16", [128, 64 * 8], BF16)
    e16 = p.sb("e16", [64, 128 * 8], BF16)
    s16 = p.sb("s16", [128, 16], BF16)
    t16 = p.sb("t16", [16, 128], BF16)
    biasc = p.sb("l0f_biasc", [128, 16], F32)
    ctoks = []
    for m in range(8):
        ctoks.append(p.dma("scalar", g16[:, m * 64:(m + 1) * 64], p.tensors["l0f_G16"][m], group="cst"))
        ctoks.append(p.dma("scalar", e16[:, m * 128:(m + 1) * 128], p.tensors["l0f_Exp16"][m], group="cst"))
    ctoks += [
        p.dma("scalar", s16[:], p.tensors["l0f_Sum16"], group="cst"),
        p.dma("scalar", t16[:], p.tensors["l0f_Tile16"], group="cst"),
        p.dma("scalar", biasc[:], p.tensors["l0f_bias"], group="cst"),
    ]
    ctok = ctoks[-1]
    negbig_t = p.sb("l0f_negbig", [128, 64], F32)
    negbig = negbig_t[:]
    nbt = p.op("vector", lambda E: E.memset(negbig, -1e30), token=True)

    xhat = [p.sb(f"l0f_xh{k}", [128, T], BF16) for k in range(16)]
    ln_toks, _rows = emit_ln_fm(p, c, x_h, xhat, tag="l0f_ln", deps=deps,
                                a_b=c.scrb[0][:], b_b=c.scrb[1][:])
    xh_tok = ln_toks[-1]
    if stages <= 1:
        return [xh_tok]
    for t in ln_toks:
        p.watermark

    vecs = [p.sb(f"l0f_vec{m}", [128, T], BF16) for m in range(8)]
    vecsq = [p.sb(f"l0f_vsq{m}", [128, T], BF16) for m in range(8)]

    def vec_cb(m, tt, ps, mm_tok):
        t1 = p.op("vector", lambda E, m=m, tt=tt, ps=ps: E.tensor_copy(
            vecs[m][:, tt * 512:(tt + 1) * 512], ps), deps=[mm_tok], token=True)
        t2 = p.op("scalar", lambda E, m=m, tt=tt: E.activation(
            out=vecsq[m][:, tt * 512:(tt + 1) * 512],
            in_=vecs[m][:, tt * 512:(tt + 1) * 512], func=AF.Square),
            deps=[t1], token=True)
        return [t1, t2]

    vtoks = emit_gemm_fm(p, lambda k: xhat[k][:], "l0f_A", 8, 16, vec_cb,
                         act_tok=ln_toks[-1], banks=c.bank, tag="l0fA", c=c)
    vec_tok = vtoks[-1][1]
    vec_tok_v = vtoks[-1][0]
    if stages <= 2:
        return [vec_tok]

    M_fm = c.scrb[0][0:64, :]
    M_tm = p.sb("l0f_Mtm", [128, 64], BF16)
    msk8 = p.sb("l0f_msk8", [128, 64], mybir.dt.uint8)
    arr_t = p.sb("l0f_arr", [128, 128], F32)
    arr = [arr_t[:, 64 * i:64 * (i + 1)] for i in range(2)]
    mx = p.sb("l0f_mx", [128, 1], F32)
    tr_toks = []
    prev_ev = {}
    for tl in range(16):
        b = c.bank[tl % 2]
        bq = c.bank[2 + tl % 2]
        dd = [vec_tok, vec_tok_v, ctok] + ([prev_ev.get(tl - 2)] if tl >= 2 else [])
        for m in range(8):
            mt = p.op("tensor", lambda E, tl=tl, m=m, b=b: E.matmul(
                b[:, 0:64], vecsq[m][:, tl * 128:(tl + 1) * 128],
                g16[:, m * 64:(m + 1) * 64],
                start=(m == 0), stop=(m == 7)),
                deps=dd if m == 0 else (), token=(m == 7))
        i = tl % 2
        e0 = p.op("vector", lambda E, i=i, b=b: E.tensor_copy(arr[i], b[:, 0:64]),
                  deps=[mt], token=True)
        cur = arr[i]
        t4tok = e0
        for it in range(4):
            rmx = p.op("vector", lambda E, cur=cur: E.reduce_max(mx[:], cur, axis=AX.X),
                       deps=[t4tok], token=True)
            if it < 3:
                msk = p.op("vector", lambda E, cur=cur: E.tensor_scalar(
                    msk8[:], cur, mx[:], None, ALU.is_ge, ALU.bypass),
                    deps=[rmx], token=True)
                t4tok = p.op("vector", lambda E, cur=cur: E.copy_predicated(
                    cur, msk8[:], negbig), deps=[msk, nbt], token=True)
            else:
                t4tok = rmx
        mk = p.op("vector", lambda E, b=b: E.tensor_scalar(
            M_tm[:], b[:, 0:64], mx[:], None, ALU.is_ge, ALU.bypass),
            deps=[t4tok], token=True)
        trt = p.op("tensor", lambda E, bq=bq: E.transpose(
            bq.bitcast(BF16)[0:64, 0:128], M_tm[:], c.ident_bf[:]),
            deps=[mk, c.ident_tok, prev_ev.get(tl - 2)], token=True)
        ev = p.op("vector", lambda E, tl=tl, bq=bq: E.tensor_copy(
            M_fm[:, tl * 128:(tl + 1) * 128], bq.bitcast(BF16)[0:64, 0:128]),
            deps=[trt], token=True)
        prev_ev[tl] = ev
        tr_toks.append(ev)
    mfm_tok = tr_toks[-1]
    if stages <= 3:
        return [mfm_tok]

    v_bf = c.scrb[2][0:16, :]
    mv = c.scrb[1][:, 0:512]
    v_toks = []
    bank_free = {}
    prev_vt = None
    for tt in range(4):
        vb = c.bank[4 + tt]
        for m in range(8):
            bi = (tt * 8 + m) % 2
            b = c.bank[bi]
            dd = [mfm_tok, vec_tok, bank_free.get(bi)]
            met = p.op("tensor", lambda E, tt=tt, m=m, b=b: E.matmul(
                b[:], e16[:, m * 128:(m + 1) * 128],
                M_fm[:, tt * 512:(tt + 1) * 512], start=True, stop=True),
                deps=dd, token=True)
            mvt = p.op("vector", lambda E, tt=tt, m=m, b=b: E.tensor_tensor(
                mv, vecs[m][:, tt * 512:(tt + 1) * 512], b[:], ALU.mult),
                deps=[met, prev_vt], token=True)
            bank_free[bi] = mvt
            prev_vt = p.op("tensor", lambda E, tt=tt, m=m, vb=vb: E.matmul(
                vb[0:16, :], s16[:], mv, start=(m == 0), stop=(m == 7)),
                deps=[mvt], token=True)
        v_toks.append(p.op("vector", lambda E, tt=tt, vb=vb: E.tensor_copy(
            v_bf[:, tt * 512:(tt + 1) * 512], vb[0:16, :]), deps=[prev_vt], token=True))
    v_tok = v_toks[-1]
    if stages <= 4:
        return [v_tok]

    u_bf = vecsq
    mexps = c.scrb[1][:, 512:1024]
    u_toks = []
    last_u = None
    last_ms = None
    for m in range(8):
        for tt in range(4):
            n = m * 4 + tt
            bi, bi2 = n % 2, 2 + n % 2
            b, b2 = c.bank[bi], c.bank[bi2]
            vr = p.op("tensor", lambda E, m=m, tt=tt, b=b: E.matmul(
                b[:], t16[:], v_bf[:, tt * 512:(tt + 1) * 512], start=True, stop=True),
                deps=[v_tok, bank_free.get(bi)], token=True)
            me = p.op("tensor", lambda E, m=m, tt=tt, b2=b2: E.matmul(
                b2[:], e16[:, m * 128:(m + 1) * 128],
                M_fm[:, tt * 512:(tt + 1) * 512], start=True, stop=True),
                deps=[v_tok, bank_free.get(bi2)], token=True)
            ms = p.op("scalar", lambda E, b2=b2: E.copy(mexps, b2[:]),
                      deps=[me, last_u], token=True)
            bank_free[bi2] = ms
            last_ms = ms
            last_u = p.op("vector", lambda E, m=m, tt=tt, b=b: E.tensor_tensor(
                u_bf[m][:, tt * 512:(tt + 1) * 512], mexps, b[:], ALU.mult),
                deps=[ms, vr], token=True)
            bank_free[bi] = last_u
            u_toks.append(last_u)
    u_tok = u_toks[-1]
    if stages <= 5:
        return [u_tok]

    xst = [c.scrb[0].bitcast(F32)[:, 0:512], c.scrb[0].bitcast(F32)[:, 512:1024],
           c.scrb[1].bitcast(F32)[:, 0:512], c.scrb[1].bitcast(F32)[:, 512:1024]]
    st_toks = []
    ev_hist = []

    def out_cb(m, tt, ps, mm_tok):
        n = m * 4 + tt
        i = n % 2
        xt = p.dma("gpsimd", xst[i],
                   x_h[m * 128:(m + 1) * 128, tt * 512:(tt + 1) * 512],
                   deps=[ev_hist[n - 2]] if n >= 2 else [u_tok, last_ms],
                   group=f"xl{i}")
        et = p.op("vector", lambda E, m=m, tt=tt, ps=ps, i=i: E.scalar_tensor_tensor(
            xst[2 + i], ps, biasc[:, m:m + 1], xst[i],
            ALU.add, ALU.add),
            deps=[mm_tok, xt, ctok, st_toks[n - 2] if n >= 2 else None], token=True)
        ev_hist.append(et)
        st = p.dma("gpsimd", xout_h[m * 128:(m + 1) * 128, tt * 512:(tt + 1) * 512],
                   xst[2 + i], deps=[et], group=f"xs{i}")
        st_toks.append(st)
        return et

    emit_gemm_multi(
        p, [dict(name="l0f_B", nk=8, act=lambda k: u_bf[k][:], bufset=0),
            dict(name="l0f_W", nk=16, act=lambda k: xhat[k][:], bufset=1)],
        16, out_cb, banks=c.bank, act_tok=[u_tok, last_ms], c=c, w_deps=[u_tok])
    return st_toks



def l1ffn_host(inp):
    d = dict(
        l1f_wr=pretile_lhs(inp["L1_ffn_w_router"]),
        l1f_wg=pretile_lhs(inp["L1_ffn_w_gate"]),
        l1f_wu=pretile_lhs(inp["L1_ffn_w_up"]),
        l1f_wsg=pretile_lhs(inp["L1_ffn_shared_w_gate"]),
        l1f_wsu=pretile_lhs(inp["L1_ffn_shared_w_up"]),
        l1f_wd=pretile_lhs(inp["L1_ffn_w_down"]),
        l1f_wsd=pretile_lhs(inp["L1_ffn_shared_w_down"]),
        l1f_amp=np.ascontiguousarray(inp["L1_ffn_amplitudes"].reshape(128, 16)),
        l1f_es=np.ascontiguousarray(inp["L1_ffn_expert_scale"].reshape(1, 128)),
    )
    return d


def emit_slot_l1ffn(p, c, x_h, xout_h, deps=(), stages=9):
    T = 2048
    amp = p.sb("l1f_ampsb", [128, 16], F32)
    mixb = p.sb("l1f_mix", [128, 8], BF16)
    esrow = p.sb("l1f_essb", [1, 128], F32)
    esrowb = p.sb("l1f_esb", [1, 128], BF16)
    es_b = p.sb("l1f_esbb", [128, 128], BF16)
    Cfm = p.sb("l1f_cfm", [128, T], BF16)
    m_sb = p.sb("l1f_msb", [8, T], BF16)
    m_row = p.sb("l1f_mrow", [1, T], BF16)
    m_scr = p.dram("l1f_mscr", [8, T], BF16, kind="Internal")
    M_b = [p.sb(f"l1f_Mb{k}", [128, T], BF16) for k in range(8)]
    curp = p.sb("l1f_cur", [128, 128], F32)
    punb = p.sb("l1f_pun", [128, 128], F32)
    ptop = p.sb("l1f_ptop", [128, 128], F32)
    msk8 = p.sb("l1f_msk8", [128, 128], mybir.dt.uint8)
    mbf = p.sb("l1f_mbf", [128, 128], BF16)
    cbf = p.sb("l1f_cbf", [128, 128], BF16)
    nmx = p.sb("l1f_nmx", [128, 1], F32)
    den = p.sb("l1f_den", [128, 1], F32)
    stp = p.sb("l1f_stp", [128, 1], F32)
    dcol = p.sb("l1f_dcol", [128, 1], F32)
    negbig_t = p.sb("l1f_negbig", [128, 128], F32)
    nbt = p.op("vector", lambda E: E.memset(negbig_t[:], -1e30), token=True)
    adk = p.dma("scalar", amp[:], p.tensors["l1f_amp"], group="cst")
    esk = p.dma("scalar", esrow[:], p.tensors["l1f_es"], group="cst")

    xhat = [p.sb(f"l1f_xh{k}", [128, T], BF16) for k in range(16)]
    ln_toks, _ = emit_ln_fm(p, c, x_h, xhat, tag="l1f_ln", deps=deps,
                            a_b=c.scrb[0][:], b_b=c.scrb[1][:])
    xh_tok = ln_toks[-1]

    t1 = p.op("scalar", lambda E: E.activation(out=amp[:], in_=amp[:], func=AF.Square),
              deps=[adk], token=True)
    t2 = p.op("vector", lambda E: E.tensor_add(amp[:, 0:8], amp[:, 0:16:2], amp[:, 1:16:2]),
              deps=[t1], token=True)
    t3 = p.op("vector", lambda E: E.reduce_sum(amp[:, 8:9], amp[:, 0:8], axis=AX.X),
              deps=[t2], token=True)
    t4 = p.op("vector", lambda E: E.tensor_scalar(amp[:, 8:9], amp[:, 8:9], 1e-8, None,
                                                  ALU.add, ALU.bypass), deps=[t3], token=True)
    t5 = p.op("vector", lambda E: E.reciprocal(amp[:, 9:10], amp[:, 8:9]), deps=[t4], token=True)
    mix_tok = p.op("vector", lambda E: E.tensor_scalar(mixb[:], amp[:, 0:8], amp[:, 9:10],
                                                       None, ALU.mult, ALU.bypass),
                   deps=[t5], token=True)
    esc = p.op("vector", lambda E: E.tensor_copy(esrowb[:], esrow[:]), deps=[esk], token=True)
    esb_mm = p.op("tensor", lambda E: E.matmul(c.bank[6][:, 0:128], c.ones_bf[:],
                                               esrowb[:], start=True, stop=True),
                  deps=[esc, c.ones_tok, xh_tok], token=True)
    esb_tok = p.op("vector", lambda E: E.tensor_copy(es_b[:], c.bank[6][:, 0:128]),
                   deps=[esb_mm], token=True)
    if stages <= 1:
        return [esb_tok, mix_tok]

    getr, postr = emit_weight_stream(p, "l1f_wr", 1, 16, c=c, w_deps=[xh_tok],
                                     tag="l1fwr")
    wr, wr_tok = getr(0)
    tl_toks = []
    prev = {}
    for tl in range(16):
        b = c.bank[tl % 2]
        bq = c.bank[2 + tl % 2]
        dd = [wr_tok, xh_tok, esb_tok, prev.get(tl - 2)]
        for k in range(16):
            mt = p.op("tensor", lambda E, tl=tl, k=k, b=b: E.matmul(
                b[:, 0:128], xhat[k][:, tl * 128:(tl + 1) * 128],
                wr[:, k * 128:(k + 1) * 128],
                start=(k == 0), stop=(k == 15)),
                deps=dd if k == 0 else (), token=(k == 15))
        rm = p.op("vector", lambda E, b=b: E.reduce_max(nmx[:], b[:, 0:128], axis=AX.X,
                                                        negate=True), deps=[mt], token=True)
        ex = p.op("scalar", lambda E, b=b: E.activation(out=punb[:], in_=b[:, 0:128],
                                                        func=AF.Exp, bias=nmx[:]),
                  deps=[rm], token=True)
        rs = p.op("vector", lambda E: E.reduce_sum(den[:], punb[:], axis=AX.X),
                  deps=[ex], token=True)
        cp = p.op("vector", lambda E: E.tensor_copy(curp[:], punb[:]), deps=[ex], token=True)
        t4tok = cp
        for it in range(4):
            rmx = p.op("vector", lambda E: E.reduce_max(nmx[:], curp[:], axis=AX.X),
                       deps=[t4tok], token=True)
            if it < 3:
                mk = p.op("vector", lambda E: E.tensor_scalar(
                    msk8[:], curp[:], nmx[:], None, ALU.is_ge, ALU.bypass),
                    deps=[rmx], token=True)
                t4tok = p.op("vector", lambda E: E.copy_predicated(
                    curp[:], msk8[:], negbig_t[:]), deps=[mk, nbt], token=True)
            else:
                t4tok = rmx
        mk2 = p.op("vector", lambda E: E.tensor_scalar(
            mbf[:], punb[:], nmx[:], None, ALU.is_ge, ALU.bypass), deps=[t4tok], token=True)
        pt = p.op("vector", lambda E: E.tensor_mul(ptop[:], punb[:], mbf[:]),
                  deps=[mk2], token=True)
        st_ = p.op("vector", lambda E: E.reduce_sum(stp[:], ptop[:], axis=AX.X),
                   deps=[pt], token=True)
        de = p.op("vector", lambda E: E.scalar_tensor_tensor(
            dcol[:], den[:], 1e-8, stp[:], ALU.mult, ALU.add), deps=[st_, rs], token=True)
        rc = p.op("vector", lambda E: E.reciprocal(dcol[:], dcol[:]), deps=[de], token=True)
        c1 = p.op("vector", lambda E: E.tensor_scalar(ptop[:], ptop[:], dcol[:], None,
                                                      ALU.mult, ALU.bypass),
                  deps=[rc], token=True)
        c2 = p.op("vector", lambda E: E.tensor_mul(cbf[:], ptop[:], es_b[:]),
                  deps=[c1], token=True)
        trt = p.op("tensor", lambda E, bq=bq: E.transpose(
            bq.bitcast(BF16)[0:128, 0:128], cbf[:], c.ident_bf[:]),
            deps=[c2, c.ident_tok, prev.get(tl - 2)], token=True)
        ev = p.op("vector", lambda E, tl=tl, bq=bq: E.tensor_copy(
            Cfm[:, tl * 128:(tl + 1) * 128], bq.bitcast(BF16)[0:128, 0:128]),
            deps=[trt], token=True)
        prev[tl] = ev
        tl_toks.append(ev)
    postr(0, tl_toks[-1])
    cfm_tok = tl_toks[-1]
    if stages <= 2:
        return [cfm_tok]

    m_toks = []
    for tt in range(4):
        vb = c.bank[4 + tt]
        mm = p.op("tensor", lambda E, tt=tt, vb=vb: E.matmul(
            vb[0:8, :], mixb[:], Cfm[:, tt * 512:(tt + 1) * 512],
            start=True, stop=True), deps=[cfm_tok, mix_tok], token=True)
        m_toks.append(p.op("vector", lambda E, tt=tt, vb=vb: E.tensor_copy(
            m_sb[:, tt * 512:(tt + 1) * 512], vb[0:8, :]), deps=[mm], token=True))
    mdma = p.dma("gpsimd", m_scr, m_sb[:], deps=[m_toks[-1], m_toks[0], m_toks[1], m_toks[2]],
                 group="mscr")
    mb_toks = []
    last_bc = None
    for k in range(8):
        rd = p.dma("gpsimd", m_row[:], m_scr[k:k + 1, :], deps=[mdma, last_bc], group="mscr")
        for tt in range(4):
            b = c.bank[tt % 2]
            mm = p.op("tensor", lambda E, k=k, tt=tt, b=b: E.matmul(
                b[:], c.ones_bf[:], m_row[:, tt * 512:(tt + 1) * 512],
                start=True, stop=True),
                deps=[rd, mb_toks[-2] if len(mb_toks) >= 2 else None], token=True)
            last_bc = mm
            mb_toks.append(p.op("vector", lambda E, k=k, tt=tt, b=b: E.tensor_copy(
                M_b[k][:, tt * 512:(tt + 1) * 512], b[:]), deps=[mm], token=True))
    mb_tok = mb_toks[-1]
    if stages <= 3:
        return [mb_tok]

    acc = p.sb("l1f_acc", [128, T], F32)
    mixed_scr = p.dram("l1f_mixscr", [8, 128, T], BF16, kind="Internal")
    gscr = [p.sb(f"l1f_g{i}", [128, 1024], BF16) for i in range(2)]
    uscr = [p.sb(f"l1f_u{i}", [128, 1024], BF16) for i in range(2)]
    pscr = [p.sb(f"l1f_p{i}", [128, 1024], BF16) for i in range(2)]
    m_list = [k8 * 8 + j for j in range(8) for k8 in range(8)]
    getg, postg = emit_weight_stream(p, "l1f_wg", 64, 16, c=c, bufset=0,
                                     w_deps=[mb_tok], m_list=m_list, tag="l1fwg")
    getu, postu = emit_weight_stream(p, "l1f_wu", 64, 16, c=c, bufset=1,
                                     w_deps=[mb_tok], m_list=m_list, tag="l1fwu")
    evac_hist = {}
    cast_toks = {}
    mx_st = {}
    acc_tok = None
    for idx in range(64):
        j, k8 = idx // 8, idx % 8
        wg, wgt = getg(idx)
        wu, wut = getu(idx)
        for half in range(2):
            hb = 2 * (half % 2)
            gbank, ubank = c.bank2[hb], c.bank2[hb + 1]
            dd = [wgt, wut, mb_tok] + list(evac_hist.get((idx - 1, half), []))
            if idx == 0:
                dd.append(cfm_tok)
            for kk in range(16):
                for u in range(2):
                    tok_g = p.op("tensor", lambda E, kk=kk, u=u, half=half, gbank=gbank, wg=wg: E.matmul(
                        gbank[:, u * 512:(u + 1) * 512], wg[:, kk * 128:(kk + 1) * 128],
                        xhat[kk][:, half * 1024 + u * 512: half * 1024 + (u + 1) * 512],
                        start=(kk == 0), stop=(kk == 15)),
                        deps=dd if (kk == 0 and u == 0) else (), token=(kk == 15 and u == 1))
            for kk in range(16):
                for u in range(2):
                    tok_u = p.op("tensor", lambda E, kk=kk, u=u, half=half, ubank=ubank, wu=wu: E.matmul(
                        ubank[:, u * 512:(u + 1) * 512], wu[:, kk * 128:(kk + 1) * 128],
                        xhat[kk][:, half * 1024 + u * 512: half * 1024 + (u + 1) * 512],
                        start=(kk == 0), stop=(kk == 15)),
                        deps=(), token=(kk == 15 and u == 1))
            sg = p.op("scalar", lambda E, half=half, gbank=gbank: E.activation(
                out=gscr[half][:], in_=gbank[:], func=AF.Silu),
                deps=[tok_g, evac_hist.get((idx - 1, half), [None])[-1]], token=True)
            ue = p.op("vector", lambda E, half=half, ubank=ubank, k8=k8: E.tensor_mul(
                uscr[half][:], ubank[:], M_b[k8][:, half * 1024:(half + 1) * 1024]),
                deps=[tok_u, evac_hist.get((idx - 1, half), [None])[-1]], token=True)
            if k8 == 0:
                pr = p.op("vector", lambda E, half=half: E.tensor_mul(
                    acc[:, half * 1024:(half + 1) * 1024], gscr[half][:], uscr[half][:]),
                    deps=[sg, ue, cast_toks.get((j - 1, half))], token=True)
                at = pr
            else:
                pr = p.op("vector", lambda E, half=half: E.tensor_mul(
                    uscr[half][:], gscr[half][:], uscr[half][:]),
                    deps=[sg, ue, acc_tok], token=True)
                at = p.op("gpsimd", lambda E, half=half: E.tensor_add(
                    acc[:, half * 1024:(half + 1) * 1024],
                    acc[:, half * 1024:(half + 1) * 1024], uscr[half][:]),
                    deps=[pr], token=True)
            acc_tok = at
            evac_hist[(idx, half)] = [sg, ue, at]
            if k8 == 7:
                ct = p.op("vector", lambda E, j=j, half=half: E.tensor_copy(
                    pscr[half][:], acc[:, half * 1024:(half + 1) * 1024]),
                    deps=[at, mx_st.get((j - 1, half))], token=True)
                mx_st[(j, half)] = p.dma(
                    "gpsimd", mixed_scr[j, :, half * 1024:(half + 1) * 1024],
                    pscr[half][:], deps=[ct], group=f"mxs{half}")
                cast_toks[(j, half)] = ct
        postg(idx, tok_g)
        postu(idx, tok_u)
    mixed_tok = cast_toks[(7, 1)]
    mixed_st_tok = mx_st[(7, 1)]
    if stages <= 4:
        return [mixed_tok]

    psh_scr = p.dram("l1f_pshscr", [16, 128, T], BF16, kind="Internal")
    getsg, postsg = emit_weight_stream(p, "l1f_wsg", 16, 16, c=c, bufset=0,
                                       w_deps=[mixed_tok], tag="l1fwsg")
    getsu, postsu = emit_weight_stream(p, "l1f_wsu", 16, 16, c=c, bufset=1,
                                       w_deps=[mixed_tok], tag="l1fwsu")
    sh_hist = {}
    sh_toks = []
    psh_st = {}
    for idx in range(16):
        wg, wgt = getsg(idx)
        wu, wut = getsu(idx)
        for half in range(2):
            hb = 2 * (half % 2)
            gbank, ubank = c.bank2[hb], c.bank2[hb + 1]
            dd = [wgt, wut] + list(sh_hist.get((idx - 1, half), []))
            if idx == 0:
                dd.append(mixed_tok)
            for kk in range(16):
                for u in range(2):
                    tok_g = p.op("tensor", lambda E, kk=kk, u=u, half=half, gbank=gbank, wg=wg: E.matmul(
                        gbank[:, u * 512:(u + 1) * 512], wg[:, kk * 128:(kk + 1) * 128],
                        xhat[kk][:, half * 1024 + u * 512: half * 1024 + (u + 1) * 512],
                        start=(kk == 0), stop=(kk == 15)),
                        deps=dd if (kk == 0 and u == 0) else (), token=(kk == 15 and u == 1))
            for kk in range(16):
                for u in range(2):
                    tok_u = p.op("tensor", lambda E, kk=kk, u=u, half=half, ubank=ubank, wu=wu: E.matmul(
                        ubank[:, u * 512:(u + 1) * 512], wu[:, kk * 128:(kk + 1) * 128],
                        xhat[kk][:, half * 1024 + u * 512: half * 1024 + (u + 1) * 512],
                        start=(kk == 0), stop=(kk == 15)),
                        deps=(), token=(kk == 15 and u == 1))
            sg = p.op("scalar", lambda E, half=half, gbank=gbank: E.activation(
                out=gscr[half][:], in_=gbank[:], func=AF.Silu),
                deps=[tok_g, sh_hist.get((idx - 1, half), [None])[-1]], token=True)
            pe = p.op("vector", lambda E, idx=idx, half=half, ubank=ubank: E.tensor_mul(
                pscr[half][:], gscr[half][:], ubank[:]),
                deps=[sg, tok_u, psh_st.get((idx - 1, half), mx_st[(7, half)])],
                token=True)
            psh_st[(idx, half)] = p.dma(
                "gpsimd", psh_scr[idx, :, half * 1024:(half + 1) * 1024],
                pscr[half][:], deps=[pe], group=f"pshs{half}")
            sh_hist[(idx, half)] = [sg, pe]
            sh_toks.append(pe)
        postsg(idx, tok_g)
        postsu(idx, tok_u)
    psh_tok = sh_toks[-1]
    psh_st_tok = psh_st[(15, 1)]
    psh_st_tok0 = psh_st[(15, 0)]
    if stages <= 5:
        return [psh_tok]

    xst = [c.scrb[0].bitcast(F32)[:, 0:512], c.scrb[0].bitcast(F32)[:, 512:1024],
           c.scrb[1].bitcast(F32)[:, 0:512], c.scrb[1].bitcast(F32)[:, 512:1024]]
    st_toks = []
    ev_hist = []

    def out_cb(m, tt, ps, mm_tok):
        n = m * 4 + tt
        i = n % 2
        xt = p.dma("gpsimd", xst[i],
                   x_h[m * 128:(m + 1) * 128, tt * 512:(tt + 1) * 512],
                   deps=[ev_hist[n - 2]] if n >= 2 else [psh_tok],
                   group=f"xl{i}")
        et = p.op("vector", lambda E, m=m, tt=tt, ps=ps, i=i: E.tensor_add(
            xst[2 + i], ps, xst[i]),
            deps=[mm_tok, xt, st_toks[n - 2] if n >= 2 else None], token=True)
        ev_hist.append(et)
        st = p.dma("gpsimd", xout_h[m * 128:(m + 1) * 128, tt * 512:(tt + 1) * 512],
                   xst[2 + i], deps=[et], group=f"xs{i}")
        st_toks.append(st)
        return et

    sg_last = sh_hist[(15, 1)][0]
    rl_toks = []
    for j in range(8):
        rl_toks.append(p.dma("gpsimd", xhat[j][:], mixed_scr[j],
                             deps=[psh_tok, mixed_st_tok, mx_st[(7, 0)], sg_last],
                             group="mxl"))
    for j in range(16):
        dst = xhat[8 + j] if j < 8 else M_b[j - 8]
        rl_toks.append(p.dma("gpsimd", dst[:], psh_scr[j],
                             deps=[psh_st_tok, psh_st_tok0], group="mxl"))

    def dact(k):
        return xhat[k][:]

    def shact(k):
        return xhat[8 + k][:] if k < 8 else M_b[k - 8][:]

    emit_gemm_multi(
        p, [dict(name="l1f_wd", nk=8, act=dact, bufset=0, tag="l1fwd"),
            dict(name="l1f_wsd", nk=16, act=shact, bufset=1, tag="l1fwsd")],
        16, out_cb, banks=c.bank, act_tok=[psh_tok, sg_last, rl_toks[-1]], c=c,
        w_deps=[psh_tok])
    return st_toks



def l1attn_host(inp):
    H, HKV, DH, C = 16, 4, 128, 128
    S = 2048
    lam = np.exp2(-8.0 * np.arange(1, H + 1) / H).astype(np.float64)
    r = np.arange(C, dtype=np.float64)
    diff = r[None, :] - r[:, None]
    maskT = np.where(diff >= 0, np.exp(-lam[:, None, None] * diff), 0.0)
    qdec = np.exp(-lam[:, None] * (r[None, :] + 1.0))
    kdec = np.exp(-lam[:, None] * (C - 1.0 - r[None, :]))
    blk = np.exp(-lam * C)
    qdec_t = np.broadcast_to(qdec.reshape(1, H * C), (128, H * C))
    inv = 1.0 / 10000.0 ** (np.arange(0, DH, 2) / DH)
    f = np.arange(S)[:, None] * inv[None, :]
    cos = np.concatenate([np.cos(f), np.cos(f)], axis=1).T
    sin = np.concatenate([np.sin(f), np.sin(f)], axis=1).T
    sin_s = sin
    Sw = np.zeros((DH, DH), np.float32)
    for d in range(64):
        Sw[d + 64, d] = -1.0
        Sw[d, d + 64] = 1.0
    return dict(
        l1a_wq=pretile_lhs(inp["L1_attn_wq"], transpose=True),
        l1a_wk=pretile_lhs(inp["L1_attn_wk"], transpose=True),
        l1a_wv=pretile_lhs(inp["L1_attn_wv"], transpose=True),
        l1a_wo=pretile_lhs(inp["L1_attn_wo"], transpose=True),
        l1a_mask=bf16(maskT.transpose(1, 0, 2).reshape(128, H * C)),
        l1a_qdec=bf16(qdec_t),
        l1a_kdec=np.ascontiguousarray(kdec.T.astype(np.float32)),
        l1a_cos=bf16(cos),
        l1a_sin=bf16(sin_s),
        l1a_sw=bf16(Sw),
    ), blk.astype(np.float32)


def emit_slot_l1attn(p, c, x_h, xout_h, blk, deps=(), stages=9):
    T = 2048
    H, HKV, DH, NC = 16, 4, 128, 16
    masks = c.scr32[0].bitcast(BF16)
    qdec = c.scr32[1].bitcast(BF16)
    kdec = p.sb("l1a_kdecs", [128, H], F32)
    sw = p.sb("l1a_sws", [128, 128], BF16)
    kv = p.sb("l1a_kv", [128, H * 128], BF16)
    scT = p.sb("l1a_sct", [128, 128], BF16)
    qsc = p.sb("l1a_qsc", [128, 128], BF16)
    ktsc = p.sb("l1a_ktsc", [128, 128], BF16)
    vts = p.sb("l1a_vts", [128, 128], BF16)
    ctoks = [
        p.dma("scalar", kdec[:], p.tensors["l1a_kdec"], group="cst"),
        p.dma("scalar", sw[:], p.tensors["l1a_sw"], group="cst"),
    ]
    ctok = ctoks[-1]
    kv0 = p.op("vector", lambda E: E.memset(kv[:], 0.0), token=True)

    xhat = [p.sb(f"l1a_xh{k}", [128, T], BF16) for k in range(16)]
    ln_toks, _ = emit_ln_fm(p, c, x_h, xhat, tag="l1a_ln", deps=deps,
                            a_b=c.scrb[0][:], b_b=c.scrb[1][:])
    xh_tok = ln_toks[-1]

    cos_t = c.scrb[0]
    sin_t = c.scrb[1]
    ct1 = p.dma("scalar", cos_t[:], p.tensors["l1a_cos"], deps=[xh_tok], group="cst")
    ct2 = p.dma("scalar", sin_t[:], p.tensors["l1a_sin"], deps=[xh_tok], group="cst")

    q_t = [p.sb(f"l1a_q{m}", [128, T], BF16) for m in range(16)]
    k_t = [p.sb(f"l1a_k{m}", [128, T], BF16) for m in range(4)]
    v_fm = [c.scrb[2], c.scrb[3]] + [p.sb(f"l1a_v{m}", [128, T], BF16) for m in range(2)]

    def q_cb(m, tt, ps, mm_tok):
        return p.op("vector", lambda E, m=m, tt=tt, ps=ps: E.tensor_copy(
            q_t[m][:, tt * 512:(tt + 1) * 512], ps), deps=[mm_tok], token=True)
    qtoks = emit_gemm_fm(p, lambda k: xhat[k][:], "l1a_wq", 16, 16, q_cb,
                         act_tok=xh_tok, banks=c.bank, tag="l1awq", c=c, bufset=0)

    def k_cb(m, tt, ps, mm_tok):
        return p.op("vector", lambda E, m=m, tt=tt, ps=ps: E.tensor_copy(
            k_t[m][:, tt * 512:(tt + 1) * 512], ps), deps=[mm_tok], token=True)
    ktoks = emit_gemm_fm(p, lambda k: xhat[k][:], "l1a_wk", 4, 16, k_cb,
                         act_tok=qtoks[-1], banks=c.bank, tag="l1awk", c=c, bufset=1,
                         w_deps=[qtoks[-1]])

    def v_cb(m, tt, ps, mm_tok):
        return p.op("vector", lambda E, m=m, tt=tt, ps=ps: E.tensor_copy(
            v_fm[m][:, tt * 512:(tt + 1) * 512], ps), deps=[mm_tok, xh_tok], token=True)
    vtoks = emit_gemm_fm(p, lambda k: xhat[k][:], "l1a_wv", 4, 16, v_cb,
                         act_tok=ktoks[-1], banks=c.bank, tag="l1awv", c=c, bufset=0,
                         w_deps=[ktoks[-1]])
    v_tok = vtoks[-1]

    rope_toks = []
    rp_prev = {}
    for ti, tile in enumerate(q_t + k_t):
        bb = c.bank2[ti % 2]
        t1s = c.scr32[0].bitcast(BF16)
        t2s = c.scr32[1].bitcast(BF16)
        dd = [v_tok, ct1, ct2, rp_prev.get(ti - 2)]
        mm4 = None
        for tt in range(4):
            mm4 = p.op("tensor", lambda E, tile=tile, tt=tt, bb=bb: E.matmul(
                bb[:, (tt % 2) * 512:(tt % 2) * 512 + 512], sw[:],
                tile[:, tt * 512:(tt + 1) * 512], start=True, stop=True),
                deps=dd if tt == 0 else [rope_toks[-1]] if tt >= 2 else [],
                token=True)
            h = tt % 2
            w1 = p.op("vector", lambda E, tile=tile, tt=tt, h=h: E.tensor_mul(
                t1s[:, h * 1024:h * 1024 + 512], tile[:, tt * 512:(tt + 1) * 512],
                cos_t[:, tt * 512:(tt + 1) * 512]), deps=[mm4], token=True)
            w2 = p.op("vector", lambda E, tile=tile, tt=tt, h=h, bb=bb: E.tensor_mul(
                t2s[:, h * 1024:h * 1024 + 512], bb[:, h * 512:h * 512 + 512],
                sin_t[:, tt * 512:(tt + 1) * 512]), deps=[w1], token=True)
            w3 = p.op("vector", lambda E, tile=tile, tt=tt, h=h: E.tensor_add(
                tile[:, tt * 512:(tt + 1) * 512], t1s[:, h * 1024:h * 1024 + 512],
                t2s[:, h * 1024:h * 1024 + 512]), deps=[w2], token=True)
            rope_toks.append(w3)
        rp_prev[ti] = rope_toks[-1]
    rope_tok = rope_toks[-1]

    mtok1 = p.dma("scalar", masks[:], p.tensors["l1a_mask"], deps=[rope_tok], group="cst")
    mtok = p.dma("scalar", qdec[:], p.tensors["l1a_qdec"], deps=[rope_tok], group="cst")
    if stages <= 3:
        return [rope_tok, mtok]

    o_t = xhat
    kv_toks = {}
    sc_free = rope_tok
    o1_prev = None
    qsc_free = None
    kt_free = None
    kc_prev = None
    vts_free = None
    last_kv = kv0
    last_o_ev = None
    for ci in range(NC):
        for kvh in range(4):
            bC = c.bank[4]
            bV = c.bank[5]
            ktr = p.op("tensor", lambda E, ci=ci, kvh=kvh, bC=bC: E.transpose(
                bC.bitcast(BF16)[0:128, 0:128], k_t[kvh][:, ci * 128:(ci + 1) * 128],
                c.ident_bf[:]),
                deps=[rope_tok, mtok, kt_free], token=True)
            vtr = p.op("tensor", lambda E, ci=ci, kvh=kvh, bV=bV: E.transpose(
                bV.bitcast(BF16)[0:128, 0:128], v_fm[kvh][:, ci * 128:(ci + 1) * 128],
                c.ident_bf[:]),
                deps=[v_tok, vts_free], token=True)
            vte = p.op("vector", lambda E, bV=bV: E.tensor_copy(
                vts[:], bV.bitcast(BF16)[0:128, 0:128]), deps=[vtr], token=True)
            vts_free = vte
            for hh in range(4):
                h = kvh * 4 + hh
                bA = c.bank[(h % 2)]
                bB = c.bank[2 + (h % 2)]
                bD = c.bank[6 + (h % 2)]
                sc_mm = p.op("tensor", lambda E, ci=ci, h=h, kvh=kvh, bA=bA: E.matmul(
                    bA[:, 0:128], k_t[kvh][:, ci * 128:(ci + 1) * 128],
                    q_t[h][:, ci * 128:(ci + 1) * 128], start=True, stop=True),
                    deps=[sc_free], token=True)
                scm = p.op("vector", lambda E, ci=ci, h=h, bA=bA: E.tensor_mul(
                    scT[:], bA[:, 0:128], masks[:, h * 128:(h + 1) * 128]),
                    deps=[sc_mm, mtok1, o1_prev], token=True)
                sc_free = scm
                qs = p.op("vector", lambda E, ci=ci, h=h: E.tensor_mul(
                    qsc[:], q_t[h][:, ci * 128:(ci + 1) * 128],
                    qdec[:, h * 128:(h + 1) * 128]), deps=[qsc_free, mtok], token=True)
                o1 = p.op("tensor", lambda E, ci=ci, h=h, kvh=kvh, bB=bB: E.matmul(
                    bB[:, 0:128], vts[:], scT[:],
                    start=True, stop=False), deps=[scm, vte, last_o_ev], token=True)
                o1_prev = o1
                o2 = p.op("tensor", lambda E, ci=ci, h=h, bB=bB: E.matmul(
                    bB[:, 0:128], kv[:, h * 128:(h + 1) * 128], qsc[:],
                    start=False, stop=True),
                    deps=[qs, kv_toks.get((ci - 1, h), kv0)], token=True)
                qsc_free = o2
                oe = p.op("scalar", lambda E, ci=ci, h=h, bB=bB: E.copy(
                    o_t[h][:, ci * 128:(ci + 1) * 128], bB[:, 0:128]),
                    deps=[o2, last_o_ev], token=True)
                last_o_ev = oe
                kts = p.op("vector", lambda E, h=h, bC=bC: E.tensor_scalar(
                    ktsc[:], bC.bitcast(BF16)[0:128, 0:128], kdec[:, h:h + 1], None,
                    ALU.mult, ALU.bypass), deps=[ktr, kc_prev], token=True)
                kc = p.op("tensor", lambda E, ci=ci, h=h, kvh=kvh, bD=bD: E.matmul(
                    bD[:, 0:128], ktsc[:], vts[:],
                    start=True, stop=True), deps=[kts, o2, last_kv], token=True)
                kt_free = kc
                kc_prev = kc
                vts_free = kc
                last_kv = p.op("vector", lambda E, h=h, bD=bD, bv=float(blk[h]): E.scalar_tensor_tensor(
                    kv[:, h * 128:(h + 1) * 128], kv[:, h * 128:(h + 1) * 128], bv,
                    bD[:, 0:128], ALU.mult, ALU.add), deps=[kc], token=True)
                kv_toks[(ci, h)] = last_kv
    attn_tok = last_o_ev
    if stages <= 4:
        return [attn_tok, kv_toks[(NC - 1, H - 1)]]

    xst = [c.scr32[0].bitcast(F32)[:, 0:512], c.scr32[0].bitcast(F32)[:, 512:1024],
           c.scr32[1].bitcast(F32)[:, 0:512], c.scr32[1].bitcast(F32)[:, 512:1024]]
    st_toks = []
    ev_hist = []
    kvlast = kv_toks[(NC - 1, H - 1)]

    def out_cb(m, tt, ps, mm_tok):
        n = m * 4 + tt
        i = n % 2
        xt = p.dma("gpsimd", xst[i],
                   x_h[m * 128:(m + 1) * 128, tt * 512:(tt + 1) * 512],
                   deps=[ev_hist[n - 2]] if n >= 2 else [attn_tok], group=f"xl{i}")
        et = p.op("vector", lambda E, m=m, tt=tt, ps=ps, i=i: E.tensor_add(
            xst[2 + i], ps, xst[i]),
            deps=[mm_tok, xt, st_toks[n - 2] if n >= 2 else None], token=True)
        ev_hist.append(et)
        st = p.dma("gpsimd", xout_h[m * 128:(m + 1) * 128, tt * 512:(tt + 1) * 512],
                   xst[2 + i], deps=[et], group=f"xs{i}")
        st_toks.append(st)
        return et

    emit_gemm_fm(p, lambda k: o_t[k][:], "l1a_wo", 16, 16, out_cb,
                 act_tok=[attn_tok, kvlast], banks=c.bank, tag="l1awo", c=c,
                 bufset=0, w_deps=[attn_tok, kvlast])
    return st_toks


D_INNER, N_HEADS, D_HEAD, D_STATE = 4096, 64, 64, 128
CONV_DIM = 4352
D_PROJ = 8512
CC = 128
NCH = 16


def l0attn_host(inp):
    W = inp["L0_attn_in_proj_w"]
    d = dict(
        l0a_inproj=pretile_lhs(W),
        l0a_outproj=pretile_lhs(inp["L0_attn_out_proj_w"]),
        l0a_gate=pretile_lhs(inp["L0_attn_gate_w"]),
        l0a_up=pretile_lhs(inp["L0_attn_up_w"]),
        l0a_down=pretile_lhs(inp["L0_attn_down_w"]),
        l0a_convw=np.ascontiguousarray(
            inp["L0_attn_conv_w"].reshape(34, 128, 4).transpose(1, 0, 2).reshape(128, 136)),
        l0a_convb=np.ascontiguousarray(inp["L0_attn_conv_b"].reshape(34, 128).T),
        l0a_ln1=np.ascontiguousarray(inp["L0_attn_ln1_w"].reshape(16, 128).T),
        l0a_ln2=np.ascontiguousarray(inp["L0_attn_ln2_w"].reshape(16, 128).T),
        l0a_gnorm=np.ascontiguousarray(inp["L0_attn_gnorm_w"].reshape(32, 128).T),
        l0a_Drep=np.ascontiguousarray(np.repeat(inp["L0_attn_D"], 64).reshape(32, 128).T),
        l0a_dtb=np.ascontiguousarray(inp["L0_attn_dt_bias"].reshape(64, 1)),
        l0a_alog=np.ascontiguousarray(inp["L0_attn_A_log"].reshape(64, 1)),
    )
    Exp = np.zeros((64, 32 * 128), np.float32)
    for j in range(32):
        for r in range(128):
            Exp[(128 * j + r) // 64, 128 * j + r] = 1.0
    d["l0a_expand"] = bf16(Exp)
    return d


def emit_slot_l0attn(p, c, x_h, xout_h, deps=(), stages=9):
    T = 2048
    nc = p.nc
    z_scr = p.dram("l0a_zscr", [32, 128, T], BF16, kind="Internal")
    xs_scr = p.dram("l0a_xsscr", [NCH, 32, 128, CC], BF16, kind="Internal")
    y_scr = p.dram("l0a_yscr", [32, 128, T], BF16, kind="Internal")
    g_scr = p.dram("l0a_gscr", [NCH, 64, CC], F32, kind="Internal")
    v1_scr = p.dram("l0a_v1scr", [2048, T], BF16, kind="Internal")
    pp_scr = p.dram("l0a_ppscr", [64, 128, T], BF16, kind="Internal")
    wdb_scr = p.dram("l0a_wdbscr", [16, 64, 128, 128], BF16, kind="Internal")

    convw = p.sb("l0a_convws", [128, 136], F32)
    convb = p.sb("l0a_convbs", [128, 34], F32)
    ln1c = p.sb("l0a_ln1s", [128, 16], F32)
    ln2c = p.sb("l0a_ln2s", [128, 16], F32)
    gnc = p.sb("l0a_gns", [128, 32], F32)
    Dc = p.sb("l0a_Ds", [128, 32], F32)
    dtbc = p.sb("l0a_dtbs", [64, 1], F32)
    alogc = p.sb("l0a_alogs", [64, 1], F32)
    expandc = p.sb("l0a_expands", [64, 4096], BF16)
    ctoks = [p.dma("scalar", t[:], p.tensors[n], group="cst") for t, n in [
        (convw, "l0a_convw"), (convb, "l0a_convb"), (ln1c, "l0a_ln1"),
        (ln2c, "l0a_ln2"), (gnc, "l0a_gnorm"), (Dc, "l0a_Drep"),
        (dtbc, "l0a_dtb"), (alogc, "l0a_alog"), (expandc, "l0a_expand")]]
    ctok = ctoks[-1]
    U32 = p.sb("l0a_U32", [128, 128], F32)
    Ubf = p.sb("l0a_Ubf", [128, 128], BF16)
    u0 = p.op("gpsimd", lambda E: E.iota(U32[:], pattern=[[1, 128]], base=0,
                                         channel_multiplier=-1,
                                         allow_small_or_imprecise_dtypes=True),
              token=True)
    u1 = p.op("vector", lambda E: E.tensor_single_scalar(U32[:], U32[:], 0.0, ALU.is_ge),
              deps=[u0], token=True)
    u2 = p.op("vector", lambda E: E.tensor_copy(Ubf[:], U32[:]), deps=[u1], token=True)

    h1 = [p.sb(f"l0a_h1_{k}", [128, T], BF16) for k in range(16)]
    a0r = p.sb("l0a_a0r", [1, T], BF16)
    b0r = p.sb("l0a_b0r", [1, T], BF16)
    ln_toks, lnr = emit_ln_fm(p, c, x_h, h1, tag="l0a_ln", extra_rms=True,
                              deps=deps, a_b=c.scrb[0][:], b_b=c.scrb[1][:],
                              plain_rows=(a0r, b0r))
    h1_tok = ln_toks[-1]
    if stages <= 0:
        return [h1_tok]

    dt_raw = p.sb("l0a_dtraw", [64, T], F32)
    dt_sp = p.sb("l0a_dtsp", [64, T], F32)
    dtA = dt_raw
    B_fm = p.sb("l0a_Bfm", [128, T], BF16)
    C_fm = p.sb("l0a_Cfm", [128, T], BF16)
    xbcb = [p.sb(f"l0a_xbc{i}", [128, 2051], BF16) for i in range(2)]
    cvb = [p.sb(f"l0a_cv{i}", [128, T], BF16) for i in range(2)]
    slb = [p.sb(f"l0a_sl{i}", [128, T], BF16) for i in range(2)]
    mz = [p.op("vector", lambda E, i=i: E.memset(xbcb[i][:, 0:3], 0.0), token=True)
          for i in range(2)]

    m_list = [66] + list(range(32, 66)) + list(range(32))
    st_state = {"conv": {}, "spill": {}, "sil": {}, "zs": {}, "dt": None}

    def ip_cb(m, tt, ps, mm_tok):
        if m == 66:
            t = p.op("vector", lambda E, tt=tt, ps=ps: E.tensor_copy(
                dt_raw[:, tt * 512:(tt + 1) * 512], ps[0:64, :]), deps=[mm_tok],
                token=True)
            st_state["dt"] = t
            return t
        if m >= 32:
            ch = m - 32
            i = ch % 2
            ev = p.op("vector", lambda E, ps=ps, tt=tt, i=i: E.tensor_copy(
                xbcb[i][:, 3 + tt * 512:3 + (tt + 1) * 512], ps),
                deps=[mm_tok, mz[i], st_state["conv"].get(ch - 2)], token=True)
            if tt < 3:
                return ev
            w4 = [convw[:, ch * 4 + j:ch * 4 + j + 1] for j in range(4)]
            t0 = p.op("vector", lambda E, i=i, w=w4[0]: E.tensor_scalar(
                cvb[i][:], xbcb[i][:, 0:2048], w, None, ALU.mult, ALU.bypass),
                deps=[ev, ctok, st_state["sil"].get(ch - 2)], token=True)
            for j in range(1, 4):
                t0 = p.op("vector", lambda E, i=i, j=j, w=w4[j]: E.scalar_tensor_tensor(
                    cvb[i][:], xbcb[i][:, j:j + 2048], w, cvb[i][:],
                    ALU.mult, ALU.add), deps=[t0], token=True)
            st_state["conv"][ch] = t0
            if ch < 32:
                sil = p.op("scalar", lambda E, i=i, ch=ch: E.activation(
                    out=slb[i][:], in_=cvb[i][:], func=AF.Silu,
                    bias=convb[:, ch:ch + 1]),
                    deps=[t0, st_state["spill"].get(ch - 2)], token=True)
                st_state["sil"][ch] = sil
                dst = bass.AP(xs_scr.tensor, ch * 16384,
                              [[128, 128], [32 * 16384, 16], [1, 128]])
                sp = p.dma("gpsimd", dst, slb[i][:], deps=[sil], group=f"xs{i}")
                st_state["spill"][ch] = sp
                return sp
            else:
                dstt = B_fm if ch == 32 else C_fm
                sil = p.op("scalar", lambda E, i=i, ch=ch, dstt=dstt: E.activation(
                    out=dstt[:], in_=cvb[i][:], func=AF.Silu,
                    bias=convb[:, ch:ch + 1]), deps=[t0], token=True)
                st_state["sil"][ch] = sil
                return sil
        ch = m
        i = ch % 2
        ev = p.op("vector", lambda E, ps=ps, tt=tt, i=i: E.tensor_copy(
            slb[i][:, tt * 512:(tt + 1) * 512], ps),
            deps=[mm_tok, st_state["zs"].get(ch - 2,
                  st_state["spill"].get(30 + ch))], token=True)
        if tt < 3:
            return ev
        zs = p.dma("gpsimd", z_scr[ch], slb[i][:], deps=[ev], group=f"zs{i}")
        st_state["zs"][ch] = zs
        return [ev, zs]

    emit_gemm_fm(p, lambda k: h1[k][:], "l0a_inproj", 67, 16, ip_cb,
                 act_tok=h1_tok, banks=c.bank, tag="l0aip", c=c, bufset=0,
                 m_list=m_list,
                 scale_col=lambda k: ln1c[:, k:k + 1], scale_tok=ctok)
    ip_tok = st_state["zs"][30]
    ip_tok2 = st_state["zs"][31]
    bc_sil = [st_state["sil"][32], st_state["sil"][33]]
    if stages <= 1:
        return [ip_tok, ip_tok2]

    d1 = p.op("vector", lambda E: E.tensor_scalar(
        dt_raw[:], dt_raw[:], dtbc[:], None, ALU.add, ALU.bypass),
        deps=[st_state["dt"], ctok], token=True)
    d2 = p.op("scalar", lambda E: E.activation(out=dt_sp[:], in_=dt_raw[:], func=AF.Abs),
              deps=[d1], token=True)
    d3 = p.op("scalar", lambda E: E.activation(out=dt_sp[:], in_=dt_sp[:], func=AF.Exp,
                                               scale=-1.0), deps=[d2], token=True)
    onec, onet = c.cc(1.0)
    d4 = p.op("scalar", lambda E: E.activation(out=dt_sp[:], in_=dt_sp[:], func=AF.Ln,
                                               bias=onec[0:64, :]), deps=[d3, onet], token=True)
    d5 = p.op("vector", lambda E: E.tensor_single_scalar(dt_raw[:], dt_raw[:], 0.0,
                                                         ALU.max), deps=[d4], token=True)
    d6 = p.op("vector", lambda E: E.tensor_add(dt_sp[:], dt_sp[:], dt_raw[:]),
              deps=[d5], token=True)
    a1 = p.op("scalar", lambda E: E.activation(out=alogc[:], in_=alogc[:], func=AF.Exp),
              deps=[ctok], token=True)
    a2 = p.op("vector", lambda E: E.tensor_scalar(alogc[:], alogc[:], -1.0, None,
                                                  ALU.mult, ALU.bypass), deps=[a1], token=True)
    d7 = p.op("vector", lambda E: E.tensor_scalar(dtA[:], dt_sp[:], alogc[:], None,
                                                  ALU.mult, ALU.bypass), deps=[d6, a2],
              token=True)
    dt_tok = d7
    if stages <= 2:
        return [dt_tok]

    g_sb = None
    gcs = p.sb("l0a_gcs", [64, 128], F32)
    dtat = p.sb("l0a_dtat", [128, 64], F32)
    g_toks = []
    prevev = None
    prevgs = None
    for ci in range(NCH):
        b5 = c.bank[5]
        b6 = c.bank[6]
        tr = p.op("tensor", lambda E, ci=ci, b5=b5: E.transpose(
            b5[0:128, 0:64],
            dtA[0:64, ci * 128:(ci + 1) * 128], c.ident[0:64, 0:64]),
            deps=[dt_tok, u2, prevev], token=True)
        te = p.op("vector", lambda E, b5=b5: E.tensor_copy(dtat[:], b5[0:128, 0:64]),
                  deps=[tr], token=True)
        gm = p.op("tensor", lambda E, ci=ci, b6=b6: E.matmul(
            b6[0:64, 0:128], dtat[:, 0:64], U32[:], start=True, stop=True),
            deps=[te], token=True)
        ge = p.op("vector", lambda E, ci=ci, b6=b6: E.tensor_copy(
            gcs[:], b6[0:64, 0:128]), deps=[gm, prevgs], token=True)
        gs = p.dma("gpsimd", g_scr[ci], gcs[:], deps=[ge], group="gsp")
        prevgs = gs
        prevev = ge
        g_toks.append(gs)
    g_tok = g_toks[-1]
    if stages <= 3:
        return [g_tok]
    xs_sp = [st_state["spill"][30], st_state["spill"][31]]
    ns = NS()
    for nm in ["z_scr", "xs_scr", "y_scr", "g_scr", "v1_scr", "pp_scr",
               "wdb_scr", "gnc", "Dc", "ln2c", "expandc", "U32", "Ubf",
               "dt_sp", "dtA", "B_fm", "C_fm", "g_sb", "dtat", "g_tok",
               "ip_tok", "ip_tok2", "bc_sil", "ctok", "u2", "h1", "lnr",
               "xs_sp", "x_h", "a0r", "b0r"]:
        setattr(ns, nm, locals()[nm])
    return scan_and_rest(p, c, x_h, xout_h, ns, stages)


def scan_and_rest(p, c, x_h, xout_h, ns, stages):
    T = 2048
    h1 = ns.h1
    WT = lambda j: h1[j // 4][:, (j % 4) * 512:(j % 4) * 512 + 512]
    WTh = lambda h: h1[h // 16][:, (h % 16) * 128:(h % 16) * 128 + 128]
    Xdt = lambda j: h1[4 + j // 4][:, (j % 4) * 512:(j % 4) * 512 + 512]
    Zsl = lambda j: h1[6 + j // 4][:, (j % 4) * 512:(j % 4) * 512 + 512]
    Xdt64 = lambda q: h1[4 + q // 32][:, (q % 32) * 64:(q % 32) * 64 + 64]
    Sbf = lambda j: h1[8 + j // 4][:, (j % 4) * 512:(j % 4) * 512 + 512]
    Sbf128 = lambda j: h1[8 + j // 16][:, (j % 16) * 128:(j % 16) * 128 + 128]
    xsb = [h1[10:12], h1[12:14]]
    xs_pc = lambda b, ch: xsb[b][ch // 16][:, (ch % 16) * 128:(ch % 16) * 128 + 128]
    egx_t = h1[14]
    y1b = h1[15]

    S_T = p.sb("l0a_ST", [128, 4096], F32)
    gc = p.sb("l0a_gc", [64, 128], F32)
    gt_c = p.sb("l0a_gtc", [128, 64], F32)
    dtt_c = p.sb("l0a_dttc", [128, 64], F32)
    w2t = p.sb("l0a_w2t", [128, 64], F32)
    w2f = p.sb("l0a_w2f", [64, 128], F32)
    Egbf = p.sb("l0a_egbf", [64, 128], BF16)
    Eglb = p.sb("l0a_eglb", [128, 64], BF16)
    eglr = p.sb("l0a_eglr", [1, 64], BF16)
    Gm = p.sb("l0a_gm", [128, 128], BF16)
    btm = p.sb("l0a_btm", [128, 128], BF16)
    Vb = p.sb("l0a_vb", [128, 512], F32)
    e0b = p.sb("l0a_e0b", [128, 512], BF16)
    tmpf = p.sb("l0a_tmpf", [128, 512], F32)

    sz = p.op("vector", lambda E: E.memset(S_T[:], 0.0), token=True)
    sz2 = p.op("vector", lambda E: E.memset(h1[8][:], 0.0), deps=[ns.ip_tok], token=True)
    sz3 = p.op("vector", lambda E: E.memset(h1[9][:], 0.0), deps=[ns.ip_tok], token=True)

    epsc, epst = c.cc(1e-5)
    prev = {}
    yspill = {}
    last_yg = []
    last_state = [sz, sz2, sz3]
    for ci in range(NCH):
        cs = slice(ci * 128, (ci + 1) * 128)
        bb = ci % 2
        xld0 = p.dma("gpsimd", xsb[bb][0][:],
                     bass.AP(ns.xs_scr.tensor, ci * 32 * 16384,
                             [[128, 128], [16384, 16], [1, 128]]),
                     deps=[prev.get(("xsfree", bb))] + (ns.xs_sp if ci < 2 else []),
                     group=f"xsl{bb}")
        xld1 = p.dma("gpsimd", xsb[bb][1][:],
                     bass.AP(ns.xs_scr.tensor, ci * 32 * 16384 + 16 * 16384,
                             [[128, 128], [16384, 16], [1, 128]]),
                     deps=[], group=f"xsl{bb}")
        gld = p.dma("gpsimd", gc[:], ns.g_scr[ci], deps=[ns.g_tok, prev.get("gfree")],
                    group="gld")
        glcol = gc[:, 127:128]
        w1 = p.op("vector", lambda E: E.tensor_scalar(w2f[:], gc[:], glcol, None,
                                                      ALU.subtract, ALU.bypass),
                  deps=[gld, prev.get("w2free")], token=True)
        w2 = p.op("scalar", lambda E: E.activation(out=w2f[:], in_=w2f[:], func=AF.Exp,
                                                   scale=-1.0), deps=[w1], token=True)
        eg = p.op("scalar", lambda E: E.activation(out=Egbf[:], in_=gc[:], func=AF.Exp),
                  deps=[gld, prev.get("egfree")], token=True)
        b4 = c.bank2[2]
        trw = p.op("tensor", lambda E, b4=b4: E.transpose(
            b4[0:128, 0:64], w2f[:], c.ident[0:64, 0:64]),
            deps=[w2, prev.get("b4free"), c.ident_tok], token=True)
        trg = p.op("tensor", lambda E, b4=b4: E.transpose(
            b4[0:128, 256:320], gc[:], c.ident[0:64, 0:64]), deps=[gld, trw], token=True)
        trd = p.op("tensor", lambda E, b4=b4, cs=cs: E.transpose(
            b4[0:128, 512:576], ns.dt_sp[0:64, cs], c.ident[0:64, 0:64]),
            deps=[trg], token=True)
        ew = p.op("vector", lambda E, b4=b4: E.tensor_copy(w2t[:], b4[0:128, 0:64]),
                  deps=[trw], token=True)
        eg2 = p.op("vector", lambda E, b4=b4: E.tensor_copy(gt_c[:], b4[0:128, 256:320]),
                   deps=[trg], token=True)
        ed = p.op("vector", lambda E, b4=b4: E.tensor_copy(dtt_c[:], b4[0:128, 512:576]),
                  deps=[trd], token=True)
        b4free = ed
        e1 = p.op("scalar", lambda E: E.activation(out=w2f[:, 0:1], in_=glcol,
                                                   func=AF.Exp), deps=[ew, gld], token=True)
        tr2 = p.op("tensor", lambda E, b4=b4: E.transpose(
            b4[0:1, 768:832], w2f[:, 0:1], c.ident[0:64, 0:64]), deps=[e1, ed], token=True)
        e2 = p.op("vector", lambda E, b4=b4: E.tensor_copy(eglr[:], b4[0:1, 768:832]),
                  deps=[tr2], token=True)
        e3 = p.op("tensor", lambda E, b4=b4: E.matmul(
            b4[:, 768:832], c.ones_bf[:],
            eglr[:], start=True, stop=True), deps=[e2, c.ones_tok], token=True)
        e4 = p.op("vector", lambda E, b4=b4: E.tensor_copy(Eglb[:], b4[:, 768:832]),
                  deps=[e3], token=True)
        b4free2 = e4
        bg = c.bank2[3]
        gmm = p.op("tensor", lambda E, bg=bg, cs=cs: E.matmul(
            bg[:, 0:128], ns.B_fm[:, cs], ns.C_fm[:, cs], start=True, stop=True),
            deps=[ns.bc_sil[0], ns.bc_sil[1], prev.get("bgfree"), ed], token=True)
        gmask = p.op("vector", lambda E, bg=bg: E.tensor_mul(
            Gm[:], bg[:, 0:128], ns.Ubf[:]), deps=[gmm, ns.u2, prev.get("gmfree")],
            token=True)
        trb = p.op("tensor", lambda E, bg=bg, cs=cs: E.transpose(
            bg.bitcast(BF16)[0:128, 256:384], ns.B_fm[:, cs], c.ident_bf[:]),
            deps=[gmask], token=True)
        eb = p.op("vector", lambda E, bg=bg: E.tensor_copy(
            btm[:], bg.bitcast(BF16)[0:128, 256:384]), deps=[trb], token=True)
        bgfree = eb
        xdt_toks = []
        for q in range(8):
            bx = c.bank2[q % 2]
            trts = []
            for s4 in range(4):
                ch = q * 4 + s4
                trts.append(p.op("tensor", lambda E, bx=bx, s4=s4, ch=ch, bb=bb: E.transpose(
                    bx.bitcast(BF16)[0:128, s4 * 128:(s4 + 1) * 128],
                    xs_pc(bb, ch), c.ident_bf[:]),
                    deps=([xld0, xld1, prev.get(("xdtp", q % 2))] if s4 == 0 else []),
                    token=(s4 == 3)))
            dtv = dtt_c[:, q * 8:(q + 1) * 8].unsqueeze(2).broadcast_to([128, 8, 64])
            xt = p.op("vector", lambda E, q=q, bx=bx, dtv=dtv: E.tensor_tensor(
                Xdt(q).rearrange("p (h x) -> p h x", h=8),
                bx.bitcast(BF16)[0:128, 0:512].rearrange("p (h x) -> p h x", h=8),
                dtv, ALU.mult), deps=[trts[-1], ed], token=True)
            prev[("xdtp", q % 2)] = xt
            w2v = w2t[:, q * 8:(q + 1) * 8].unsqueeze(2).broadcast_to([128, 8, 64])
            zt = p.op("gpsimd", lambda E, q=q, w2v=w2v: E.tensor_tensor(
                Zsl(q).rearrange("p (h x) -> p h x", h=8),
                Xdt(q).rearrange("p (h x) -> p h x", h=8),
                w2v, ALU.mult), deps=[xt, ew], token=True)
            xdt_toks.append(zt)
        xdt_tok = xdt_toks[-1]
        wt_toks = []
        for j in range(16):
            vd = p.dma("sync", Vb[:], ns.g_scr[ci].rearrange("h x -> (h x)")
                       [j * 512:(j + 1) * 512].unsqueeze(0).broadcast_to([128, 512]),
                       deps=[ns.g_tok, wt_toks[-1] if wt_toks else prev.get("vfree")],
                       group="vbc")
            gv = gt_c[:, 4 * j:4 * (j + 1)].unsqueeze(2).broadcast_to([128, 4, 128])
            s1 = p.op("vector", lambda E, j=j, gv=gv: E.scalar_tensor_tensor(
                e0b[:].rearrange("p (h x) -> p h x", h=4),
                Vb[:].rearrange("p (h x) -> p h x", h=4), 1.0, gv,
                ALU.mult, ALU.subtract), deps=[vd, eg2], token=True)
            s2 = p.op("vector", lambda E, j=j: E.tensor_single_scalar(
                e0b[:], e0b[:], 0.0, ALU.min), deps=[s1], token=True)
            s3 = p.op("scalar", lambda E, j=j: E.activation(
                out=WT(j), in_=e0b[:], func=AF.Exp),
                deps=[s2, prev.get(("wtfree", 0)) if j == 0 else None],
                token=True)
            gmv = Gm[:].unsqueeze(1).broadcast_to([128, 4, 128])
            s4_ = p.op("gpsimd", lambda E, j=j, gmv=gmv: E.tensor_tensor(
                WT(j).rearrange("p (h x) -> p h x", h=4),
                WT(j).rearrange("p (h x) -> p h x", h=4), gmv, ALU.mult),
                deps=[s3, gmask], token=True)
            wt_toks.append(s4_)
        wt_tok = wt_toks[-1]
        prev["vfree"] = wt_toks[-1]
        ygrp_toks = []
        for grp in range(4):
            bi = c.bank2[0]
            bn = c.bank2[1]
            be = c.bank2[2]
            ddw = [wt_tok, xdt_tok, b4free2] + list(last_yg)
            intra_tok = None
            for sub in range(8):
                j = grp * 8 + sub
                h0 = 2 * j
                for hh in range(2):
                    h = h0 + hh
                    intra_tok = p.op("tensor", lambda E, j=j, h=h, hh=hh, sub=sub, bi=bi: E.matmul(
                        bi[64 * hh:64 * hh + 64, sub * 128:(sub + 1) * 128],
                        Xdt64(h), WTh(h),
                        start=True, stop=True, tile_position=(0, 64 * hh)),
                        deps=ddw if (sub == 0 and hh == 0) else (),
                        token=(sub == 7 and hh == 1))
            inter_tok = None
            for sub in range(8):
                j = grp * 8 + sub
                inter_tok = p.op("tensor", lambda E, j=j, sub=sub, bn=bn, cs=cs: E.matmul(
                    bn[:, sub * 128:(sub + 1) * 128], Sbf128(j), ns.C_fm[:, cs],
                    start=True, stop=True),
                    deps=[last_state[0]] if sub == 0 else (), token=(sub == 7))
            for sub in range(8):
                j = grp * 8 + sub
                tok = p.op("tensor", lambda E, j=j, sub=sub, be=be: E.matmul(
                    be[:, sub * 128:(sub + 1) * 128],
                    ns.expandc[:, j * 128:(j + 1) * 128], Egbf[:],
                    start=True, stop=True),
                    deps=[eg] if sub == 0 else (), token=(sub == 7))
            mme = tok
            ex = p.op("scalar", lambda E, be=be: E.copy(egx_t[:, 0:1024], be[:]),
                      deps=[mme], token=True)
            ty1 = p.op("vector", lambda E, bn=bn: E.tensor_mul(
                y1b[:, 0:1024], bn[:], egx_t[:, 0:1024]), deps=[ex, prev.get("yf")],
                token=True)
            y2s = []
            for sub in range(8):
                j = grp * 8 + sub
                xpiece = xs_pc(bb, j)
                y2s.append(p.op("vector", lambda E, j=j, sub=sub, bi=bi, xpiece=xpiece: E.scalar_tensor_tensor(
                    y1b[:, 1024 + sub * 128:1024 + (sub + 1) * 128], xpiece,
                    ns.Dc[:, j:j + 1],
                    bi[:, sub * 128:(sub + 1) * 128], ALU.mult, ALU.add),
                    deps=[intra_tok if sub == 0 else None, ty1], token=(sub == 7)))
            yf = p.op("gpsimd", lambda E: E.tensor_add(
                egx_t[:, 1024:2048], y1b[:, 0:1024], y1b[:, 1024:2048]),
                deps=[y2s[-1], ty1] + prev.get("ysp", []), token=True)
            prev["yf"] = yf
            for half in range(2):
                dst = bass.AP(ns.y_scr.tensor, (grp * 8 + half * 4) * 128 * T + ci * 128,
                              [[2048, 128], [128 * T, 4], [1, 128]])
                yspill[(ci, grp, half)] = p.dma(
                    "gpsimd", dst, egx_t[:, 1024 + half * 512:1024 + (half + 1) * 512],
                    deps=[yf], group=f"ysp{half}")
            prev["ysp"] = [yspill[(ci, grp, 0)], yspill[(ci, grp, 1)]]
            ygrp_toks.append([ty1, y2s[-1], yf, ex])
            last_yg = [ex, ty1, y2s[-1], inter_tok, yf]
        st_toks = []
        for sg in range(2):
            bs0 = c.bank2[0]
            bs1 = c.bank2[1]
            contrib_tok = None
            for sl4 in range(4):
                sl = sg * 4 + sl4
                bsx = bs0 if sl4 < 2 else bs1
                off = (sl4 % 2) * 512
                contrib_tok = p.op("tensor", lambda E, sl=sl, bsx=bsx, off=off: E.matmul(
                    bsx[:, off:off + 512], btm[:], Zsl(sl), start=True, stop=True),
                    deps=(([eb, xdt_tok] + list(ygrp_toks[-1])) if sg == 0 else
                          [st_toks[-1]]) if sl4 == 0 else (),
                    token=(sl4 == 3))
            for sl4 in range(4):
                sl = sg * 4 + sl4
                bsx = bs0 if sl4 < 2 else bs1
                off = (sl4 % 2) * 512
                egv = Eglb[:, sl * 4:(sl + 1) * 4]
                egv = egv.unsqueeze(2).broadcast_to([128, 4, 128])
                t1_ = p.op("gpsimd", lambda E, sl=sl, egv=egv: E.tensor_tensor(
                    tmpf[:].rearrange("p (h x) -> p h x", h=4),
                    S_T[:, sl * 512:(sl + 1) * 512].rearrange("p (h x) -> p h x", h=4),
                    egv, ALU.mult), deps=[e4, st_toks[-1] if st_toks else None,
                                          contrib_tok], token=True)
                t2_ = p.op("vector", lambda E, sl=sl, bsx=bsx, off=off: E.tensor_add(
                    S_T[:, sl * 512:(sl + 1) * 512], tmpf[:], bsx[:, off:off + 512]),
                    deps=[t1_], token=True)
                t3_ = p.op("vector", lambda E, sl=sl: E.tensor_copy(
                    Sbf(sl), S_T[:, sl * 512:(sl + 1) * 512]), deps=[t2_], token=True)
                st_toks.append(t3_)
        last_state = [st_toks[-1], st_toks[-2], st_toks[-3]]
        prev[("xsfree", bb)] = st_toks[-1]
        prev["gfree"] = e4
        prev["w2free"] = zt
        prev["egfree"] = mme
        prev["b4free"] = b4free
        prev["gmfree"] = s4_
        prev["bgfree"] = bgfree
        prev[("wtfree", 0)] = intra_tok
    scan_tok = st_toks[-1]
    ylast = [yspill[(NCH - 1, g, h)] for g in range(4) for h in range(2)]
    if stages <= 4:
        return [scan_tok] + ylast
    return gating_and_ffn(p, c, x_h, xout_h, ns, scan_tok, ylast, stages)


def gating_and_ffn(p, c, x_h, xout_h, ns, scan_tok, ylast, stages):
    T = 2048
    h1 = ns.h1
    epsc, epst = c.cc(1e-5)
    moA_scr = p.dram("l0a_moascr", [16, 128, T], BF16, kind="Internal")

    yb, zb = c.scrb[2], c.scrb[3]
    sq_mm_tok = None
    prev_sp = {}
    last_sq = None
    for ch in range(32):
        yld = p.dma("gpsimd", yb[:], ns.y_scr[ch],
                    deps=[prev_sp.get(ch - 1)] + (ylast if ch == 0 else []),
                    group="g1y")
        zld = p.dma("gpsimd", zb[:], ns.z_scr[ch],
                    deps=[last_sq, ns.ip_tok, ns.ip_tok2] if ch == 0 else [last_sq],
                    group="g1z")
        sl = p.op("scalar", lambda E: E.activation(out=zb[:], in_=zb[:], func=AF.Silu),
                  deps=[zld], token=True)
        g1 = p.op("vector", lambda E: E.tensor_mul(yb[:], yb[:], zb[:]),
                  deps=[yld, sl], token=True)
        sq = p.op("vector", lambda E: E.tensor_mul(zb[:], yb[:], yb[:]),
                  deps=[g1], token=True)
        for tt in range(4):
            sq_mm_tok = p.op("tensor", lambda E, tt=tt, ch=ch: E.matmul(
                c.bank[tt][0:1, :], c.ones_col[:], zb[:, tt * 512:(tt + 1) * 512],
                start=(ch == 0), stop=(ch == 31)),
                deps=[sq, c.ones_tok] + ([scan_tok] if (ch == 0 and tt == 0) else [])
                if tt == 0 else (),
                token=(tt == 3))
        last_sq = sq_mm_tok
        prev_sp[ch] = p.dma("gpsimd", ns.y_scr[ch], yb[:], deps=[sq], group="g1sp")
    g1_tok = prev_sp[31]
    rstd_b = c.scrb[0]
    rb_toks = []
    for tt in range(4):
        pc = c.bank[tt][0:1, :]
        r1 = p.op("scalar", lambda E, pc=pc: E.mul(pc, pc, 1.0 / 4096.0),
                  deps=[sq_mm_tok], token=True)
        r2 = p.op("scalar", lambda E, pc=pc: E.activation(out=pc, in_=pc, func=AF.Ln,
                                                          bias=epsc[0:1, :]),
                  deps=[r1, epst], token=True)
        r3 = p.op("scalar", lambda E, pc=pc: E.activation(out=pc, in_=pc, func=AF.Exp,
                                                          scale=-0.5), deps=[r2], token=True)
        rb_toks.append(p.op("scalar", lambda E, pc=pc, tt=tt: E.copy(
            c.rowbf[0:1, tt * 512:(tt + 1) * 512], pc), deps=[r3, g1_tok], token=True))
    ta = emit_bcast_row(p, c, c.rowbf[0:1, :],
                        [rstd_b[:, tt * 512:(tt + 1) * 512] for tt in range(4)],
                        deps=rb_toks, banks=(4, 5))
    rstd_tok = ta[-1]

    a0b_t, b0b_t = None, None
    v1st = []
    for rnd in range(2):
        lds = []
        for kk in range(16):
            ch = rnd * 16 + kk
            lds.append(p.dma("gpsimd", h1[kk][:], ns.y_scr[ch],
                             deps=([g1_tok] if rnd == 0 else [moA_tok, moA_tok2])
                             if kk == 0 else [],
                             group="ygl"))
        ap_toks = []
        for kk in range(16):
            ap_toks.append(p.op("vector", lambda E, kk=kk: E.tensor_mul(
                h1[kk][:], h1[kk][:], rstd_b[:]), deps=[lds[-1], rstd_tok], token=True))
        yg_tok = ap_toks[-1]
        if rnd == 0:
            mo_st = {}

            def cbA(m, tt, ps, mm_tok):
                n = m * 4 + tt
                i = n % 2
                ev = p.op("vector", lambda E, ps=ps, i=i, tt=tt: E.tensor_copy(
                    c.scr32[i].bitcast(BF16)[:, 0:512], ps),
                    deps=[mm_tok, mo_st.get(n - 2)], token=True)
                mo_st[n] = p.dma("gpsimd", moA_scr[m, :, tt * 512:(tt + 1) * 512],
                                 c.scr32[i].bitcast(BF16)[:, 0:512], deps=[ev],
                                 group=f"moa{i}")
                return ev

            emit_gemm_fm(p, lambda k: h1[k][:], "l0a_outproj", 16, 16, cbA,
                         act_tok=yg_tok, banks=c.bank, tag="l0aopA", c=c, bufset=0,
                         w_deps=[yg_tok], scale_col=lambda k: ns.gnc[:, k:k + 1],
                         scale_tok=ns.ctok)
            moA_tok = mo_st[63]
            moA_tok2 = mo_st[62]
        else:
            a0b_t = p.tensors["l0a_cv0"]
            b0b_t = p.tensors["l0a_cv1"]
            t1 = emit_bcast_row(p, c, ns.a0r[0:1, :],
                                [a0b_t[:, tt * 512:(tt + 1) * 512] for tt in range(4)],
                                deps=[yg_tok, moA_tok, moA_tok2], banks=(4, 5))
            t2 = emit_bcast_row(p, c, ns.b0r[0:1, :],
                                [b0b_t[:, tt * 512:(tt + 1) * 512] for tt in range(4)],
                                deps=[t1[-1]], banks=(4, 5))
            ab_tok = t2[-1]
            xst = [c.scr32[0][:, 0:512], c.scr32[0][:, 512:1024]]
            most = [c.scr32[1].bitcast(BF16)[:, 0:512], c.scr32[1].bitcast(BF16)[:, 512:1024]]
            u1t = [c.scrb[2][:, 0:512], c.scrb[2][:, 512:1024]]
            v1b = [c.scrb[3][:, 0:512], c.scrb[3][:, 512:1024]]
            ev_h = {}

            def cbB(m, tt, ps, mm_tok):
                n = m * 4 + tt
                i = n % 2
                xt = p.dma("gpsimd", xst[i], x_h[m * 128:(m + 1) * 128,
                                                 tt * 512:(tt + 1) * 512],
                           deps=[ev_h.get(n - 2, ab_tok)], group=f"xl{i}")
                mt = p.dma("gpsimd", most[i], moA_scr[m, :, tt * 512:(tt + 1) * 512],
                           deps=[ev_h.get(n - 2, moA_tok)], group=f"mol{i}")
                w1 = p.op("vector", lambda E, i=i: E.tensor_mul(
                    u1t[i], xst[i], a0b_t[:, 0:512]),
                    deps=[xt, ab_tok, v1st[n - 2] if n >= 2 else None], token=True)
                w2 = p.op("gpsimd", lambda E, i=i: E.tensor_add(
                    u1t[i], u1t[i], b0b_t[:, 0:512]), deps=[w1], token=True)
                w3 = p.op("gpsimd", lambda E, i=i: E.tensor_add(
                    u1t[i], u1t[i], most[i]), deps=[w2, mt], token=True)
                w4 = p.op("vector", lambda E, i=i, ps=ps: E.tensor_add(
                    v1b[i], ps, u1t[i]), deps=[w3, mm_tok], token=True)
                ev_h[n] = w4
                v1st.append(p.dma("gpsimd", ns.v1_scr[m * 128:(m + 1) * 128,
                                                      tt * 512:(tt + 1) * 512],
                                  v1b[i], deps=[w4], group=f"v1s{i}"))
                return w4

            emit_gemm_fm(p, lambda k: h1[k][:], "l0a_outproj", 16, 16, cbB,
                         act_tok=[yg_tok, ab_tok], banks=c.bank, tag="l0aopB", c=c,
                         bufset=1, w_deps=[yg_tok], k_off=16,
                         scale_col=lambda k: ns.gnc[:, k + 16:k + 17],
                         scale_tok=ns.ctok)
    v1_tok = v1st[-1]
    v1_tok0 = v1st[-2]
    if stages <= 5:
        return [v1_tok, v1_tok0]
    return swiglu(p, c, x_h, xout_h, ns, [v1_tok, v1_tok0], stages)


def swiglu(p, c, x_h, xout_h, ns, v1_toks, stages):
    T = 2048
    h1 = ns.h1
    epsc, epst = c.cc(1e-5)
    sqb = c.scrb[3]
    v1l = c.scrb[2]
    smm = None
    last_sq = None
    for kk in range(16):
        ld = p.dma("gpsimd", v1l[:], ns.v1_scr[kk * 128:(kk + 1) * 128, :],
                   deps=[last_sq] + (v1_toks if kk == 0 else []), group="v1l")
        sq = p.op("vector", lambda E: E.tensor_mul(sqb[:], v1l[:], v1l[:]),
                  deps=[ld, last_sq], token=True)
        for tt in range(4):
            smm = p.op("tensor", lambda E, tt=tt, kk=kk: E.matmul(
                c.bank[tt][0:1, :], c.ones_col[:], sqb[:, tt * 512:(tt + 1) * 512],
                start=(kk == 0), stop=(kk == 15)),
                deps=[sq] if tt == 0 else (), token=(tt == 3))
        last_sq = smm
    rb_toks = []
    for tt in range(4):
        pc = c.bank[tt][0:1, :]
        r1 = p.op("scalar", lambda E, pc=pc: E.mul(pc, pc, 1.0 / 2048.0),
                  deps=[smm], token=True)
        r2 = p.op("scalar", lambda E, pc=pc: E.activation(out=pc, in_=pc, func=AF.Ln,
                                                          bias=epsc[0:1, :]),
                  deps=[r1, epst], token=True)
        r3 = p.op("scalar", lambda E, pc=pc: E.activation(out=pc, in_=pc, func=AF.Exp,
                                                          scale=-0.5), deps=[r2], token=True)
        rb_toks.append(p.op("scalar", lambda E, pc=pc, tt=tt: E.copy(
            c.rowbf[0:1, tt * 512:(tt + 1) * 512], pc), deps=[r3, last_sq], token=True))
    rstd2_b = c.scrb[0]
    ta = emit_bcast_row(p, c, c.rowbf[0:1, :],
                        [rstd2_b[:, tt * 512:(tt + 1) * 512] for tt in range(4)],
                        deps=rb_toks, banks=(4, 5))
    rstd2_tok = ta[-1]
    h2_toks = []
    lastld = None
    for kk in range(16):
        ld = p.dma("gpsimd", v1l[:], ns.v1_scr[kk * 128:(kk + 1) * 128, :],
                   deps=[rstd2_tok if kk == 0 else lastld], group="v1l")
        lastld = p.op("vector", lambda E, kk=kk: E.tensor_mul(
            h1[kk][:], v1l[:], rstd2_b[:]), deps=[ld, rstd2_tok], token=True)
        h2_toks.append(lastld)
    h2_tok = h2_toks[-1]
    if stages <= 6:
        return [h2_tok]

    wst = p.tensors["l0a_ST"]
    cvt_toks = []
    last_cv = None
    for m in range(16):
        for half in range(2):
            dms = []
            for kq in range(32):
                k = half * 32 + kq
                dms.append(p.dma("sync", wst[:, kq * 128:(kq + 1) * 128],
                                 p.tensors["l0a_down"][m, k],
                                 deps=[last_cv, h2_tok] if kq == 0 else [],
                                 group=f"wdc{half}"))
            cv = p.op("vector", lambda E, m=m, half=half: E.tensor_copy(
                p.tensors["l0a_cv0"][:], wst[:, 0:2048]), deps=[dms[-1]], token=True)
            cv2 = p.op("vector", lambda E, m=m, half=half: E.tensor_copy(
                p.tensors["l0a_cv1"][:], wst[:, 2048:4096]), deps=[dms[-1]], token=True)
            st1 = p.dma("gpsimd", bass.AP(ns.wdb_scr.tensor,
                        m * 64 * 16384 + (half * 32) * 16384,
                        [[128, 128], [16384, 16], [1, 128]]),
                        p.tensors["l0a_cv0"][:], deps=[cv], group="wds")
            st2 = p.dma("gpsimd", bass.AP(ns.wdb_scr.tensor,
                        m * 64 * 16384 + (half * 32 + 16) * 16384,
                        [[128, 128], [16384, 16], [1, 128]]),
                        p.tensors["l0a_cv1"][:], deps=[cv2], group="wds")
            last_cv = st2
            cvt_toks += [st1, st2]
    wdb_tok = cvt_toks[-1]

    gsl = [c.scrb[2][:, 0:1024], c.scrb[2][:, 1024:2048]]
    psl = [c.scrb[3][:, 0:1024], c.scrb[3][:, 1024:2048]]
    getg, postg = emit_weight_stream(p, "l0a_gate", 64, 16, c=c, bufset=0,
                                     w_deps=[h2_tok], tag="l0ag",
                                     scale_col=lambda k: p.tensors["l0a_ln2s"][:, k:k + 1],
                                     scale_tok=ns.ctok)
    getu, postu = emit_weight_stream(p, "l0a_up", 64, 16, c=c, bufset=1,
                                     w_deps=[h2_tok], tag="l0au",
                                     scale_col=lambda k: p.tensors["l0a_ln2s"][:, k:k + 1],
                                     scale_tok=ns.ctok)
    ev_hist = {}
    pp_st = {}
    for idx in range(64):
        wg, wgt = getg(idx)
        wu, wut = getu(idx)
        for half in range(2):
            hb = 2 * (half % 2)
            gbank, ubank = c.bank2[hb], c.bank2[hb + 1]
            dd = [wgt, wut, h2_tok] + list(ev_hist.get((idx - 1, half), []))
            for kk in range(16):
                for u in range(2):
                    tok_g = p.op("tensor", lambda E, kk=kk, u=u, half=half, gbank=gbank, wg=wg: E.matmul(
                        gbank[:, u * 512:(u + 1) * 512], wg[:, kk * 128:(kk + 1) * 128],
                        h1[kk][:, half * 1024 + u * 512: half * 1024 + (u + 1) * 512],
                        start=(kk == 0), stop=(kk == 15)),
                        deps=dd if (kk == 0 and u == 0) else (), token=(kk == 15 and u == 1))
            for kk in range(16):
                for u in range(2):
                    tok_u = p.op("tensor", lambda E, kk=kk, u=u, half=half, ubank=ubank, wu=wu: E.matmul(
                        ubank[:, u * 512:(u + 1) * 512], wu[:, kk * 128:(kk + 1) * 128],
                        h1[kk][:, half * 1024 + u * 512: half * 1024 + (u + 1) * 512],
                        start=(kk == 0), stop=(kk == 15)),
                        deps=(), token=(kk == 15 and u == 1))
            sg = p.op("scalar", lambda E, half=half, gbank=gbank: E.activation(
                out=gsl[half], in_=gbank[:], func=AF.Silu),
                deps=[tok_g, ev_hist.get((idx - 1, half), [None])[-1]], token=True)
            pe = p.op("vector", lambda E, half=half, ubank=ubank: E.tensor_mul(
                psl[half], gsl[half], ubank[:]),
                deps=[sg, tok_u, pp_st.get((idx - 1, half))], token=True)
            pp_st[(idx, half)] = p.dma("gpsimd",
                                       ns.pp_scr[idx, :, half * 1024:(half + 1) * 1024],
                                       psl[half], deps=[pe], group=f"pps{half}")
            ev_hist[(idx, half)] = [sg, pe]
        postg(idx, tok_g)
        postu(idx, tok_u)
    pp_tok = pp_st[(63, 1)]
    pp_tok0 = pp_st[(63, 0)]
    if stages <= 7:
        return [pp_tok, pp_tok0]

    wdA = p.tensors["l0a_ST"].bitcast(BF16)
    wdB = [p.tensors["l0a_Bfm"], p.tensors["l0a_Cfm"],
           p.tensors["l0a_sl0"], p.tensors["l0a_sl1"]]

    def wdBt(sl):
        return wdB[sl // 16][:, (sl % 16) * 128:(sl % 16) * 128 + 128]

    ptg = lambda k: h1[k // 4][:, (k % 4) * 512:(k % 4) * 512 + 512]
    xs1 = [c.scr32[0][:, 0:512], c.scr32[0][:, 512:1024]]
    v1s = [c.scr32[1].bitcast(BF16)[:, 0:512], c.scr32[1].bitcast(BF16)[:, 512:1024]]
    ost = [c.scrb[2].bitcast(F32)[:, 0:512], c.scrb[2].bitcast(F32)[:, 512:1024]]
    out_st = []
    wl_toks = {}
    ptg_toks = {}
    ev_fin = {}
    last_mm = {}
    for tg in range(4):
        pl = None
        for k in range(64):
            pl = p.dma("gpsimd", ptg(k), ns.pp_scr[k, :, tg * 512:(tg + 1) * 512],
                       deps=([pp_tok, pp_tok0, ev_fin.get(tg - 1)] if k == 0 else []),
                       group="ptl")
            ptg_toks[(tg, k)] = pl
        for mp in range(8):
            bp = c.bank2[mp % 2]
            for h_ in range(2):
                wbuf = wdA if h_ == 0 else None
                for sl in range(64):
                    k = h_ * 32 + sl // 2
                    mm_ = 2 * mp + sl % 2
                    dst = wdA[:, sl * 128:(sl + 1) * 128] if h_ == 0 else wdBt(sl)
                    wl_toks[(tg, mp, h_, sl)] = p.dma(
                        "sync", dst, ns.wdb_scr[mm_, k],
                        deps=[wdb_tok, last_mm.get((tg, mp - 1, h_),
                              last_mm.get((tg - 1, 7, h_)))] if sl == 0 else [],
                        group=f"wdl{h_}")
            mmt = None
            for h_ in range(2):
                for sl in range(64):
                    k = h_ * 32 + sl // 2
                    msub = sl % 2
                    src = wdA[:, sl * 128:(sl + 1) * 128] if h_ == 0 else wdBt(sl)
                    mmt = p.op("tensor", lambda E, src=src, msub=msub, bp=bp, k=k, sl=sl: E.matmul(
                        bp[:, msub * 512:(msub + 1) * 512], src, ptg(k),
                        start=(k == 0 and sl < 2),
                        stop=(k == 63 and sl >= 62)),
                        deps=[wl_toks[(tg, mp, h_, 0)], pl,
                              ev_fin.get((tg, mp - 2))] if sl == 0 else (),
                        token=(sl == 63))
                last_mm[(tg, mp, h_)] = mmt
            evs = []
            for msub in range(2):
                m = 2 * mp + msub
                i = msub
                xt = p.dma("gpsimd", xs1[i], x_h[m * 128:(m + 1) * 128,
                                                 tg * 512:(tg + 1) * 512],
                           deps=[ev_fin.get((tg, mp - 1), mmt)], group=f"xf{i}")
                vt = p.dma("gpsimd", v1s[i],
                           ns.v1_scr[m * 128:(m + 1) * 128, tg * 512:(tg + 1) * 512],
                           deps=[], group=f"vf{i}")
                e1 = p.op("vector", lambda E, i=i, bp=bp, msub=msub: E.tensor_add(
                    ost[i], bp[:, msub * 512:(msub + 1) * 512], v1s[i]),
                    deps=[mmt, vt, out_st[-2] if len(out_st) >= 2 else None], token=True)
                e2 = p.op("vector", lambda E, i=i: E.tensor_add(
                    ost[i], ost[i], xs1[i]), deps=[e1, xt], token=True)
                evs.append(e2)
                out_st.append(p.dma("gpsimd",
                                    xout_h[m * 128:(m + 1) * 128, tg * 512:(tg + 1) * 512],
                                    ost[i], deps=[e2], group=f"of{i}"))
            ev_fin[(tg, mp)] = evs[-1]
        ev_fin[tg] = evs[-1]
    return out_st






def _host_all(inputs):
    h = {}
    h.update(l0attn_host(inputs))
    h.update(l0ffn_host(inputs))
    h1a, blk = l1attn_host(inputs)
    h.update(h1a)
    h.update(l1ffn_host(inputs))
    return h, blk


def _build_fused(host, blk):
    p = P()
    x_h = p.dram("x", [2048, 2048])
    x1_h = p.dram("x1s", [2048, 2048], F32, kind="Internal")
    x2_h = p.dram("x2s", [2048, 2048], F32, kind="Internal")
    x3_h = p.dram("x3s", [2048, 2048], F32, kind="Internal")
    xo_h = p.dram("xo", [2048, 2048], F32, kind="ExternalOutput")
    for k, v in host.items():
        p.dram(k, list(v.shape), F32 if v.dtype == np.float32 else BF16)
    c = common_setup(p)
    p.push_scope()
    t = emit_slot_l0attn(p, c, x_h, x1_h)
    p.pop_scope()
    p.barrier()
    p.push_scope()
    t = emit_slot_l0ffn(p, c, x1_h, x2_h)
    p.pop_scope()
    p.barrier()
    p.push_scope()
    t = emit_slot_l1attn(p, c, x2_h, x3_h, blk)
    p.pop_scope()
    p.barrier()
    p.push_scope()
    t = emit_slot_l1ffn(p, c, x3_h, xo_h)
    p.pop_scope()
    p.op("gpsimd", lambda E: E.engine_nop(), deps=[t[-1], t[-2]])
    return p.build()


def kernel(**inputs):
    host, blk = _host_all(inputs)
    x = np.asarray(inputs["L0_attn_hidden_states"])
    shards = [np.ascontiguousarray(x[b].T) for b in range(8)]
    nc = _build_fused(host, blk)
    in_maps = []
    for b in range(8):
        m = dict(host)
        m["x"] = shards[b]
        in_maps.append(m)
    res = run_bass_kernel_spmd(nc, in_maps, core_ids=list(range(8)))
    out = np.stack([res.results[b]["xo"].T for b in range(8)], axis=0)
    return np.ascontiguousarray(out.astype(np.float32))


# version 04
# speedup vs baseline: 1.2664x; 1.2664x over previous
import math
from contextlib import ExitStack

import numpy as np
import ml_dtypes

import concourse.bass as bass
import concourse.mybir as mybir
from concourse.bass_utils import run_bass_kernel_spmd

F32 = mybir.dt.float32
BF16 = mybir.dt.bfloat16
AF = mybir.ActivationFunctionType
ALU = mybir.AluOpType
AX = mybir.AxisListType

ENG_ALL = ("sync", "tensor", "vector", "scalar", "gpsimd")


class P:

    def __init__(self, name="prog"):
        self.nc = bass.Bass("TRN2", target_bir_lowering=False)
        self.ops = {e: [] for e in ENG_ALL}
        self.counts = {}
        self.watermark = {e: {} for e in ENG_ALL}
        self.es = ExitStack()
        self.scopes = [self.es]
        self.pending = {e: [] for e in ENG_ALL}
        self.tensors = {}
        self.n_waits = 0
        self.n_ops = 0

    def push_scope(self):
        self.scopes.append(ExitStack())

    def pop_scope(self):
        self.scopes.pop().close()

    def barrier(self):
        snap = [(sn, v) for sn, v in self.counts.items()]
        for e in ENG_ALL:
            self.pending[e] = list(snap)

    def dram(self, name, shape, dtype=F32, kind="ExternalInput"):
        t = self.nc.dram_tensor(name, list(shape), dtype, kind=kind).ap()
        self.tensors[name] = t
        return t

    sb_bytes = 0

    def sb(self, name, shape, dtype):
        t = self.scopes[-1].enter_context(self.nc.sbuf_tensor(name, list(shape), dtype))
        self.tensors[name] = t
        n = 1
        for s in shape[1:]:
            n *= s
        self.sb_bytes += n * mybir.dt.size(dtype)
        return t

    def ps(self, name, shape, dtype=F32):
        t = self.es.enter_context(self.nc.psum_tensor(name, list(shape), dtype))
        self.tensors[name] = t
        return t

    def _sem(self, name):
        if name not in self.counts:
            self.counts[name] = 0

    def op(self, eng, fn, deps=(), token=False, dma=False, dma_group=""):
        waits = []
        if self.pending[eng]:
            deps = list(deps) + self.pending[eng]
            self.pending[eng] = []
        for d in deps:
            if d is None:
                continue
            sn, v = d
            if sn.startswith("D"):
                v = self.counts[sn]
            wm = self.watermark[eng].get(sn, 0)
            if v > wm:
                waits.append((sn, v))
                self.watermark[eng][sn] = v
        inc = None
        tok = None
        if dma:
            sn = "D" + eng + dma_group
            self._sem(sn)
            self.counts[sn] += 16
            inc = (sn, 16)
            tok = (sn, self.counts[sn])
        elif token:
            sn = "C" + eng
            self._sem(sn)
            self.counts[sn] += 1
            inc = (sn, 1)
            tok = (sn, self.counts[sn])
        self.ops[eng].append((waits, fn, inc))
        self.n_waits += len(waits)
        self.n_ops += 1
        return tok

    def dma(self, eng, out_ap, in_ap, deps=(), group=""):
        return self.op(eng, lambda E: E.dma_start(out=out_ap, in_=in_ap),
                       deps=deps, dma=True, dma_group=group)

    def build(self):
        nc = self.nc
        sems = {}
        for name in self.counts:
            sems[name] = self.es.enter_context(nc.semaphore(name))
        blk = self.es.enter_context(nc.Block())
        for eng in ENG_ALL:
            if not self.ops[eng]:
                continue
            dec = getattr(blk, eng)

            def body(E, eng=eng):
                for waits, fn, inc in self.ops[eng]:
                    for (sn, v) in waits:
                        E.wait_ge(sems[sn], v)
                    instr = fn(E)
                    if inc is not None:
                        instr.then_inc(sems[inc[0]], inc[1])

            dec(body)
        self.es.close()
        return nc



def pretile_lhs(w, transpose=False):
    if transpose:
        w = w.T
    K, N = w.shape
    nk = (K + 127) // 128
    nm = (N + 127) // 128
    wp = np.zeros((nk * 128, nm * 128), np.float32)
    wp[:K, :N] = w
    t = wp.reshape(nk, 128, nm, 128).transpose(2, 0, 1, 3)
    return np.ascontiguousarray(t)


def bf16(x):
    return np.asarray(x).astype(ml_dtypes.bfloat16)



def emit_weight_stream(p, w_name, nm, nk, wq="sync", conv_eng="vector",
                       scale_col=None, scale_tok=None, nbuf=2, tag=None,
                       m_list=None, c=None, w_deps=(), bufset=0, k_off=0,
                       scale_tile=None):
    nc = p.nc
    tag = tag or w_name
    w_h = p.tensors[w_name]
    ms = list(range(nm)) if m_list is None else list(m_list)
    nm = len(ms)
    GK = 4
    ngr = (nk + GK - 1) // GK
    nfb = 2
    if c is not None:
        f32b = (c.wf32 if bufset == 0 else c.wf32b)[:nfb]
        bfb = [b[:, 0:nk * 128] for b in (c.wbf if bufset == 0 else c.wbfb)[:nbuf]]
    else:
        f32b = [p.sb(f"{tag}_f{i}", [128, 512], F32) for i in range(nfb)]
        bfb = [p.sb(f"{tag}_b{i}", [128, nk * 128], BF16) for i in range(nbuf)]
    state = {"conv_tok": {}, "conv_m_tok": [None] * nm,
             "use_tok": [None] * nm, "issued": -1}

    def issue(mi):
        i = mi % nbuf
        m = ms[mi]
        for g in range(ngr):
            k0 = g * GK
            kw = min(GK, nk - k0)
            j = (mi * ngr + g) % nfb
            deps = list(w_deps) if mi * ngr + g < nfb else []
            ct = state["conv_tok"].get(mi * ngr + g - nfb)
            if ct is not None:
                deps.append(ct)
            nkp = w_h.tensor.shape[1]
            src = bass.AP(w_h.tensor,
                          w_h.offset + (m * nkp + k0 + k_off) * 16384,
                          [[128, 128], [16384, kw], [1, 128]])
            dtok = p.dma(wq, f32b[j][:, 0:kw * 128], src, deps=deps,
                         group=f"wb{bufset}_{(mi * ngr + g) % 4}")
            cdeps = [dtok]
            if g == 0:
                if mi - nbuf >= 0:
                    cdeps.append(state["use_tok"][mi - nbuf])
                elif mi < nbuf:
                    cdeps.extend(w_deps)
                if scale_tok is not None:
                    cdeps.append(scale_tok)
            dst = bfb[i][:, k0 * 128:(k0 + kw) * 128]
            if scale_col is not None and scale_tile is not None:
                sv = scale_tile[:, k0 + k_off:k0 + k_off + kw].unsqueeze(2)
                sv = sv.broadcast_to([128, kw, 128])
                tok = p.op(conv_eng, lambda E, j=j, dst=dst, sv=sv, kw=kw: E.tensor_tensor(
                    dst.rearrange("p (a x) -> p a x", a=kw),
                    f32b[j][:, 0:kw * 128].rearrange("p (a x) -> p a x", a=kw),
                    sv, ALU.mult), deps=cdeps, token=True)
            elif scale_col is not None:
                for q in range(kw):
                    tok = p.op(conv_eng, lambda E, j=j, k0=k0, q=q, sc=scale_col(k0 + q): E.tensor_scalar(
                        bfb[i][:, (k0 + q) * 128:(k0 + q + 1) * 128],
                        f32b[j][:, q * 128:(q + 1) * 128], sc, None,
                        ALU.mult, ALU.bypass),
                        deps=cdeps if q == 0 else (), token=(q == kw - 1))
            else:
                tok = p.op(conv_eng, lambda E, j=j, dst=dst, kw=kw: E.tensor_copy(
                    dst, f32b[j][:, 0:kw * 128]), deps=cdeps, token=True)
            state["conv_tok"][mi * ngr + g] = tok
        state["conv_m_tok"][mi] = tok

    def get(mi):
        while state["issued"] < min(mi + nbuf - 1, nm - 1):
            state["issued"] += 1
            issue(state["issued"])
        return bfb[mi % nbuf], state["conv_m_tok"][mi]

    def post_use(mi, tok):
        state["use_tok"][mi] = tok

    return get, post_use



class NS:
    pass


def common_setup(p, n_scr32=2, n_scrb=4, nk_max=16):
    c = NS()
    c.bank2 = [p.ps(f"bank2_{i}", [128, 1024], F32) for i in range(4)]
    c.bank = [c.bank2[i // 2][:, (i % 2) * 512:(i % 2) * 512 + 512] for i in range(8)]
    c.wf32 = [p.sb(f"c_wf32_{i}", [128, 512], F32) for i in range(2)]
    c.wbf = [p.sb(f"c_wbf_{i}", [128, nk_max * 128], BF16) for i in range(2)]
    c.wf32b = [p.sb(f"c_wf32b_{i}", [128, 512], F32) for i in range(2)]
    c.wbfb = [p.sb(f"c_wbfb_{i}", [128, 16 * 128], BF16) for i in range(2)]
    c.rowtmp = p.sb("c_rowtmp", [1, 512], F32)
    c.rowtmp2 = p.sb("c_rowtmp2", [1, 512], F32)
    c.ident = p.sb("c_ident", [128, 128], F32)
    c.ident_bf = p.sb("c_identbf", [128, 128], BF16)
    c.ones_bf = p.sb("c_onesbf", [1, 128], BF16)
    c.ones_col = p.sb("c_onescol", [128, 1], BF16)
    c.scr32 = [p.sb(f"c_scr32_{i}", [128, 1024], F32) for i in range(n_scr32)]
    c.scrb = [p.sb(f"c_scrb_{i}", [128, 2048], BF16) for i in range(n_scrb)]
    c.rowbf = c.scrb[2][0:1, :]
    c._cc = {}

    def cc(v):
        if v not in c._cc:
            i = len(c._cc)
            t = p.sb(f"c_cc{i}", [128, 1], F32)
            tok = p.op("vector", lambda E, t=t, v=v: E.memset(t[:], v), token=True)
            c._cc[v] = (t, tok)
        return c._cc[v]
    c.cc = cc
    cc(1e-5)
    cc(1.0)
    t0 = p.op("gpsimd", lambda E: E.iota(c.ident[:], pattern=[[-1, 128]], base=0,
                                         channel_multiplier=1,
                                         allow_small_or_imprecise_dtypes=True),
              token=True)
    t1 = p.op("vector", lambda E: E.tensor_single_scalar(
        c.ident[:], c.ident[:], 0.0, ALU.is_equal), deps=[t0], token=True)
    t2 = p.op("vector", lambda E: E.tensor_copy(c.ident_bf[:], c.ident[:]),
              deps=[t1], token=True)
    p.op("vector", lambda E: E.memset(c.ones_bf[:], 1.0), token=False)
    t3 = p.op("vector", lambda E: E.memset(c.ones_col[:], 1.0), token=True)
    c.ident_tok = t2
    c.ones_tok = t3
    return c


def emit_bcast_row(p, c, row_ap, out_bf_tiles_and_cols, deps, banks=(0, 1),
                   evac_eng="vector", T=2048):
    ntt = T // 512
    toks = []
    for tt in range(ntt):
        b = c.bank[banks[tt % len(banks)]]
        mt = p.op("tensor", lambda E, tt=tt, b=b: E.matmul(
            b[:], c.ones_bf[:], row_ap[:, tt * 512:(tt + 1) * 512],
            start=True, stop=True),
            deps=list(deps) + [c.ones_tok] + ([toks[tt - len(banks)]] if tt >= len(banks) else []),
            token=True)
        et = p.op(evac_eng, lambda E, tt=tt, b=b: E.tensor_copy(
            out_bf_tiles_and_cols[tt], b[:]), deps=[mt], token=True)
        toks.append(et)
    return toks


def emit_ln_fm(p, c, x_h, xhat_tiles, xq="gpsimd", eps=1e-5, T=2048,
               a_b=None, b_b=None, extra_rms=False, deps=(), tag="ln",
               plain_rows=None):
    nk = 16
    ntt = T // 512
    st = c.scr32
    sq = c.scrb
    r = NS()
    r.abf = c.rowbf[0:1, 0:T]
    r.bbf = c.rowbf[0:1, 0:T]
    if a_b is None:
        a_b = p.sb(f"{tag}_ab", [128, T], BF16)
        b_b = p.sb(f"{tag}_bb", [128, T], BF16)
    cast_toks = {}
    sq_toks = {}
    smm_toks = {}
    tok = None
    for k in range(nk):
        for h in range(2):
            n = 2 * k + h
            i = n % 2
            dd = [cast_toks.get(n - 2), sq_toks.get(n - 2)] + (list(deps) if n == 0 else [])
            xt = p.dma(xq, st[i][:], x_h[k * 128:(k + 1) * 128, h * 1024:(h + 1) * 1024],
                       deps=dd, group=f"ln{i}")
            cast_toks[n] = p.op("vector", lambda E, i=i: E.tensor_copy(
                sq[2][:, i * 1024:(i + 1) * 1024], st[i][:]),
                deps=[xt, smm_toks.get(n - 2)], token=True)
            sq_toks[n] = p.op("scalar", lambda E, i=i: E.activation(
                out=sq[3][:, i * 1024:(i + 1) * 1024], in_=st[i][:], func=AF.Square),
                deps=[xt, smm_toks.get(n - 2)], token=True)
            for u in range(2):
                tt = 2 * h + u
                p.op("tensor", lambda E, k=k, i=i, tt=tt, u=u: E.matmul(
                    c.bank[tt][0:1, :], c.ones_col[:],
                    sq[2][:, i * 1024 + u * 512: i * 1024 + (u + 1) * 512],
                    start=(k == 0), stop=(k == nk - 1)),
                    deps=[cast_toks[n], c.ones_tok] if u == 0 else ())
                tok = p.op("tensor", lambda E, k=k, i=i, tt=tt, u=u: E.matmul(
                    c.bank[4 + tt][0:1, :], c.ones_col[:],
                    sq[3][:, i * 1024 + u * 512: i * 1024 + (u + 1) * 512],
                    start=(k == 0), stop=(k == nk - 1)),
                    deps=[sq_toks[n]] if u == 0 else (),
                    token=(u == 1))
            smm_toks[n] = tok
    mm_tok = tok
    tmp = c.rowtmp[0:1, :]
    tmp2 = c.rowtmp2[0:1, :]
    tmp_free2 = None
    epsc, epst = c.cc(eps)
    n_ = 2048.0
    abf_toks = []
    b_toks = []
    tmp_free = None
    for tt in range(ntt):
        s_pc = c.bank[tt][0:1, :]
        q_pc = c.bank[4 + tt][0:1, :]
        t1 = p.op("scalar", lambda E, s_pc=s_pc: E.mul(s_pc, s_pc, 1.0 / n_),
                  deps=[mm_tok], token=True)
        t2 = p.op("scalar", lambda E, q_pc=q_pc: E.mul(q_pc, q_pc, 1.0 / n_),
                  deps=[mm_tok], token=True)
        t3 = p.op("scalar", lambda E, s_pc=s_pc: E.activation(
            out=tmp, in_=s_pc, func=AF.Square), deps=[t1, tmp_free], token=True)
        t4 = p.op("vector", lambda E, q_pc=q_pc: E.tensor_sub(q_pc, q_pc, tmp),
                  deps=[t2, t3], token=True)
        t5 = p.op("scalar", lambda E, q_pc=q_pc: E.activation(
            out=q_pc, in_=q_pc, func=AF.Ln, bias=epsc[0:1, :]),
            deps=[t4, epst], token=True) if not extra_rms else None
        if extra_rms:
            t4b = p.op("vector", lambda E, q_pc=q_pc: E.tensor_copy(tmp, q_pc),
                       deps=[t4], token=True)
            t5 = p.op("scalar", lambda E, q_pc=q_pc: E.activation(
                out=q_pc, in_=q_pc, func=AF.Ln, bias=epsc[0:1, :]),
                deps=[t4b, epst], token=True)
        t6 = p.op("scalar", lambda E, q_pc=q_pc: E.activation(
            out=q_pc, in_=q_pc, func=AF.Exp, scale=-0.5), deps=[t5], token=True)
        t6p = t6
        if plain_rows is not None:
            a0r, b0r = plain_rows
            pa = p.op("scalar", lambda E, q_pc=q_pc, tt=tt: E.copy(
                a0r[0:1, tt * 512:(tt + 1) * 512], q_pc), deps=[t6p], token=True)
            pb1 = p.op("vector", lambda E, s_pc=s_pc, tt=tt: E.tensor_mul(
                tmp2, s_pc, a0r[0:1, tt * 512:(tt + 1) * 512]), deps=[pa, tmp_free2],
                token=True)
            pb2 = p.op("vector", lambda E: E.tensor_scalar(
                tmp2, tmp2, -1.0, None, ALU.mult, ALU.bypass), deps=[pb1], token=True)
            tmp_free2 = p.op("vector", lambda E, tt=tt: E.tensor_copy(
                b0r[0:1, tt * 512:(tt + 1) * 512], tmp2), deps=[pb2], token=True)
            t6 = p.op("scalar", lambda E, q_pc=q_pc: E.copy(q_pc, q_pc),
                      deps=[pa], token=True) if False else t6
        if extra_rms:
            t7 = p.op("vector", lambda E, q_pc=q_pc: E.tensor_mul(tmp, tmp, q_pc),
                      deps=[t6], token=True)
            t8 = p.op("vector", lambda E, q_pc=q_pc: E.tensor_mul(tmp, tmp, q_pc),
                      deps=[t7], token=True)
            t9 = p.op("scalar", lambda E: E.activation(
                out=tmp, in_=tmp, func=AF.Ln, bias=epsc[0:1, :]), deps=[t8], token=True)
            t10 = p.op("scalar", lambda E: E.activation(
                out=tmp, in_=tmp, func=AF.Exp, scale=-0.5), deps=[t9], token=True)
            t6 = p.op("vector", lambda E, q_pc=q_pc: E.tensor_mul(q_pc, q_pc, tmp),
                      deps=[t10], token=True)
        tmp_free = t6
        ab = p.op("scalar", lambda E, q_pc=q_pc, tt=tt: E.copy(
            r.abf[:, tt * 512:(tt + 1) * 512], q_pc), deps=[t6], token=True)
        abf_toks.append(ab)
        tb1 = p.op("vector", lambda E, s_pc=s_pc, tt=tt: E.tensor_mul(
            s_pc, s_pc, r.abf[:, tt * 512:(tt + 1) * 512]), deps=[ab], token=True)
        tb2 = p.op("vector", lambda E, s_pc=s_pc: E.tensor_scalar(
            s_pc, s_pc, -1.0, None, ALU.mult, ALU.bypass), deps=[tb1], token=True)
        b_toks.append(tb2)
    ta = emit_bcast_row(p, c, r.abf, [a_b[:, tt * 512:(tt + 1) * 512] for tt in range(4)],
                        deps=abf_toks + b_toks, banks=(4, 5))
    cb_toks = []
    for tt in range(ntt):
        cb_toks.append(p.op("scalar", lambda E, tt=tt: E.copy(
            r.bbf[:, tt * 512:(tt + 1) * 512], c.bank[tt][0:1, :]),
            deps=[b_toks[tt], ta[-1]], token=True))
    tb = emit_bcast_row(p, c, r.bbf, [b_b[:, tt * 512:(tt + 1) * 512] for tt in range(4)],
                        deps=cb_toks, banks=(4, 5))
    bc_tok = [ta[-1], tb[-1]]
    ap_toks = {}
    m1_toks = {}
    for k in range(nk):
        for h in range(2):
            n = 2 * k + h
            i = n % 2
            dd2 = [m1_toks.get(n - 2)] if n >= 2 else [
                cast_toks[2 * nk - 2 + n], sq_toks[2 * nk - 2 + n]]
            xt = p.dma(xq, st[i][:], x_h[k * 128:(k + 1) * 128, h * 1024:(h + 1) * 1024],
                       deps=dd2, group=f"ln{i}")
            m1_toks[n] = p.op("vector", lambda E, k=k, h=h, i=i: E.tensor_mul(
                sq[2][:, i * 1024:(i + 1) * 1024], st[i][:],
                a_b[:, h * 1024:(h + 1) * 1024]),
                deps=[xt] + bc_tok + [ap_toks.get(n - 2)], token=True)
            ap_toks[n] = p.op("vector", lambda E, k=k, h=h, i=i: E.tensor_add(
                xhat_tiles[k][:, h * 1024:(h + 1) * 1024],
                sq[2][:, i * 1024:(i + 1) * 1024],
                b_b[:, h * 1024:(h + 1) * 1024]), deps=[m1_toks[n]], token=True)
    r.a_b, r.b_b = a_b, b_b
    r.bank_free_tok = tb[-1]
    return [ap_toks[2 * k + 1] for k in range(nk)], r


def emit_gemm_multi(p, streams, nm, out_cb, T=2048, banks=None, act_tok=None,
                    w_deps=(), m_list=None, c=None, wq="sync", conv_eng="vector"):
    ntt = T // 512
    pst = banks
    gets = []
    for si, s in enumerate(streams):
        g, pu = emit_weight_stream(p, s["name"], nm, s["nk"], wq=wq,
                                   conv_eng=conv_eng,
                                   scale_col=s.get("scale_col"),
                                   scale_tok=s.get("scale_tok"),
                                   tag=s.get("tag", s["name"]),
                                   m_list=m_list, c=c, w_deps=w_deps,
                                   bufset=s.get("bufset", si),
                                   k_off=s.get("k_off", 0),
                                   scale_tile=s.get("scale_tile"))
        gets.append((g, pu, s))
    evac_toks = {}
    all_toks = []
    ms = list(range(nm)) if m_list is None else m_list
    for mi, m in enumerate(ms):
        wts = []
        deps0 = []
        for g, pu, s in gets:
            wt, wtok = g(mi)
            wts.append(wt)
            deps0.append(wtok)
        if act_tok is not None:
            deps0.extend(act_tok if isinstance(act_tok, list) else [act_tok])
        pb = (mi % 2) * ntt
        for tt in range(ntt):
            e = evac_toks.get((mi - 2, tt))
            if e is not None:
                deps0.extend(e if isinstance(e, list) else [e])
        nktot = sum(s["nk"] for _, _, s in gets)
        kk = 0
        for si, (g, pu, s) in enumerate(gets):
            act = s["act"]
            wt = wts[si]
            for k in range(s["nk"]):
                for tt in range(ntt):
                    tok = p.op("tensor", lambda E, k=k, tt=tt, wt=wt, pb=pb, act=act,
                               first=(kk == 0), last=(kk == nktot - 1): E.matmul(
                        pst[pb + tt][:], wt[:, k * 128:(k + 1) * 128],
                        act(k)[:, tt * 512:(tt + 1) * 512],
                        start=first, stop=last),
                        deps=deps0 if (kk == 0 and tt == 0) else (),
                        token=(kk == nktot - 1 and tt == ntt - 1))
                kk += 1
        mm_tok = tok
        for tt in range(ntt):
            etok = out_cb(m, tt, pst[pb + tt][:], mm_tok)
            evac_toks[(mi, tt)] = etok
            all_toks.append(etok)
        for si, (g, pu, s) in enumerate(gets):
            pu(mi, mm_tok)
    return all_toks


def emit_gemm_fm(p, act, w_name, nm, nk, out_cb, T=2048, wq="sync",
                 conv_eng="vector", scale_col=None, scale_tok=None, tag=None,
                 act_tok=None, banks=None, m_list=None, c=None, w_deps=(),
                 bufset=0, k_off=0, scale_tile=None):
    return emit_gemm_multi(
        p, [dict(name=w_name, nk=nk, act=act, scale_col=scale_col,
                 scale_tok=scale_tok, tag=tag or w_name, bufset=bufset,
                 k_off=k_off, scale_tile=scale_tile)],
        nm, out_cb, T=T, banks=banks, act_tok=act_tok, w_deps=w_deps,
        m_list=m_list, c=c, wq=wq, conv_eng=conv_eng)



def l0ffn_consts():
    G16 = np.zeros((8, 128, 64), np.float32)
    Exp16 = np.zeros((8, 64, 128), np.float32)
    for m in range(8):
        for row in range(128):
            G16[m, row, 8 * m + row // 16] = 1.0
            Exp16[m, 8 * m + row // 16, row] = 1.0
    Sum16 = np.zeros((128, 16), np.float32)
    Tile16 = np.zeros((16, 128), np.float32)
    for row in range(128):
        Sum16[row, row % 16] = 0.25
        Tile16[row % 16, row] = 0.25
    return dict(G16=bf16(G16), Exp16=bf16(Exp16), Sum16=bf16(Sum16),
                Tile16=bf16(Tile16))


def l0ffn_host(inp):
    A = inp["L0_ffn_A"].reshape(1024, 2048)
    B_r = inp["L0_ffn_B"].transpose(0, 2, 1).reshape(1024, 2048)
    W = inp["L0_ffn_W"]
    d = dict(l0f_A=pretile_lhs(A, transpose=True),
             l0f_B=pretile_lhs(B_r),
             l0f_W=pretile_lhs(W, transpose=True),
             l0f_bias=np.ascontiguousarray(inp["L0_ffn_bias"].reshape(16, 128).T))
    d.update({"l0f_" + k: v for k, v in l0ffn_consts().items()})
    return d


def emit_slot_l0ffn(p, c, x_h, xout_h, deps=(), stages=6):
    T = 2048
    ntt = 4

    g16 = p.sb("g16", [128, 64 * 8], BF16)
    e16 = p.sb("e16", [64, 128 * 8], BF16)
    s16 = p.sb("s16", [128, 16], BF16)
    t16 = p.sb("t16", [16, 128], BF16)
    biasc = p.sb("l0f_biasc", [128, 16], F32)
    ctoks = []
    for m in range(8):
        ctoks.append(p.dma("scalar", g16[:, m * 64:(m + 1) * 64], p.tensors["l0f_G16"][m], group="cst"))
        ctoks.append(p.dma("scalar", e16[:, m * 128:(m + 1) * 128], p.tensors["l0f_Exp16"][m], group="cst"))
    ctoks += [
        p.dma("scalar", s16[:], p.tensors["l0f_Sum16"], group="cst"),
        p.dma("scalar", t16[:], p.tensors["l0f_Tile16"], group="cst"),
        p.dma("scalar", biasc[:], p.tensors["l0f_bias"], group="cst"),
    ]
    ctok = ctoks[-1]
    negbig_t = p.sb("l0f_negbig", [128, 64], F32)
    negbig = negbig_t[:]
    nbt = p.op("vector", lambda E: E.memset(negbig, -1e30), token=True)

    xhat = [p.sb(f"l0f_xh{k}", [128, T], BF16) for k in range(16)]
    ln_toks, _rows = emit_ln_fm(p, c, x_h, xhat, tag="l0f_ln", deps=deps,
                                a_b=c.scrb[0][:], b_b=c.scrb[1][:])
    xh_tok = ln_toks[-1]
    if stages <= 1:
        return [xh_tok]
    for t in ln_toks:
        p.watermark

    vecs = [p.sb(f"l0f_vec{m}", [128, T], BF16) for m in range(8)]
    vecsq = [p.sb(f"l0f_vsq{m}", [128, T], BF16) for m in range(8)]

    def vec_cb(m, tt, ps, mm_tok):
        t1 = p.op("vector", lambda E, m=m, tt=tt, ps=ps: E.tensor_copy(
            vecs[m][:, tt * 512:(tt + 1) * 512], ps), deps=[mm_tok], token=True)
        t2 = p.op("scalar", lambda E, m=m, tt=tt: E.activation(
            out=vecsq[m][:, tt * 512:(tt + 1) * 512],
            in_=vecs[m][:, tt * 512:(tt + 1) * 512], func=AF.Square),
            deps=[t1], token=True)
        return [t1, t2]

    vtoks = emit_gemm_fm(p, lambda k: xhat[k][:], "l0f_A", 8, 16, vec_cb,
                         act_tok=ln_toks[-1], banks=c.bank, tag="l0fA", c=c)
    vec_tok = vtoks[-1][1]
    vec_tok_v = vtoks[-1][0]
    if stages <= 2:
        return [vec_tok]

    M_fm = c.scrb[0][0:64, :]
    M_tm = p.sb("l0f_Mtm", [128, 64], BF16)
    msk8 = p.sb("l0f_msk8", [128, 64], mybir.dt.uint8)
    arr_t = p.sb("l0f_arr", [128, 128], F32)
    arr = [arr_t[:, 64 * i:64 * (i + 1)] for i in range(2)]
    mx = p.sb("l0f_mx", [128, 1], F32)
    tr_toks = []
    prev_ev = {}
    for tl in range(16):
        b = c.bank[tl % 2]
        bq = c.bank[2 + tl % 2]
        dd = [vec_tok, vec_tok_v, ctok] + ([prev_ev.get(tl - 2)] if tl >= 2 else [])
        for m in range(8):
            mt = p.op("tensor", lambda E, tl=tl, m=m, b=b: E.matmul(
                b[:, 0:64], vecsq[m][:, tl * 128:(tl + 1) * 128],
                g16[:, m * 64:(m + 1) * 64],
                start=(m == 0), stop=(m == 7)),
                deps=dd if m == 0 else (), token=(m == 7))
        i = tl % 2
        e0 = p.op("vector", lambda E, i=i, b=b: E.tensor_copy(arr[i], b[:, 0:64]),
                  deps=[mt], token=True)
        cur = arr[i]
        t4tok = e0
        for it in range(4):
            rmx = p.op("vector", lambda E, cur=cur: E.reduce_max(mx[:], cur, axis=AX.X),
                       deps=[t4tok], token=True)
            if it < 3:
                msk = p.op("vector", lambda E, cur=cur: E.tensor_scalar(
                    msk8[:], cur, mx[:], None, ALU.is_ge, ALU.bypass),
                    deps=[rmx], token=True)
                t4tok = p.op("vector", lambda E, cur=cur: E.copy_predicated(
                    cur, msk8[:], negbig), deps=[msk, nbt], token=True)
            else:
                t4tok = rmx
        mk = p.op("vector", lambda E, b=b: E.tensor_scalar(
            M_tm[:], b[:, 0:64], mx[:], None, ALU.is_ge, ALU.bypass),
            deps=[t4tok], token=True)
        trt = p.op("tensor", lambda E, bq=bq: E.transpose(
            bq.bitcast(BF16)[0:64, 0:128], M_tm[:], c.ident_bf[:]),
            deps=[mk, c.ident_tok, prev_ev.get(tl - 2)], token=True)
        ev = p.op("vector", lambda E, tl=tl, bq=bq: E.tensor_copy(
            M_fm[:, tl * 128:(tl + 1) * 128], bq.bitcast(BF16)[0:64, 0:128]),
            deps=[trt], token=True)
        prev_ev[tl] = ev
        tr_toks.append(ev)
    mfm_tok = tr_toks[-1]
    if stages <= 3:
        return [mfm_tok]

    v_bf = c.scrb[2][0:16, :]
    mv = c.scrb[1][:, 0:512]
    v_toks = []
    bank_free = {}
    prev_vt = None
    for tt in range(4):
        vb = c.bank[4 + tt]
        for m in range(8):
            bi = (tt * 8 + m) % 2
            b = c.bank[bi]
            dd = [mfm_tok, vec_tok, bank_free.get(bi)]
            met = p.op("tensor", lambda E, tt=tt, m=m, b=b: E.matmul(
                b[:], e16[:, m * 128:(m + 1) * 128],
                M_fm[:, tt * 512:(tt + 1) * 512], start=True, stop=True),
                deps=dd, token=True)
            mvt = p.op("vector", lambda E, tt=tt, m=m, b=b: E.tensor_tensor(
                mv, vecs[m][:, tt * 512:(tt + 1) * 512], b[:], ALU.mult),
                deps=[met, prev_vt], token=True)
            bank_free[bi] = mvt
            prev_vt = p.op("tensor", lambda E, tt=tt, m=m, vb=vb: E.matmul(
                vb[0:16, :], s16[:], mv, start=(m == 0), stop=(m == 7)),
                deps=[mvt], token=True)
        v_toks.append(p.op("vector", lambda E, tt=tt, vb=vb: E.tensor_copy(
            v_bf[:, tt * 512:(tt + 1) * 512], vb[0:16, :]), deps=[prev_vt], token=True))
    v_tok = v_toks[-1]
    if stages <= 4:
        return [v_tok]

    u_bf = vecsq
    mexps = c.scrb[1][:, 512:1024]
    u_toks = []
    last_u = None
    last_ms = None
    for m in range(8):
        for tt in range(4):
            n = m * 4 + tt
            bi, bi2 = n % 2, 2 + n % 2
            b, b2 = c.bank[bi], c.bank[bi2]
            vr = p.op("tensor", lambda E, m=m, tt=tt, b=b: E.matmul(
                b[:], t16[:], v_bf[:, tt * 512:(tt + 1) * 512], start=True, stop=True),
                deps=[v_tok, bank_free.get(bi)], token=True)
            me = p.op("tensor", lambda E, m=m, tt=tt, b2=b2: E.matmul(
                b2[:], e16[:, m * 128:(m + 1) * 128],
                M_fm[:, tt * 512:(tt + 1) * 512], start=True, stop=True),
                deps=[v_tok, bank_free.get(bi2)], token=True)
            ms = p.op("scalar", lambda E, b2=b2: E.copy(mexps, b2[:]),
                      deps=[me, last_u], token=True)
            bank_free[bi2] = ms
            last_ms = ms
            last_u = p.op("vector", lambda E, m=m, tt=tt, b=b: E.tensor_tensor(
                u_bf[m][:, tt * 512:(tt + 1) * 512], mexps, b[:], ALU.mult),
                deps=[ms, vr], token=True)
            bank_free[bi] = last_u
            u_toks.append(last_u)
    u_tok = u_toks[-1]
    if stages <= 5:
        return [u_tok]

    xst = [c.scrb[0].bitcast(F32)[:, 0:512], c.scrb[0].bitcast(F32)[:, 512:1024],
           c.scrb[1].bitcast(F32)[:, 0:512], c.scrb[1].bitcast(F32)[:, 512:1024]]
    st_toks = []
    ev_hist = []

    def out_cb(m, tt, ps, mm_tok):
        n = m * 4 + tt
        i = n % 2
        xt = p.dma("gpsimd", xst[i],
                   x_h[m * 128:(m + 1) * 128, tt * 512:(tt + 1) * 512],
                   deps=[ev_hist[n - 2]] if n >= 2 else [u_tok, last_ms],
                   group=f"xl{i}")
        et = p.op("vector", lambda E, m=m, tt=tt, ps=ps, i=i: E.scalar_tensor_tensor(
            xst[2 + i], ps, biasc[:, m:m + 1], xst[i],
            ALU.add, ALU.add),
            deps=[mm_tok, xt, ctok, st_toks[n - 2] if n >= 2 else None], token=True)
        ev_hist.append(et)
        st = p.dma("gpsimd", xout_h[m * 128:(m + 1) * 128, tt * 512:(tt + 1) * 512],
                   xst[2 + i], deps=[et], group=f"xs{i}")
        st_toks.append(st)
        return et

    emit_gemm_multi(
        p, [dict(name="l0f_B", nk=8, act=lambda k: u_bf[k][:], bufset=0),
            dict(name="l0f_W", nk=16, act=lambda k: xhat[k][:], bufset=1)],
        16, out_cb, banks=c.bank, act_tok=[u_tok, last_ms], c=c, w_deps=[u_tok])
    return st_toks



def l1ffn_host(inp):
    d = dict(
        l1f_wr=pretile_lhs(inp["L1_ffn_w_router"]),
        l1f_wg=pretile_lhs(inp["L1_ffn_w_gate"]),
        l1f_wu=pretile_lhs(inp["L1_ffn_w_up"]),
        l1f_wsg=pretile_lhs(inp["L1_ffn_shared_w_gate"]),
        l1f_wsu=pretile_lhs(inp["L1_ffn_shared_w_up"]),
        l1f_wd=pretile_lhs(inp["L1_ffn_w_down"]),
        l1f_wsd=pretile_lhs(inp["L1_ffn_shared_w_down"]),
        l1f_amp=np.ascontiguousarray(inp["L1_ffn_amplitudes"].reshape(128, 16)),
        l1f_es=np.ascontiguousarray(inp["L1_ffn_expert_scale"].reshape(1, 128)),
    )
    return d


def emit_slot_l1ffn(p, c, x_h, xout_h, deps=(), stages=9):
    T = 2048
    amp = p.sb("l1f_ampsb", [128, 16], F32)
    mixb = p.sb("l1f_mix", [128, 8], BF16)
    esrow = p.sb("l1f_essb", [1, 128], F32)
    esrowb = p.sb("l1f_esb", [1, 128], BF16)
    es_b = p.sb("l1f_esbb", [128, 128], BF16)
    Cfm = p.sb("l1f_cfm", [128, T], BF16)
    m_sb = p.sb("l1f_msb", [8, T], BF16)
    m_row = p.sb("l1f_mrow", [1, T], BF16)
    m_scr = p.dram("l1f_mscr", [8, T], BF16, kind="Internal")
    M_b = [p.sb(f"l1f_Mb{k}", [128, T], BF16) for k in range(8)]
    curp = p.sb("l1f_cur", [128, 128], F32)
    punb = p.sb("l1f_pun", [128, 128], F32)
    ptop = p.sb("l1f_ptop", [128, 128], F32)
    msk8 = p.sb("l1f_msk8", [128, 128], mybir.dt.uint8)
    mbf = p.sb("l1f_mbf", [128, 128], BF16)
    cbf = p.sb("l1f_cbf", [128, 128], BF16)
    nmx = p.sb("l1f_nmx", [128, 1], F32)
    den = p.sb("l1f_den", [128, 1], F32)
    stp = p.sb("l1f_stp", [128, 1], F32)
    dcol = p.sb("l1f_dcol", [128, 1], F32)
    negbig_t = p.sb("l1f_negbig", [128, 128], F32)
    nbt = p.op("vector", lambda E: E.memset(negbig_t[:], -1e30), token=True)
    adk = p.dma("scalar", amp[:], p.tensors["l1f_amp"], group="cst")
    esk = p.dma("scalar", esrow[:], p.tensors["l1f_es"], group="cst")

    xhat = [p.sb(f"l1f_xh{k}", [128, T], BF16) for k in range(16)]
    ln_toks, _ = emit_ln_fm(p, c, x_h, xhat, tag="l1f_ln", deps=deps,
                            a_b=c.scrb[0][:], b_b=c.scrb[1][:])
    xh_tok = ln_toks[-1]

    t1 = p.op("scalar", lambda E: E.activation(out=amp[:], in_=amp[:], func=AF.Square),
              deps=[adk], token=True)
    t2 = p.op("vector", lambda E: E.tensor_add(amp[:, 0:8], amp[:, 0:16:2], amp[:, 1:16:2]),
              deps=[t1], token=True)
    t3 = p.op("vector", lambda E: E.reduce_sum(amp[:, 8:9], amp[:, 0:8], axis=AX.X),
              deps=[t2], token=True)
    t4 = p.op("vector", lambda E: E.tensor_scalar(amp[:, 8:9], amp[:, 8:9], 1e-8, None,
                                                  ALU.add, ALU.bypass), deps=[t3], token=True)
    t5 = p.op("vector", lambda E: E.reciprocal(amp[:, 9:10], amp[:, 8:9]), deps=[t4], token=True)
    mix_tok = p.op("vector", lambda E: E.tensor_scalar(mixb[:], amp[:, 0:8], amp[:, 9:10],
                                                       None, ALU.mult, ALU.bypass),
                   deps=[t5], token=True)
    esc = p.op("vector", lambda E: E.tensor_copy(esrowb[:], esrow[:]), deps=[esk], token=True)
    esb_mm = p.op("tensor", lambda E: E.matmul(c.bank[6][:, 0:128], c.ones_bf[:],
                                               esrowb[:], start=True, stop=True),
                  deps=[esc, c.ones_tok, xh_tok], token=True)
    esb_tok = p.op("vector", lambda E: E.tensor_copy(es_b[:], c.bank[6][:, 0:128]),
                   deps=[esb_mm], token=True)
    if stages <= 1:
        return [esb_tok, mix_tok]

    getr, postr = emit_weight_stream(p, "l1f_wr", 1, 16, c=c, w_deps=[xh_tok],
                                     tag="l1fwr")
    wr, wr_tok = getr(0)
    tl_toks = []
    prev = {}
    for tl in range(16):
        b = c.bank[tl % 2]
        bq = c.bank[2 + tl % 2]
        dd = [wr_tok, xh_tok, esb_tok, prev.get(tl - 2)]
        for k in range(16):
            mt = p.op("tensor", lambda E, tl=tl, k=k, b=b: E.matmul(
                b[:, 0:128], xhat[k][:, tl * 128:(tl + 1) * 128],
                wr[:, k * 128:(k + 1) * 128],
                start=(k == 0), stop=(k == 15)),
                deps=dd if k == 0 else (), token=(k == 15))
        rm = p.op("vector", lambda E, b=b: E.reduce_max(nmx[:], b[:, 0:128], axis=AX.X,
                                                        negate=True), deps=[mt], token=True)
        ex = p.op("scalar", lambda E, b=b: E.activation(out=punb[:], in_=b[:, 0:128],
                                                        func=AF.Exp, bias=nmx[:]),
                  deps=[rm], token=True)
        rs = p.op("vector", lambda E: E.reduce_sum(den[:], punb[:], axis=AX.X),
                  deps=[ex], token=True)
        cp = p.op("vector", lambda E: E.tensor_copy(curp[:], punb[:]), deps=[ex], token=True)
        t4tok = cp
        for it in range(4):
            rmx = p.op("vector", lambda E: E.reduce_max(nmx[:], curp[:], axis=AX.X),
                       deps=[t4tok], token=True)
            if it < 3:
                mk = p.op("vector", lambda E: E.tensor_scalar(
                    msk8[:], curp[:], nmx[:], None, ALU.is_ge, ALU.bypass),
                    deps=[rmx], token=True)
                t4tok = p.op("vector", lambda E: E.copy_predicated(
                    curp[:], msk8[:], negbig_t[:]), deps=[mk, nbt], token=True)
            else:
                t4tok = rmx
        mk2 = p.op("vector", lambda E: E.tensor_scalar(
            mbf[:], punb[:], nmx[:], None, ALU.is_ge, ALU.bypass), deps=[t4tok], token=True)
        pt = p.op("vector", lambda E: E.tensor_mul(ptop[:], punb[:], mbf[:]),
                  deps=[mk2], token=True)
        st_ = p.op("vector", lambda E: E.reduce_sum(stp[:], ptop[:], axis=AX.X),
                   deps=[pt], token=True)
        de = p.op("vector", lambda E: E.scalar_tensor_tensor(
            dcol[:], den[:], 1e-8, stp[:], ALU.mult, ALU.add), deps=[st_, rs], token=True)
        rc = p.op("vector", lambda E: E.reciprocal(dcol[:], dcol[:]), deps=[de], token=True)
        c1 = p.op("vector", lambda E: E.tensor_scalar(ptop[:], ptop[:], dcol[:], None,
                                                      ALU.mult, ALU.bypass),
                  deps=[rc], token=True)
        c2 = p.op("vector", lambda E: E.tensor_mul(cbf[:], ptop[:], es_b[:]),
                  deps=[c1], token=True)
        trt = p.op("tensor", lambda E, bq=bq: E.transpose(
            bq.bitcast(BF16)[0:128, 0:128], cbf[:], c.ident_bf[:]),
            deps=[c2, c.ident_tok, prev.get(tl - 2)], token=True)
        ev = p.op("vector", lambda E, tl=tl, bq=bq: E.tensor_copy(
            Cfm[:, tl * 128:(tl + 1) * 128], bq.bitcast(BF16)[0:128, 0:128]),
            deps=[trt], token=True)
        prev[tl] = ev
        tl_toks.append(ev)
    postr(0, tl_toks[-1])
    cfm_tok = tl_toks[-1]
    if stages <= 2:
        return [cfm_tok]

    m_toks = []
    for tt in range(4):
        vb = c.bank[4 + tt]
        mm = p.op("tensor", lambda E, tt=tt, vb=vb: E.matmul(
            vb[0:8, :], mixb[:], Cfm[:, tt * 512:(tt + 1) * 512],
            start=True, stop=True), deps=[cfm_tok, mix_tok], token=True)
        m_toks.append(p.op("vector", lambda E, tt=tt, vb=vb: E.tensor_copy(
            m_sb[:, tt * 512:(tt + 1) * 512], vb[0:8, :]), deps=[mm], token=True))
    mdma = p.dma("gpsimd", m_scr, m_sb[:], deps=[m_toks[-1], m_toks[0], m_toks[1], m_toks[2]],
                 group="mscr")
    mb_toks = []
    last_bc = None
    for k in range(8):
        rd = p.dma("gpsimd", m_row[:], m_scr[k:k + 1, :], deps=[mdma, last_bc], group="mscr")
        for tt in range(4):
            b = c.bank[tt % 2]
            mm = p.op("tensor", lambda E, k=k, tt=tt, b=b: E.matmul(
                b[:], c.ones_bf[:], m_row[:, tt * 512:(tt + 1) * 512],
                start=True, stop=True),
                deps=[rd, mb_toks[-2] if len(mb_toks) >= 2 else None], token=True)
            last_bc = mm
            mb_toks.append(p.op("vector", lambda E, k=k, tt=tt, b=b: E.tensor_copy(
                M_b[k][:, tt * 512:(tt + 1) * 512], b[:]), deps=[mm], token=True))
    mb_tok = mb_toks[-1]
    if stages <= 3:
        return [mb_tok]

    acc = p.sb("l1f_acc", [128, T], F32)
    mixed_scr = p.dram("l1f_mixscr", [8, 128, T], BF16, kind="Internal")
    gscr = [p.sb(f"l1f_g{i}", [128, 1024], BF16) for i in range(2)]
    uscr = [p.sb(f"l1f_u{i}", [128, 1024], BF16) for i in range(2)]
    pscr = [p.sb(f"l1f_p{i}", [128, 1024], BF16) for i in range(2)]
    m_list = [k8 * 8 + j for j in range(8) for k8 in range(8)]
    getg, postg = emit_weight_stream(p, "l1f_wg", 64, 16, c=c, bufset=0,
                                     w_deps=[mb_tok], m_list=m_list, tag="l1fwg")
    getu, postu = emit_weight_stream(p, "l1f_wu", 64, 16, c=c, bufset=1,
                                     w_deps=[mb_tok], m_list=m_list, tag="l1fwu")
    evac_hist = {}
    cast_toks = {}
    mx_st = {}
    acc_tok = None
    for idx in range(64):
        j, k8 = idx // 8, idx % 8
        wg, wgt = getg(idx)
        wu, wut = getu(idx)
        for half in range(2):
            hb = 2 * (half % 2)
            gbank, ubank = c.bank2[hb], c.bank2[hb + 1]
            dd = [wgt, wut, mb_tok] + list(evac_hist.get((idx - 1, half), []))
            if idx == 0:
                dd.append(cfm_tok)
            for kk in range(16):
                for u in range(2):
                    tok_g = p.op("tensor", lambda E, kk=kk, u=u, half=half, gbank=gbank, wg=wg: E.matmul(
                        gbank[:, u * 512:(u + 1) * 512], wg[:, kk * 128:(kk + 1) * 128],
                        xhat[kk][:, half * 1024 + u * 512: half * 1024 + (u + 1) * 512],
                        start=(kk == 0), stop=(kk == 15)),
                        deps=dd if (kk == 0 and u == 0) else (), token=(kk == 15 and u == 1))
            for kk in range(16):
                for u in range(2):
                    tok_u = p.op("tensor", lambda E, kk=kk, u=u, half=half, ubank=ubank, wu=wu: E.matmul(
                        ubank[:, u * 512:(u + 1) * 512], wu[:, kk * 128:(kk + 1) * 128],
                        xhat[kk][:, half * 1024 + u * 512: half * 1024 + (u + 1) * 512],
                        start=(kk == 0), stop=(kk == 15)),
                        deps=(), token=(kk == 15 and u == 1))
            sg = p.op("scalar", lambda E, half=half, gbank=gbank: E.activation(
                out=gscr[half][:], in_=gbank[:], func=AF.Silu),
                deps=[tok_g, evac_hist.get((idx - 1, half), [None])[-1]], token=True)
            ue = p.op("vector", lambda E, half=half, ubank=ubank, k8=k8: E.tensor_mul(
                uscr[half][:], ubank[:], M_b[k8][:, half * 1024:(half + 1) * 1024]),
                deps=[tok_u, evac_hist.get((idx - 1, half), [None])[-1]], token=True)
            if k8 == 0:
                pr = p.op("vector", lambda E, half=half: E.tensor_mul(
                    acc[:, half * 1024:(half + 1) * 1024], gscr[half][:], uscr[half][:]),
                    deps=[sg, ue, cast_toks.get((j - 1, half))], token=True)
                at = pr
            else:
                pr = p.op("vector", lambda E, half=half: E.tensor_mul(
                    uscr[half][:], gscr[half][:], uscr[half][:]),
                    deps=[sg, ue, acc_tok], token=True)
                at = p.op("gpsimd", lambda E, half=half: E.tensor_add(
                    acc[:, half * 1024:(half + 1) * 1024],
                    acc[:, half * 1024:(half + 1) * 1024], uscr[half][:]),
                    deps=[pr], token=True)
            acc_tok = at
            evac_hist[(idx, half)] = [sg, ue, at]
            if k8 == 7:
                ct = p.op("vector", lambda E, j=j, half=half: E.tensor_copy(
                    pscr[half][:], acc[:, half * 1024:(half + 1) * 1024]),
                    deps=[at, mx_st.get((j - 1, half))], token=True)
                mx_st[(j, half)] = p.dma(
                    "gpsimd", mixed_scr[j, :, half * 1024:(half + 1) * 1024],
                    pscr[half][:], deps=[ct], group=f"mxs{half}")
                cast_toks[(j, half)] = ct
        postg(idx, tok_g)
        postu(idx, tok_u)
    mixed_tok = cast_toks[(7, 1)]
    mixed_st_tok = mx_st[(7, 1)]
    if stages <= 4:
        return [mixed_tok]

    psh_scr = p.dram("l1f_pshscr", [16, 128, T], BF16, kind="Internal")
    getsg, postsg = emit_weight_stream(p, "l1f_wsg", 16, 16, c=c, bufset=0,
                                       w_deps=[mixed_tok], tag="l1fwsg")
    getsu, postsu = emit_weight_stream(p, "l1f_wsu", 16, 16, c=c, bufset=1,
                                       w_deps=[mixed_tok], tag="l1fwsu")
    sh_hist = {}
    sh_toks = []
    psh_st = {}
    for idx in range(16):
        wg, wgt = getsg(idx)
        wu, wut = getsu(idx)
        for half in range(2):
            hb = 2 * (half % 2)
            gbank, ubank = c.bank2[hb], c.bank2[hb + 1]
            dd = [wgt, wut] + list(sh_hist.get((idx - 1, half), []))
            if idx == 0:
                dd.append(mixed_tok)
            for kk in range(16):
                for u in range(2):
                    tok_g = p.op("tensor", lambda E, kk=kk, u=u, half=half, gbank=gbank, wg=wg: E.matmul(
                        gbank[:, u * 512:(u + 1) * 512], wg[:, kk * 128:(kk + 1) * 128],
                        xhat[kk][:, half * 1024 + u * 512: half * 1024 + (u + 1) * 512],
                        start=(kk == 0), stop=(kk == 15)),
                        deps=dd if (kk == 0 and u == 0) else (), token=(kk == 15 and u == 1))
            for kk in range(16):
                for u in range(2):
                    tok_u = p.op("tensor", lambda E, kk=kk, u=u, half=half, ubank=ubank, wu=wu: E.matmul(
                        ubank[:, u * 512:(u + 1) * 512], wu[:, kk * 128:(kk + 1) * 128],
                        xhat[kk][:, half * 1024 + u * 512: half * 1024 + (u + 1) * 512],
                        start=(kk == 0), stop=(kk == 15)),
                        deps=(), token=(kk == 15 and u == 1))
            sg = p.op("scalar", lambda E, half=half, gbank=gbank: E.activation(
                out=gscr[half][:], in_=gbank[:], func=AF.Silu),
                deps=[tok_g, sh_hist.get((idx - 1, half), [None])[-1]], token=True)
            pe = p.op("vector", lambda E, idx=idx, half=half, ubank=ubank: E.tensor_mul(
                pscr[half][:], gscr[half][:], ubank[:]),
                deps=[sg, tok_u, psh_st.get((idx - 1, half), mx_st[(7, half)])],
                token=True)
            psh_st[(idx, half)] = p.dma(
                "gpsimd", psh_scr[idx, :, half * 1024:(half + 1) * 1024],
                pscr[half][:], deps=[pe], group=f"pshs{half}")
            sh_hist[(idx, half)] = [sg, pe]
            sh_toks.append(pe)
        postsg(idx, tok_g)
        postsu(idx, tok_u)
    psh_tok = sh_toks[-1]
    psh_st_tok = psh_st[(15, 1)]
    psh_st_tok0 = psh_st[(15, 0)]
    if stages <= 5:
        return [psh_tok]

    xst = [c.scrb[0].bitcast(F32)[:, 0:512], c.scrb[0].bitcast(F32)[:, 512:1024],
           c.scrb[1].bitcast(F32)[:, 0:512], c.scrb[1].bitcast(F32)[:, 512:1024]]
    st_toks = []
    ev_hist = []

    def out_cb(m, tt, ps, mm_tok):
        n = m * 4 + tt
        i = n % 2
        xt = p.dma("gpsimd", xst[i],
                   x_h[m * 128:(m + 1) * 128, tt * 512:(tt + 1) * 512],
                   deps=[ev_hist[n - 2]] if n >= 2 else [psh_tok],
                   group=f"xl{i}")
        et = p.op("vector", lambda E, m=m, tt=tt, ps=ps, i=i: E.tensor_add(
            xst[2 + i], ps, xst[i]),
            deps=[mm_tok, xt, st_toks[n - 2] if n >= 2 else None], token=True)
        ev_hist.append(et)
        st = p.dma("gpsimd", xout_h[m * 128:(m + 1) * 128, tt * 512:(tt + 1) * 512],
                   xst[2 + i], deps=[et], group=f"xs{i}")
        st_toks.append(st)
        return et

    sg_last = sh_hist[(15, 1)][0]
    rl_toks = []
    for j in range(8):
        rl_toks.append(p.dma("gpsimd", xhat[j][:], mixed_scr[j],
                             deps=[psh_tok, mixed_st_tok, mx_st[(7, 0)], sg_last],
                             group="mxl"))
    for j in range(16):
        dst = xhat[8 + j] if j < 8 else M_b[j - 8]
        rl_toks.append(p.dma("gpsimd", dst[:], psh_scr[j],
                             deps=[psh_st_tok, psh_st_tok0], group="mxl"))

    def dact(k):
        return xhat[k][:]

    def shact(k):
        return xhat[8 + k][:] if k < 8 else M_b[k - 8][:]

    emit_gemm_multi(
        p, [dict(name="l1f_wd", nk=8, act=dact, bufset=0, tag="l1fwd"),
            dict(name="l1f_wsd", nk=16, act=shact, bufset=1, tag="l1fwsd")],
        16, out_cb, banks=c.bank, act_tok=[psh_tok, sg_last, rl_toks[-1]], c=c,
        w_deps=[psh_tok])
    return st_toks



def l1attn_host(inp):
    H, HKV, DH, C = 16, 4, 128, 128
    S = 2048
    lam = np.exp2(-8.0 * np.arange(1, H + 1) / H).astype(np.float64)
    r = np.arange(C, dtype=np.float64)
    diff = r[None, :] - r[:, None]
    maskT = np.where(diff >= 0, np.exp(-lam[:, None, None] * diff), 0.0)
    qdec = np.exp(-lam[:, None] * (r[None, :] + 1.0))
    kdec = np.exp(-lam[:, None] * (C - 1.0 - r[None, :]))
    blk = np.exp(-lam * C)
    qdec_t = np.broadcast_to(qdec.reshape(1, H * C), (128, H * C))
    inv = 1.0 / 10000.0 ** (np.arange(0, DH, 2) / DH)
    f = np.arange(S)[:, None] * inv[None, :]
    cos = np.concatenate([np.cos(f), np.cos(f)], axis=1).T
    sin = np.concatenate([np.sin(f), np.sin(f)], axis=1).T
    sin_s = sin
    Sw = np.zeros((DH, DH), np.float32)
    for d in range(64):
        Sw[d + 64, d] = -1.0
        Sw[d, d + 64] = 1.0
    return dict(
        l1a_wq=pretile_lhs(inp["L1_attn_wq"], transpose=True),
        l1a_wk=pretile_lhs(inp["L1_attn_wk"], transpose=True),
        l1a_wv=pretile_lhs(inp["L1_attn_wv"], transpose=True),
        l1a_wo=pretile_lhs(inp["L1_attn_wo"], transpose=True),
        l1a_mask=bf16(maskT.transpose(1, 0, 2).reshape(128, H * C)),
        l1a_qdec=bf16(qdec_t),
        l1a_kdec=np.ascontiguousarray(kdec.T.astype(np.float32)),
        l1a_cos=bf16(cos),
        l1a_sin=bf16(sin_s),
        l1a_sw=bf16(Sw),
    ), blk.astype(np.float32)


def emit_slot_l1attn(p, c, x_h, xout_h, blk, deps=(), stages=9):
    T = 2048
    H, HKV, DH, NC = 16, 4, 128, 16
    masks = c.scr32[0].bitcast(BF16)
    qdec = c.scr32[1].bitcast(BF16)
    kdec = p.sb("l1a_kdecs", [128, H], F32)
    sw = p.sb("l1a_sws", [128, 128], BF16)
    kv = p.sb("l1a_kv", [128, H * 128], BF16)
    scT = p.sb("l1a_sct", [128, 128], BF16)
    qsc = p.sb("l1a_qsc", [128, 128], BF16)
    ktsc = p.sb("l1a_ktsc", [128, 128], BF16)
    vts = p.sb("l1a_vts", [128, 128], BF16)
    ctoks = [
        p.dma("scalar", kdec[:], p.tensors["l1a_kdec"], group="cst"),
        p.dma("scalar", sw[:], p.tensors["l1a_sw"], group="cst"),
    ]
    ctok = ctoks[-1]
    kv0 = p.op("vector", lambda E: E.memset(kv[:], 0.0), token=True)

    xhat = [p.sb(f"l1a_xh{k}", [128, T], BF16) for k in range(16)]
    ln_toks, _ = emit_ln_fm(p, c, x_h, xhat, tag="l1a_ln", deps=deps,
                            a_b=c.scrb[0][:], b_b=c.scrb[1][:])
    xh_tok = ln_toks[-1]

    cos_t = c.scrb[0]
    sin_t = c.scrb[1]
    ct1 = p.dma("scalar", cos_t[:], p.tensors["l1a_cos"], deps=[xh_tok], group="cst")
    ct2 = p.dma("scalar", sin_t[:], p.tensors["l1a_sin"], deps=[xh_tok], group="cst")

    q_t = [p.sb(f"l1a_q{m}", [128, T], BF16) for m in range(16)]
    k_t = [p.sb(f"l1a_k{m}", [128, T], BF16) for m in range(4)]
    v_fm = [c.scrb[2], c.scrb[3], c.wbfb[0][:, 0:T], c.wbfb[1][:, 0:T]]

    def q_cb(m, tt, ps, mm_tok):
        return p.op("vector", lambda E, m=m, tt=tt, ps=ps: E.tensor_copy(
            q_t[m][:, tt * 512:(tt + 1) * 512], ps), deps=[mm_tok], token=True)
    qtoks = emit_gemm_fm(p, lambda k: xhat[k][:], "l1a_wq", 16, 16, q_cb,
                         act_tok=xh_tok, banks=c.bank, tag="l1awq", c=c, bufset=0)

    def k_cb(m, tt, ps, mm_tok):
        return p.op("vector", lambda E, m=m, tt=tt, ps=ps: E.tensor_copy(
            k_t[m][:, tt * 512:(tt + 1) * 512], ps), deps=[mm_tok], token=True)
    ktoks = emit_gemm_fm(p, lambda k: xhat[k][:], "l1a_wk", 4, 16, k_cb,
                         act_tok=qtoks[-1], banks=c.bank, tag="l1awk", c=c, bufset=1,
                         w_deps=[qtoks[-1]])

    def v_cb(m, tt, ps, mm_tok):
        return p.op("vector", lambda E, m=m, tt=tt, ps=ps: E.tensor_copy(
            v_fm[m][:, tt * 512:(tt + 1) * 512], ps), deps=[mm_tok, xh_tok], token=True)
    vtoks = emit_gemm_fm(p, lambda k: xhat[k][:], "l1a_wv", 4, 16, v_cb,
                         act_tok=ktoks[-1], banks=c.bank, tag="l1awv", c=c, bufset=0,
                         w_deps=[ktoks[-1]])
    v_tok = vtoks[-1]

    rope_toks = []
    rp_prev = {}
    for ti, tile in enumerate(q_t + k_t):
        bb = c.bank2[ti % 2]
        t1s = c.scr32[0].bitcast(BF16)
        t2s = c.scr32[1].bitcast(BF16)
        dd = [v_tok, ct1, ct2, rp_prev.get(ti - 2)]
        mm4 = None
        for tt in range(4):
            mm4 = p.op("tensor", lambda E, tile=tile, tt=tt, bb=bb: E.matmul(
                bb[:, (tt % 2) * 512:(tt % 2) * 512 + 512], sw[:],
                tile[:, tt * 512:(tt + 1) * 512], start=True, stop=True),
                deps=dd if tt == 0 else [rope_toks[-1]] if tt >= 2 else [],
                token=True)
            h = tt % 2
            w1 = p.op("vector", lambda E, tile=tile, tt=tt, h=h: E.tensor_mul(
                t1s[:, h * 1024:h * 1024 + 512], tile[:, tt * 512:(tt + 1) * 512],
                cos_t[:, tt * 512:(tt + 1) * 512]), deps=[mm4], token=True)
            w2 = p.op("vector", lambda E, tile=tile, tt=tt, h=h, bb=bb: E.tensor_mul(
                t2s[:, h * 1024:h * 1024 + 512], bb[:, h * 512:h * 512 + 512],
                sin_t[:, tt * 512:(tt + 1) * 512]), deps=[w1], token=True)
            w3 = p.op("vector", lambda E, tile=tile, tt=tt, h=h: E.tensor_add(
                tile[:, tt * 512:(tt + 1) * 512], t1s[:, h * 1024:h * 1024 + 512],
                t2s[:, h * 1024:h * 1024 + 512]), deps=[w2], token=True)
            rope_toks.append(w3)
        rp_prev[ti] = rope_toks[-1]
    rope_tok = rope_toks[-1]

    mtok1 = p.dma("scalar", masks[:], p.tensors["l1a_mask"], deps=[rope_tok], group="cst")
    mtok = p.dma("scalar", qdec[:], p.tensors["l1a_qdec"], deps=[rope_tok], group="cst")
    if stages <= 3:
        return [rope_tok, mtok]

    o_t = xhat
    kv_toks = {}
    sc_free = rope_tok
    o1_prev = None
    qsc_free = None
    kt_free = None
    kc_prev = None
    vts_free = None
    last_kv = kv0
    last_o_ev = None
    for ci in range(NC):
        for kvh in range(4):
            bC = c.bank[4]
            bV = c.bank[5]
            ktr = p.op("tensor", lambda E, ci=ci, kvh=kvh, bC=bC: E.transpose(
                bC.bitcast(BF16)[0:128, 0:128], k_t[kvh][:, ci * 128:(ci + 1) * 128],
                c.ident_bf[:]),
                deps=[rope_tok, mtok, kt_free], token=True)
            vtr = p.op("tensor", lambda E, ci=ci, kvh=kvh, bV=bV: E.transpose(
                bV.bitcast(BF16)[0:128, 0:128], v_fm[kvh][:, ci * 128:(ci + 1) * 128],
                c.ident_bf[:]),
                deps=[v_tok, vts_free], token=True)
            vte = p.op("vector", lambda E, bV=bV: E.tensor_copy(
                vts[:], bV.bitcast(BF16)[0:128, 0:128]), deps=[vtr], token=True)
            vts_free = vte
            for hh in range(4):
                h = kvh * 4 + hh
                bA = c.bank[(h % 2)]
                bB = c.bank[2 + (h % 2)]
                bD = c.bank[6 + (h % 2)]
                sc_mm = p.op("tensor", lambda E, ci=ci, h=h, kvh=kvh, bA=bA: E.matmul(
                    bA[:, 0:128], k_t[kvh][:, ci * 128:(ci + 1) * 128],
                    q_t[h][:, ci * 128:(ci + 1) * 128], start=True, stop=True),
                    deps=[sc_free], token=True)
                scm = p.op("vector", lambda E, ci=ci, h=h, bA=bA: E.tensor_mul(
                    scT[:], bA[:, 0:128], masks[:, h * 128:(h + 1) * 128]),
                    deps=[sc_mm, mtok1, o1_prev], token=True)
                sc_free = scm
                qs = p.op("vector", lambda E, ci=ci, h=h: E.tensor_mul(
                    qsc[:], q_t[h][:, ci * 128:(ci + 1) * 128],
                    qdec[:, h * 128:(h + 1) * 128]), deps=[qsc_free, mtok], token=True)
                o1 = p.op("tensor", lambda E, ci=ci, h=h, kvh=kvh, bB=bB: E.matmul(
                    bB[:, 0:128], vts[:], scT[:],
                    start=True, stop=False), deps=[scm, vte, last_o_ev], token=True)
                o1_prev = o1
                o2 = p.op("tensor", lambda E, ci=ci, h=h, bB=bB: E.matmul(
                    bB[:, 0:128], kv[:, h * 128:(h + 1) * 128], qsc[:],
                    start=False, stop=True),
                    deps=[qs, kv_toks.get((ci - 1, h), kv0)], token=True)
                qsc_free = o2
                oe = p.op("scalar", lambda E, ci=ci, h=h, bB=bB: E.copy(
                    o_t[h][:, ci * 128:(ci + 1) * 128], bB[:, 0:128]),
                    deps=[o2, last_o_ev], token=True)
                last_o_ev = oe
                kts = p.op("vector", lambda E, h=h, bC=bC: E.tensor_scalar(
                    ktsc[:], bC.bitcast(BF16)[0:128, 0:128], kdec[:, h:h + 1], None,
                    ALU.mult, ALU.bypass), deps=[ktr, kc_prev], token=True)
                kc = p.op("tensor", lambda E, ci=ci, h=h, kvh=kvh, bD=bD: E.matmul(
                    bD[:, 0:128], ktsc[:], vts[:],
                    start=True, stop=True), deps=[kts, o2, last_kv], token=True)
                kt_free = kc
                kc_prev = kc
                vts_free = kc
                last_kv = p.op("vector", lambda E, h=h, bD=bD, bv=float(blk[h]): E.scalar_tensor_tensor(
                    kv[:, h * 128:(h + 1) * 128], kv[:, h * 128:(h + 1) * 128], bv,
                    bD[:, 0:128], ALU.mult, ALU.add), deps=[kc], token=True)
                kv_toks[(ci, h)] = last_kv
    attn_tok = last_o_ev
    if stages <= 4:
        return [attn_tok, kv_toks[(NC - 1, H - 1)]]

    xst = [c.scr32[0].bitcast(F32)[:, 0:512], c.scr32[0].bitcast(F32)[:, 512:1024],
           c.scr32[1].bitcast(F32)[:, 0:512], c.scr32[1].bitcast(F32)[:, 512:1024]]
    st_toks = []
    ev_hist = []
    kvlast = kv_toks[(NC - 1, H - 1)]

    def out_cb(m, tt, ps, mm_tok):
        n = m * 4 + tt
        i = n % 2
        xt = p.dma("gpsimd", xst[i],
                   x_h[m * 128:(m + 1) * 128, tt * 512:(tt + 1) * 512],
                   deps=[ev_hist[n - 2]] if n >= 2 else [attn_tok], group=f"xl{i}")
        et = p.op("vector", lambda E, m=m, tt=tt, ps=ps, i=i: E.tensor_add(
            xst[2 + i], ps, xst[i]),
            deps=[mm_tok, xt, st_toks[n - 2] if n >= 2 else None], token=True)
        ev_hist.append(et)
        st = p.dma("gpsimd", xout_h[m * 128:(m + 1) * 128, tt * 512:(tt + 1) * 512],
                   xst[2 + i], deps=[et], group=f"xs{i}")
        st_toks.append(st)
        return et

    emit_gemm_fm(p, lambda k: o_t[k][:], "l1a_wo", 16, 16, out_cb,
                 act_tok=[attn_tok, kvlast], banks=c.bank, tag="l1awo", c=c,
                 bufset=0, w_deps=[attn_tok, kvlast])
    return st_toks


D_INNER, N_HEADS, D_HEAD, D_STATE = 4096, 64, 64, 128
CONV_DIM = 4352
D_PROJ = 8512
CC = 128
NCH = 16


def l0attn_host(inp):
    W = inp["L0_attn_in_proj_w"]
    d = dict(
        l0a_inproj=pretile_lhs(W),
        l0a_outproj=pretile_lhs(inp["L0_attn_out_proj_w"]),
        l0a_gate=pretile_lhs(inp["L0_attn_gate_w"]),
        l0a_up=pretile_lhs(inp["L0_attn_up_w"]),
        l0a_down=pretile_lhs(inp["L0_attn_down_w"]),
        l0a_convw=np.ascontiguousarray(
            inp["L0_attn_conv_w"].reshape(34, 128, 4).transpose(1, 0, 2).reshape(128, 136)),
        l0a_convb=np.ascontiguousarray(inp["L0_attn_conv_b"].reshape(34, 128).T),
        l0a_ln1=np.ascontiguousarray(inp["L0_attn_ln1_w"].reshape(16, 128).T),
        l0a_ln2=np.ascontiguousarray(inp["L0_attn_ln2_w"].reshape(16, 128).T),
        l0a_gnorm=np.ascontiguousarray(inp["L0_attn_gnorm_w"].reshape(32, 128).T),
        l0a_Drep=np.ascontiguousarray(np.repeat(inp["L0_attn_D"], 64).reshape(32, 128).T),
        l0a_dtb=np.ascontiguousarray(inp["L0_attn_dt_bias"].reshape(64, 1)),
        l0a_alog=np.ascontiguousarray(inp["L0_attn_A_log"].reshape(64, 1)),
    )
    Exp = np.zeros((64, 32 * 128), np.float32)
    for j in range(32):
        for r in range(128):
            Exp[(128 * j + r) // 64, 128 * j + r] = 1.0
    d["l0a_expand"] = bf16(Exp)
    return d


def emit_slot_l0attn(p, c, x_h, xout_h, deps=(), stages=9):
    T = 2048
    nc = p.nc
    z_scr = p.dram("l0a_zscr", [32, 128, T], BF16, kind="Internal")
    xs_scr = p.dram("l0a_xsscr", [NCH, 32, 128, CC], BF16, kind="Internal")
    y_scr = p.dram("l0a_yscr", [32, 128, T], BF16, kind="Internal")
    g_scr = p.dram("l0a_gscr", [NCH, 64, CC], F32, kind="Internal")
    v1_scr = p.dram("l0a_v1scr", [2048, T], BF16, kind="Internal")
    pp_scr = p.dram("l0a_ppscr", [64, 128, T], BF16, kind="Internal")
    wdb_scr = p.dram("l0a_wdbscr", [16, 64, 128, 128], BF16, kind="Internal")

    convw = p.sb("l0a_convws", [128, 136], F32)
    convb = p.sb("l0a_convbs", [128, 34], F32)
    ln1c = p.sb("l0a_ln1s", [128, 16], F32)
    ln2c = p.sb("l0a_ln2s", [128, 16], F32)
    gnc = p.sb("l0a_gns", [128, 32], F32)
    Dc = p.sb("l0a_Ds", [128, 32], F32)
    dtbc = p.sb("l0a_dtbs", [64, 1], F32)
    alogc = p.sb("l0a_alogs", [64, 1], F32)
    expandc = p.sb("l0a_expands", [64, 4096], BF16)
    ctoks = [p.dma("scalar", t[:], p.tensors[n], group="cst") for t, n in [
        (convw, "l0a_convw"), (convb, "l0a_convb"), (ln1c, "l0a_ln1"),
        (ln2c, "l0a_ln2"), (gnc, "l0a_gnorm"), (Dc, "l0a_Drep"),
        (dtbc, "l0a_dtb"), (alogc, "l0a_alog"), (expandc, "l0a_expand")]]
    ctok = ctoks[-1]
    U32 = p.sb("l0a_U32", [128, 128], F32)
    Ubf = p.sb("l0a_Ubf", [128, 128], BF16)
    u0 = p.op("gpsimd", lambda E: E.iota(U32[:], pattern=[[1, 128]], base=0,
                                         channel_multiplier=-1,
                                         allow_small_or_imprecise_dtypes=True),
              token=True)
    u1 = p.op("vector", lambda E: E.tensor_single_scalar(U32[:], U32[:], 0.0, ALU.is_ge),
              deps=[u0], token=True)
    u2 = p.op("vector", lambda E: E.tensor_copy(Ubf[:], U32[:]), deps=[u1], token=True)

    h1 = [p.sb(f"l0a_h1_{k}", [128, T], BF16) for k in range(16)]
    a0r = c.wbfb[0][0:1, 0:T]
    b0r = c.wbfb[1][0:1, 0:T]
    ln_toks, lnr = emit_ln_fm(p, c, x_h, h1, tag="l0a_ln", extra_rms=True,
                              deps=deps, a_b=c.scrb[0][:], b_b=c.scrb[1][:],
                              plain_rows=(a0r, b0r))
    h1_tok = ln_toks[-1]
    if stages <= 0:
        return [h1_tok]

    dt_raw = p.sb("l0a_dtraw", [64, T], F32)
    dt_sp = p.sb("l0a_dtsp", [64, T], F32)
    dtA = dt_raw
    B_fm = p.sb("l0a_Bfm", [128, T], BF16)
    C_fm = p.sb("l0a_Cfm", [128, T], BF16)
    xbcb = [p.sb(f"l0a_xbc{i}", [128, 2051], BF16) for i in range(2)]
    cvb = [p.sb(f"l0a_cv{i}", [128, T], BF16) for i in range(2)]
    slb = [p.sb(f"l0a_sl{i}", [128, T], BF16) for i in range(2)]
    mz = [p.op("vector", lambda E, i=i: E.memset(xbcb[i][:, 0:3], 0.0), token=True)
          for i in range(2)]

    m_list = [66] + list(range(32, 66)) + list(range(32))
    st_state = {"conv": {}, "spill": {}, "sil": {}, "zs": {}, "dt": None}

    def ip_cb(m, tt, ps, mm_tok):
        if m == 66:
            t = p.op("vector", lambda E, tt=tt, ps=ps: E.tensor_copy(
                dt_raw[:, tt * 512:(tt + 1) * 512], ps[0:64, :]), deps=[mm_tok],
                token=True)
            st_state["dt"] = t
            return t
        if m >= 32:
            ch = m - 32
            i = ch % 2
            ev = p.op("vector", lambda E, ps=ps, tt=tt, i=i: E.tensor_copy(
                xbcb[i][:, 3 + tt * 512:3 + (tt + 1) * 512], ps),
                deps=[mm_tok, mz[i], st_state["conv"].get(ch - 2)], token=True)
            if tt < 3:
                return ev
            w4 = [convw[:, ch * 4 + j:ch * 4 + j + 1] for j in range(4)]
            t0 = p.op("vector", lambda E, i=i, w=w4[0]: E.tensor_scalar(
                cvb[i][:], xbcb[i][:, 0:2048], w, None, ALU.mult, ALU.bypass),
                deps=[ev, ctok, st_state["sil"].get(ch - 2)], token=True)
            for j in range(1, 4):
                t0 = p.op("vector", lambda E, i=i, j=j, w=w4[j]: E.scalar_tensor_tensor(
                    cvb[i][:], xbcb[i][:, j:j + 2048], w, cvb[i][:],
                    ALU.mult, ALU.add), deps=[t0], token=True)
            st_state["conv"][ch] = t0
            if ch < 32:
                sil = p.op("scalar", lambda E, i=i, ch=ch: E.activation(
                    out=slb[i][:], in_=cvb[i][:], func=AF.Silu,
                    bias=convb[:, ch:ch + 1]),
                    deps=[t0, st_state["spill"].get(ch - 2)], token=True)
                st_state["sil"][ch] = sil
                dst = bass.AP(xs_scr.tensor, ch * 16384,
                              [[128, 128], [32 * 16384, 16], [1, 128]])
                sp = p.dma("gpsimd", dst, slb[i][:], deps=[sil], group=f"xs{i}")
                st_state["spill"][ch] = sp
                return sp
            else:
                dstt = B_fm if ch == 32 else C_fm
                sil = p.op("scalar", lambda E, i=i, ch=ch, dstt=dstt: E.activation(
                    out=dstt[:], in_=cvb[i][:], func=AF.Silu,
                    bias=convb[:, ch:ch + 1]), deps=[t0], token=True)
                st_state["sil"][ch] = sil
                return sil
        ch = m
        i = ch % 2
        ev = p.op("vector", lambda E, ps=ps, tt=tt, i=i: E.tensor_copy(
            slb[i][:, tt * 512:(tt + 1) * 512], ps),
            deps=[mm_tok, st_state["zs"].get(ch - 2,
                  st_state["spill"].get(30 + ch))], token=True)
        if tt < 3:
            return ev
        zs = p.dma("gpsimd", z_scr[ch], slb[i][:], deps=[ev], group=f"zs{i}")
        st_state["zs"][ch] = zs
        return [ev, zs]

    emit_gemm_fm(p, lambda k: h1[k][:], "l0a_inproj", 67, 16, ip_cb,
                 act_tok=h1_tok, banks=c.bank, tag="l0aip", c=c, bufset=0,
                 m_list=m_list,
                 scale_col=lambda k: ln1c[:, k:k + 1], scale_tok=ctok,
                 scale_tile=ln1c[:])
    ip_tok = st_state["zs"][30]
    ip_tok2 = st_state["zs"][31]
    bc_sil = [st_state["sil"][32], st_state["sil"][33]]
    if stages <= 1:
        return [ip_tok, ip_tok2]

    d1 = p.op("vector", lambda E: E.tensor_scalar(
        dt_raw[:], dt_raw[:], dtbc[:], None, ALU.add, ALU.bypass),
        deps=[st_state["dt"], ctok], token=True)
    d2 = p.op("scalar", lambda E: E.activation(out=dt_sp[:], in_=dt_raw[:], func=AF.Abs),
              deps=[d1], token=True)
    d3 = p.op("scalar", lambda E: E.activation(out=dt_sp[:], in_=dt_sp[:], func=AF.Exp,
                                               scale=-1.0), deps=[d2], token=True)
    onec, onet = c.cc(1.0)
    d4 = p.op("scalar", lambda E: E.activation(out=dt_sp[:], in_=dt_sp[:], func=AF.Ln,
                                               bias=onec[0:64, :]), deps=[d3, onet], token=True)
    d5 = p.op("vector", lambda E: E.tensor_single_scalar(dt_raw[:], dt_raw[:], 0.0,
                                                         ALU.max), deps=[d4], token=True)
    d6 = p.op("vector", lambda E: E.tensor_add(dt_sp[:], dt_sp[:], dt_raw[:]),
              deps=[d5], token=True)
    a1 = p.op("scalar", lambda E: E.activation(out=alogc[:], in_=alogc[:], func=AF.Exp),
              deps=[ctok], token=True)
    a2 = p.op("vector", lambda E: E.tensor_scalar(alogc[:], alogc[:], -1.0, None,
                                                  ALU.mult, ALU.bypass), deps=[a1], token=True)
    d7 = p.op("vector", lambda E: E.tensor_scalar(dtA[:], dt_sp[:], alogc[:], None,
                                                  ALU.mult, ALU.bypass), deps=[d6, a2],
              token=True)
    dt_tok = d7
    if stages <= 2:
        return [dt_tok]

    g_sb = None
    gcs = p.sb("l0a_gcs", [64, 128], F32)
    dtat = p.sb("l0a_dtat", [128, 64], F32)
    g_toks = []
    prevev = None
    prevgs = None
    for ci in range(NCH):
        b5 = c.bank[5]
        b6 = c.bank[6]
        tr = p.op("tensor", lambda E, ci=ci, b5=b5: E.transpose(
            b5[0:128, 0:64],
            dtA[0:64, ci * 128:(ci + 1) * 128], c.ident[0:64, 0:64]),
            deps=[dt_tok, u2, prevev], token=True)
        te = p.op("vector", lambda E, b5=b5: E.tensor_copy(dtat[:], b5[0:128, 0:64]),
                  deps=[tr], token=True)
        gm = p.op("tensor", lambda E, ci=ci, b6=b6: E.matmul(
            b6[0:64, 0:128], dtat[:, 0:64], U32[:], start=True, stop=True),
            deps=[te], token=True)
        ge = p.op("vector", lambda E, ci=ci, b6=b6: E.tensor_copy(
            gcs[:], b6[0:64, 0:128]), deps=[gm, prevgs], token=True)
        gs = p.dma("gpsimd", g_scr[ci], gcs[:], deps=[ge], group="gsp")
        prevgs = gs
        prevev = ge
        g_toks.append(gs)
    g_tok = g_toks[-1]
    if stages <= 3:
        return [g_tok]
    xs_sp = [st_state["spill"][30], st_state["spill"][31]]
    ns = NS()
    g_toks_all = g_toks
    for nm in ["z_scr", "xs_scr", "y_scr", "g_scr", "v1_scr", "pp_scr",
               "wdb_scr", "gnc", "Dc", "ln2c", "expandc", "U32", "Ubf",
               "dt_sp", "dtA", "B_fm", "C_fm", "g_sb", "dtat", "g_tok",
               "ip_tok", "ip_tok2", "bc_sil", "ctok", "u2", "h1", "lnr",
               "xs_sp", "x_h", "a0r", "b0r"]:
        setattr(ns, nm, locals()[nm])
    ns.g_toks = g_toks_all
    return scan_and_rest(p, c, x_h, xout_h, ns, stages)


def scan_and_rest(p, c, x_h, xout_h, ns, stages):
    T = 2048
    h1 = ns.h1
    WT = lambda j: h1[j // 4][:, (j % 4) * 512:(j % 4) * 512 + 512]
    WTh = lambda h: h1[h // 16][:, (h % 16) * 128:(h % 16) * 128 + 128]
    Xdt = lambda j: h1[4 + j // 4][:, (j % 4) * 512:(j % 4) * 512 + 512]
    Zsl = lambda j: h1[6 + j // 4][:, (j % 4) * 512:(j % 4) * 512 + 512]
    Xdt64 = lambda q: h1[4 + q // 32][:, (q % 32) * 64:(q % 32) * 64 + 64]
    Sbf = lambda j: h1[8 + j // 4][:, (j % 4) * 512:(j % 4) * 512 + 512]
    Sbf128 = lambda j: h1[8 + j // 16][:, (j % 16) * 128:(j % 16) * 128 + 128]
    xsb = [h1[10:12], h1[12:14]]
    xs_pc = lambda b, ch: xsb[b][ch // 16][:, (ch % 16) * 128:(ch % 16) * 128 + 128]
    egx_t = h1[14]
    y1b = h1[15]

    S_T = p.sb("l0a_ST", [128, 4096], F32)
    gc = p.sb("l0a_gc", [64, 128], F32)
    gt_c = p.sb("l0a_gtc", [128, 64], F32)
    dtt_c = p.sb("l0a_dttc", [128, 64], F32)
    w2t = p.sb("l0a_w2t", [128, 64], F32)
    w2f = p.sb("l0a_w2f", [64, 128], F32)
    Egbf = p.sb("l0a_egbf", [64, 128], BF16)
    Eglb = p.sb("l0a_eglb", [128, 64], BF16)
    eglr = p.sb("l0a_eglr", [1, 64], BF16)
    Gm = p.sb("l0a_gm", [128, 128], BF16)
    btm = p.sb("l0a_btm", [128, 128], BF16)
    Vb = [p.sb(f"l0a_vb{i}", [128, 512], F32) for i in range(2)]
    e0bt = p.sb("l0a_e0b", [128, 512], BF16)
    e0b = [e0bt, e0bt]
    tmpf = p.sb("l0a_tmpf", [128, 512], F32)

    sz = p.op("vector", lambda E: E.memset(S_T[:], 0.0), token=True)
    sz2 = p.op("vector", lambda E: E.memset(h1[8][:], 0.0), deps=[ns.ip_tok], token=True)
    sz3 = p.op("vector", lambda E: E.memset(h1[9][:], 0.0), deps=[ns.ip_tok], token=True)

    epsc, epst = c.cc(1e-5)
    prev = {}
    yspill = {}
    last_yg = []
    last_state = [sz, sz2, sz3]
    for ci in range(NCH):
        cs = slice(ci * 128, (ci + 1) * 128)
        bb = ci % 2
        xld0 = p.dma("gpsimd", xsb[bb][0][:],
                     bass.AP(ns.xs_scr.tensor, ci * 32 * 16384,
                             [[128, 128], [16384, 16], [1, 128]]),
                     deps=[prev.get(("xsfree", bb))] + (ns.xs_sp if ci < 2 else []),
                     group=f"xsl{bb}")
        xld1 = p.dma("gpsimd", xsb[bb][1][:],
                     bass.AP(ns.xs_scr.tensor, ci * 32 * 16384 + 16 * 16384,
                             [[128, 128], [16384, 16], [1, 128]]),
                     deps=[], group=f"xsl{bb}")
        gs_tok = ns.g_toks[ci]
        gld = p.dma("gpsimd", gc[:], ns.g_scr[ci], deps=[gs_tok, prev.get("gfree")],
                    group="gld")
        glcol = gc[:, 127:128]
        w1 = p.op("vector", lambda E: E.tensor_scalar(w2f[:], gc[:], glcol, None,
                                                      ALU.subtract, ALU.bypass),
                  deps=[gld, prev.get("w2free")], token=True)
        w2 = p.op("scalar", lambda E: E.activation(out=w2f[:], in_=w2f[:], func=AF.Exp,
                                                   scale=-1.0), deps=[w1], token=True)
        eg = p.op("scalar", lambda E: E.activation(out=Egbf[:], in_=gc[:], func=AF.Exp),
                  deps=[gld, prev.get("egfree")], token=True)
        b4 = c.bank2[2]
        trw = p.op("tensor", lambda E, b4=b4: E.transpose(
            b4[0:128, 0:64], w2f[:], c.ident[0:64, 0:64]),
            deps=[w2, prev.get("b4free"), c.ident_tok], token=True)
        trg = p.op("tensor", lambda E, b4=b4: E.transpose(
            b4[0:128, 256:320], gc[:], c.ident[0:64, 0:64]), deps=[gld, trw], token=True)
        trd = p.op("tensor", lambda E, b4=b4, cs=cs: E.transpose(
            b4[0:128, 512:576], ns.dt_sp[0:64, cs], c.ident[0:64, 0:64]),
            deps=[trg], token=True)
        ew = p.op("vector", lambda E, b4=b4: E.tensor_copy(w2t[:], b4[0:128, 0:64]),
                  deps=[trw], token=True)
        eg2 = p.op("vector", lambda E, b4=b4: E.tensor_copy(gt_c[:], b4[0:128, 256:320]),
                   deps=[trg], token=True)
        ed = p.op("vector", lambda E, b4=b4: E.tensor_copy(dtt_c[:], b4[0:128, 512:576]),
                  deps=[trd], token=True)
        b4free = ed
        e1 = p.op("scalar", lambda E: E.activation(out=w2f[:, 0:1], in_=glcol,
                                                   func=AF.Exp), deps=[ew, gld], token=True)
        tr2 = p.op("tensor", lambda E, b4=b4: E.transpose(
            b4[0:1, 768:832], w2f[:, 0:1], c.ident[0:64, 0:64]), deps=[e1, ed], token=True)
        e2 = p.op("vector", lambda E, b4=b4: E.tensor_copy(eglr[:], b4[0:1, 768:832]),
                  deps=[tr2], token=True)
        e3 = p.op("tensor", lambda E, b4=b4: E.matmul(
            b4[:, 768:832], c.ones_bf[:],
            eglr[:], start=True, stop=True), deps=[e2, c.ones_tok], token=True)
        e4 = p.op("vector", lambda E, b4=b4: E.tensor_copy(Eglb[:], b4[:, 768:832]),
                  deps=[e3], token=True)
        b4free2 = e4
        bg = c.bank2[3]
        gmm = p.op("tensor", lambda E, bg=bg, cs=cs: E.matmul(
            bg[:, 0:128], ns.B_fm[:, cs], ns.C_fm[:, cs], start=True, stop=True),
            deps=[ns.bc_sil[0], ns.bc_sil[1], prev.get("bgfree"), ed], token=True)
        gmask = p.op("vector", lambda E, bg=bg: E.tensor_mul(
            Gm[:], bg[:, 0:128], ns.Ubf[:]), deps=[gmm, ns.u2, prev.get("gmfree")],
            token=True)
        trb = p.op("tensor", lambda E, bg=bg, cs=cs: E.transpose(
            bg.bitcast(BF16)[0:128, 256:384], ns.B_fm[:, cs], c.ident_bf[:]),
            deps=[gmask], token=True)
        eb = p.op("vector", lambda E, bg=bg: E.tensor_copy(
            btm[:], bg.bitcast(BF16)[0:128, 256:384]), deps=[trb], token=True)
        bgfree = eb
        xdt_toks = []
        for q in range(8):
            bx = c.bank2[q % 2]
            trts = []
            for s4 in range(4):
                ch = q * 4 + s4
                trts.append(p.op("tensor", lambda E, bx=bx, s4=s4, ch=ch, bb=bb: E.transpose(
                    bx.bitcast(BF16)[0:128, s4 * 128:(s4 + 1) * 128],
                    xs_pc(bb, ch), c.ident_bf[:]),
                    deps=([xld0, xld1, prev.get(("xdtp", q % 2))] if s4 == 0 else []),
                    token=(s4 == 3)))
            dtv = dtt_c[:, q * 8:(q + 1) * 8].unsqueeze(2).broadcast_to([128, 8, 64])
            xt = p.op("vector", lambda E, q=q, bx=bx, dtv=dtv: E.tensor_tensor(
                Xdt(q).rearrange("p (h x) -> p h x", h=8),
                bx.bitcast(BF16)[0:128, 0:512].rearrange("p (h x) -> p h x", h=8),
                dtv, ALU.mult), deps=[trts[-1], ed], token=True)
            prev[("xdtp", q % 2)] = xt
            w2v = w2t[:, q * 8:(q + 1) * 8].unsqueeze(2).broadcast_to([128, 8, 64])
            zt = p.op("gpsimd", lambda E, q=q, w2v=w2v: E.tensor_tensor(
                Zsl(q).rearrange("p (h x) -> p h x", h=8),
                Xdt(q).rearrange("p (h x) -> p h x", h=8),
                w2v, ALU.mult), deps=[xt, ew], token=True)
            xdt_toks.append(zt)
        xdt_tok = xdt_toks[-1]
        wt_toks = []
        s1_toks = {}
        s3_toks = {}
        for j in range(16):
            i2 = j % 2
            vd = p.dma("sync", Vb[i2][:], ns.g_scr[ci].rearrange("h x -> (h x)")
                       [j * 512:(j + 1) * 512].unsqueeze(0).broadcast_to([128, 512]),
                       deps=[gs_tok if j == 0 else None,
                             s1_toks.get(j - 2, prev.get("vfree"))],
                       group=f"vbc{i2}")
            gv = gt_c[:, 4 * j:4 * (j + 1)].unsqueeze(2).broadcast_to([128, 4, 128])
            s1 = p.op("vector", lambda E, j=j, i2=i2, gv=gv: E.scalar_tensor_tensor(
                e0b[i2][:].rearrange("p (h x) -> p h x", h=4),
                Vb[i2][:].rearrange("p (h x) -> p h x", h=4), 1.0, gv,
                ALU.mult, ALU.subtract),
                deps=[vd, eg2, s3_toks.get(j - 1)], token=True)
            s1_toks[j] = s1
            s2 = p.op("vector", lambda E, j=j, i2=i2: E.tensor_single_scalar(
                e0b[i2][:], e0b[i2][:], 0.0, ALU.min), deps=[s1], token=True)
            s3 = p.op("scalar", lambda E, j=j, i2=i2: E.activation(
                out=WT(j), in_=e0b[i2][:], func=AF.Exp),
                deps=[s2, prev.get(("wtfree", 0)) if j == 0 else None],
                token=True)
            s3_toks[j] = s3
            gmv = Gm[:].unsqueeze(1).broadcast_to([128, 4, 128])
            s4_ = p.op("gpsimd", lambda E, j=j, gmv=gmv: E.tensor_tensor(
                WT(j).rearrange("p (h x) -> p h x", h=4),
                WT(j).rearrange("p (h x) -> p h x", h=4), gmv, ALU.mult),
                deps=[s3, gmask], token=True)
            wt_toks.append(s4_)
        wt_tok = wt_toks[-1]
        prev["vfree"] = s1_toks[15]
        ygrp_toks = []
        for grp in range(4):
            bi = c.bank2[0]
            bn = c.bank2[1]
            be = c.bank2[2]
            ddw = [wt_toks[4 * grp + 3], xdt_tok, b4free2] + list(last_yg)
            intra_tok = None
            for sub in range(8):
                j = grp * 8 + sub
                h0 = 2 * j
                for hh in range(2):
                    h = h0 + hh
                    intra_tok = p.op("tensor", lambda E, j=j, h=h, hh=hh, sub=sub, bi=bi: E.matmul(
                        bi[64 * hh:64 * hh + 64, sub * 128:(sub + 1) * 128],
                        Xdt64(h), WTh(h),
                        start=True, stop=True, tile_position=(0, 64 * hh)),
                        deps=ddw if (sub == 0 and hh == 0) else (),
                        token=(sub == 7 and hh == 1))
            inter_tok = None
            for sub in range(8):
                j = grp * 8 + sub
                inter_tok = p.op("tensor", lambda E, j=j, sub=sub, bn=bn, cs=cs: E.matmul(
                    bn[:, sub * 128:(sub + 1) * 128], Sbf128(j), ns.C_fm[:, cs],
                    start=True, stop=True),
                    deps=[last_state[0]] if sub == 0 else (), token=(sub == 7))
            for sub in range(8):
                j = grp * 8 + sub
                tok = p.op("tensor", lambda E, j=j, sub=sub, be=be: E.matmul(
                    be[:, sub * 128:(sub + 1) * 128],
                    ns.expandc[:, j * 128:(j + 1) * 128], Egbf[:],
                    start=True, stop=True),
                    deps=[eg] if sub == 0 else (), token=(sub == 7))
            mme = tok
            ex = p.op("scalar", lambda E, be=be: E.copy(egx_t[:, 0:1024], be[:]),
                      deps=[mme], token=True)
            ty1 = p.op("vector", lambda E, bn=bn: E.tensor_mul(
                y1b[:, 0:1024], bn[:], egx_t[:, 0:1024]), deps=[ex, prev.get("yf")],
                token=True)
            y2s = []
            for sub in range(8):
                j = grp * 8 + sub
                xpiece = xs_pc(bb, j)
                y2s.append(p.op("vector", lambda E, j=j, sub=sub, bi=bi, xpiece=xpiece: E.scalar_tensor_tensor(
                    y1b[:, 1024 + sub * 128:1024 + (sub + 1) * 128], xpiece,
                    ns.Dc[:, j:j + 1],
                    bi[:, sub * 128:(sub + 1) * 128], ALU.mult, ALU.add),
                    deps=[intra_tok if sub == 0 else None, ty1], token=(sub == 7)))
            yf = p.op("gpsimd", lambda E: E.tensor_add(
                egx_t[:, 1024:2048], y1b[:, 0:1024], y1b[:, 1024:2048]),
                deps=[y2s[-1], ty1] + prev.get("ysp", []), token=True)
            prev["yf"] = yf
            for half in range(2):
                dst = bass.AP(ns.y_scr.tensor, (grp * 8 + half * 4) * 128 * T + ci * 128,
                              [[2048, 128], [128 * T, 4], [1, 128]])
                yspill[(ci, grp, half)] = p.dma(
                    "gpsimd", dst, egx_t[:, 1024 + half * 512:1024 + (half + 1) * 512],
                    deps=[yf], group=f"ysp{half}")
            prev["ysp"] = [yspill[(ci, grp, 0)], yspill[(ci, grp, 1)]]
            ygrp_toks.append([ty1, y2s[-1], yf, ex])
            last_yg = [ex, ty1, y2s[-1], inter_tok, yf]
        st_toks = []
        for sg in range(2):
            bs0 = c.bank2[0]
            bs1 = c.bank2[1]
            contrib_tok = None
            for sl4 in range(4):
                sl = sg * 4 + sl4
                bsx = bs0 if sl4 < 2 else bs1
                off = (sl4 % 2) * 512
                contrib_tok = p.op("tensor", lambda E, sl=sl, bsx=bsx, off=off: E.matmul(
                    bsx[:, off:off + 512], btm[:], Zsl(sl), start=True, stop=True),
                    deps=(([eb, xdt_tok] + list(ygrp_toks[-1])) if sg == 0 else
                          [st_toks[-1]]) if sl4 == 0 else (),
                    token=(sl4 == 3))
            for sl4 in range(4):
                sl = sg * 4 + sl4
                bsx = bs0 if sl4 < 2 else bs1
                off = (sl4 % 2) * 512
                egv = Eglb[:, sl * 4:(sl + 1) * 4]
                egv = egv.unsqueeze(2).broadcast_to([128, 4, 128])
                t1_ = p.op("gpsimd", lambda E, sl=sl, egv=egv: E.tensor_tensor(
                    tmpf[:].rearrange("p (h x) -> p h x", h=4),
                    S_T[:, sl * 512:(sl + 1) * 512].rearrange("p (h x) -> p h x", h=4),
                    egv, ALU.mult), deps=[e4, st_toks[-1] if st_toks else None,
                                          contrib_tok], token=True)
                t2_ = p.op("vector", lambda E, sl=sl, bsx=bsx, off=off: E.tensor_add(
                    S_T[:, sl * 512:(sl + 1) * 512], tmpf[:], bsx[:, off:off + 512]),
                    deps=[t1_], token=True)
                t3_ = p.op("vector", lambda E, sl=sl: E.tensor_copy(
                    Sbf(sl), S_T[:, sl * 512:(sl + 1) * 512]), deps=[t2_], token=True)
                st_toks.append(t3_)
        last_state = [st_toks[-1], st_toks[-2], st_toks[-3]]
        prev[("xsfree", bb)] = st_toks[-1]
        prev["gfree"] = e4
        prev["w2free"] = zt
        prev["egfree"] = mme
        prev["b4free"] = b4free
        prev["gmfree"] = s4_
        prev["bgfree"] = bgfree
        prev[("wtfree", 0)] = intra_tok
    scan_tok = st_toks[-1]
    ylast = [yspill[(NCH - 1, g, h)] for g in range(4) for h in range(2)]
    if stages <= 4:
        return [scan_tok] + ylast
    return gating_and_ffn(p, c, x_h, xout_h, ns, scan_tok, ylast, stages)


def gating_and_ffn(p, c, x_h, xout_h, ns, scan_tok, ylast, stages):
    T = 2048
    h1 = ns.h1
    epsc, epst = c.cc(1e-5)
    moA_scr = p.dram("l0a_moascr", [16, 128, T], BF16, kind="Internal")

    ybs = [c.scrb[2], p.tensors["l0a_cv0"]]
    zbs = [c.scrb[3], p.tensors["l0a_cv1"]]
    sq_mm_tok = None
    prev_sp = {}
    mm_toks_g = {}
    sq_toks_g = {}
    for ch in range(32):
        i = ch % 2
        yb, zb = ybs[i], zbs[i]
        yld = p.dma("gpsimd", yb[:], ns.y_scr[ch],
                    deps=[prev_sp.get(ch - 2)] + (ylast if ch == 0 else []),
                    group=f"g1y{i}")
        zld = p.dma("gpsimd", zb[:], ns.z_scr[ch],
                    deps=([ns.ip_tok, ns.ip_tok2, scan_tok] if ch < 2 else
                          [mm_toks_g.get(ch - 2)]),
                    group=f"g1z{i}")
        sl = p.op("scalar", lambda E, zb=zb: E.activation(out=zb[:], in_=zb[:], func=AF.Silu),
                  deps=[zld], token=True)
        g1 = p.op("vector", lambda E, yb=yb, zb=zb: E.tensor_mul(yb[:], yb[:], zb[:]),
                  deps=[yld, sl], token=True)
        sq = p.op("vector", lambda E, yb=yb, zb=zb: E.tensor_mul(zb[:], yb[:], yb[:]),
                  deps=[g1], token=True)
        sq_toks_g[ch] = sq
        for tt in range(4):
            sq_mm_tok = p.op("tensor", lambda E, tt=tt, ch=ch, zb=zb: E.matmul(
                c.bank[tt][0:1, :], c.ones_col[:], zb[:, tt * 512:(tt + 1) * 512],
                start=(ch == 0), stop=(ch == 31)),
                deps=[sq, c.ones_tok] if tt == 0 else (),
                token=(tt == 3))
        mm_toks_g[ch] = sq_mm_tok
        prev_sp[ch] = p.dma("gpsimd", ns.y_scr[ch], yb[:], deps=[sq], group=f"g1sp{i}")
    g1_tok = prev_sp[31]
    g1_tok0 = prev_sp[30]
    rstd_b = c.scrb[0]
    rb_toks = []
    for tt in range(4):
        pc = c.bank[tt][0:1, :]
        r1 = p.op("scalar", lambda E, pc=pc: E.mul(pc, pc, 1.0 / 4096.0),
                  deps=[sq_mm_tok], token=True)
        r2 = p.op("scalar", lambda E, pc=pc: E.activation(out=pc, in_=pc, func=AF.Ln,
                                                          bias=epsc[0:1, :]),
                  deps=[r1, epst], token=True)
        r3 = p.op("scalar", lambda E, pc=pc: E.activation(out=pc, in_=pc, func=AF.Exp,
                                                          scale=-0.5), deps=[r2], token=True)
        rb_toks.append(p.op("scalar", lambda E, pc=pc, tt=tt: E.copy(
            c.rowbf[0:1, tt * 512:(tt + 1) * 512], pc),
            deps=[r3, g1_tok, g1_tok0], token=True))
    ta = emit_bcast_row(p, c, c.rowbf[0:1, :],
                        [rstd_b[:, tt * 512:(tt + 1) * 512] for tt in range(4)],
                        deps=rb_toks, banks=(4, 5))
    rstd_tok = ta[-1]

    a0b_t, b0b_t = None, None
    v1st = []
    for rnd in range(2):
        lds = []
        for kk in range(16):
            ch = rnd * 16 + kk
            lds.append(p.dma("gpsimd", h1[kk][:], ns.y_scr[ch],
                             deps=([g1_tok, g1_tok0] if rnd == 0 else
                                   [moA_tok, moA_tok2])
                             if kk == 0 else [],
                             group="ygl"))
        ap_toks = []
        for kk in range(16):
            ap_toks.append(p.op("vector", lambda E, kk=kk: E.tensor_mul(
                h1[kk][:], h1[kk][:], rstd_b[:]), deps=[lds[-1], rstd_tok], token=True))
        yg_tok = ap_toks[-1]
        if rnd == 0:
            mo_st = {}

            def cbA(m, tt, ps, mm_tok):
                n = m * 4 + tt
                i = n % 2
                ev = p.op("vector", lambda E, ps=ps, i=i, tt=tt: E.tensor_copy(
                    c.scr32[i].bitcast(BF16)[:, 0:512], ps),
                    deps=[mm_tok, mo_st.get(n - 2)], token=True)
                mo_st[n] = p.dma("gpsimd", moA_scr[m, :, tt * 512:(tt + 1) * 512],
                                 c.scr32[i].bitcast(BF16)[:, 0:512], deps=[ev],
                                 group=f"moa{i}")
                return ev

            emit_gemm_fm(p, lambda k: h1[k][:], "l0a_outproj", 16, 16, cbA,
                         act_tok=yg_tok, banks=c.bank, tag="l0aopA", c=c, bufset=0,
                         w_deps=[yg_tok], scale_col=lambda k: ns.gnc[:, k:k + 1],
                         scale_tok=ns.ctok, scale_tile=ns.gnc[:])
            moA_tok = mo_st[63]
            moA_tok2 = mo_st[62]
        else:
            a0b_t = p.tensors["l0a_cv0"]
            b0b_t = p.tensors["l0a_cv1"]
            t1 = emit_bcast_row(p, c, ns.a0r[0:1, :],
                                [a0b_t[:, tt * 512:(tt + 1) * 512] for tt in range(4)],
                                deps=[yg_tok, moA_tok, moA_tok2], banks=(4, 5))
            t2 = emit_bcast_row(p, c, ns.b0r[0:1, :],
                                [b0b_t[:, tt * 512:(tt + 1) * 512] for tt in range(4)],
                                deps=[t1[-1]], banks=(4, 5))
            ab_tok = t2[-1]
            xst = [c.scr32[0][:, 0:512], c.scr32[0][:, 512:1024]]
            most = [c.scr32[1].bitcast(BF16)[:, 0:512], c.scr32[1].bitcast(BF16)[:, 512:1024]]
            u1t = [c.scrb[2][:, 0:512], c.scrb[2][:, 512:1024]]
            v1b = [c.scrb[3][:, 0:512], c.scrb[3][:, 512:1024]]
            ev_h = {}

            def cbB(m, tt, ps, mm_tok):
                n = m * 4 + tt
                i = n % 2
                xt = p.dma("gpsimd", xst[i], x_h[m * 128:(m + 1) * 128,
                                                 tt * 512:(tt + 1) * 512],
                           deps=[ev_h.get(n - 2, ab_tok)], group=f"xl{i}")
                mt = p.dma("gpsimd", most[i], moA_scr[m, :, tt * 512:(tt + 1) * 512],
                           deps=[ev_h.get(n - 2, moA_tok)], group=f"mol{i}")
                w1 = p.op("vector", lambda E, i=i: E.tensor_mul(
                    u1t[i], xst[i], a0b_t[:, 0:512]),
                    deps=[xt, ab_tok, v1st[n - 2] if n >= 2 else None], token=True)
                w2 = p.op("gpsimd", lambda E, i=i: E.tensor_add(
                    u1t[i], u1t[i], b0b_t[:, 0:512]), deps=[w1], token=True)
                w3 = p.op("gpsimd", lambda E, i=i: E.tensor_add(
                    u1t[i], u1t[i], most[i]), deps=[w2, mt], token=True)
                w4 = p.op("vector", lambda E, i=i, ps=ps: E.tensor_add(
                    v1b[i], ps, u1t[i]), deps=[w3, mm_tok], token=True)
                ev_h[n] = w4
                v1st.append(p.dma("gpsimd", ns.v1_scr[m * 128:(m + 1) * 128,
                                                      tt * 512:(tt + 1) * 512],
                                  v1b[i], deps=[w4], group=f"v1s{i}"))
                return w4

            emit_gemm_fm(p, lambda k: h1[k][:], "l0a_outproj", 16, 16, cbB,
                         act_tok=[yg_tok, ab_tok], banks=c.bank, tag="l0aopB", c=c,
                         bufset=1, w_deps=[yg_tok, ab_tok], k_off=16,
                         scale_col=lambda k: ns.gnc[:, k + 16:k + 17],
                         scale_tok=ns.ctok, scale_tile=ns.gnc[:])
    v1_tok = v1st[-1]
    v1_tok0 = v1st[-2]
    if stages <= 5:
        return [v1_tok, v1_tok0]
    return swiglu(p, c, x_h, xout_h, ns, [v1_tok, v1_tok0], stages)


def swiglu(p, c, x_h, xout_h, ns, v1_toks, stages):
    T = 2048
    h1 = ns.h1
    epsc, epst = c.cc(1e-5)
    v1ls = [c.scrb[2], p.tensors["l0a_cv0"]]
    sqbs = [c.scrb[3], p.tensors["l0a_cv1"]]
    smm = None
    last_sq = None
    mm_toks_r = {}
    sq_toks_r = {}
    for kk in range(16):
        i = kk % 2
        v1l, sqb = v1ls[i], sqbs[i]
        ld = p.dma("gpsimd", v1l[:], ns.v1_scr[kk * 128:(kk + 1) * 128, :],
                   deps=[sq_toks_r.get(kk - 2)] + (v1_toks if kk < 2 else []),
                   group=f"v1l{i}")
        sq = p.op("vector", lambda E, v1l=v1l, sqb=sqb: E.tensor_mul(sqb[:], v1l[:], v1l[:]),
                  deps=[ld, mm_toks_r.get(kk - 2)], token=True)
        sq_toks_r[kk] = sq
        for tt in range(4):
            smm = p.op("tensor", lambda E, tt=tt, kk=kk, sqb=sqb: E.matmul(
                c.bank[tt][0:1, :], c.ones_col[:], sqb[:, tt * 512:(tt + 1) * 512],
                start=(kk == 0), stop=(kk == 15)),
                deps=[sq] if tt == 0 else (), token=(tt == 3))
        mm_toks_r[kk] = smm
        last_sq = smm
    rb_toks = []
    for tt in range(4):
        pc = c.bank[tt][0:1, :]
        r1 = p.op("scalar", lambda E, pc=pc: E.mul(pc, pc, 1.0 / 2048.0),
                  deps=[smm], token=True)
        r2 = p.op("scalar", lambda E, pc=pc: E.activation(out=pc, in_=pc, func=AF.Ln,
                                                          bias=epsc[0:1, :]),
                  deps=[r1, epst], token=True)
        r3 = p.op("scalar", lambda E, pc=pc: E.activation(out=pc, in_=pc, func=AF.Exp,
                                                          scale=-0.5), deps=[r2], token=True)
        rb_toks.append(p.op("scalar", lambda E, pc=pc, tt=tt: E.copy(
            c.rowbf[0:1, tt * 512:(tt + 1) * 512], pc), deps=[r3, last_sq], token=True))
    rstd2_b = c.scrb[0]
    ta = emit_bcast_row(p, c, c.rowbf[0:1, :],
                        [rstd2_b[:, tt * 512:(tt + 1) * 512] for tt in range(4)],
                        deps=rb_toks, banks=(4, 5))
    rstd2_tok = ta[-1]
    h2_toks = {}
    for kk in range(16):
        i = kk % 2
        v1l = v1ls[i]
        ld = p.dma("gpsimd", v1l[:], ns.v1_scr[kk * 128:(kk + 1) * 128, :],
                   deps=[rstd2_tok if kk < 2 else h2_toks.get(kk - 2)],
                   group=f"v1l{i}")
        h2_toks[kk] = p.op("vector", lambda E, kk=kk, v1l=v1l: E.tensor_mul(
            h1[kk][:], v1l[:], rstd2_b[:]), deps=[ld, rstd2_tok], token=True)
    h2_tok = h2_toks[15]
    if stages <= 6:
        return [h2_tok]

    wst = p.tensors["l0a_ST"]
    cvt_toks = []
    last_cv = None
    for m in range(16):
        for half in range(2):
            dms = []
            for sub in range(2):
                k0 = half * 32 + sub * 16
                src_ = bass.AP(p.tensors["l0a_down"].tensor,
                               (m * 64 + k0) * 16384,
                               [[128, 128], [16384, 16], [1, 128]])
                dms.append(p.dma("sync", wst[:, sub * 2048:(sub + 1) * 2048], src_,
                                 deps=[last_cv, h2_tok] if sub == 0 else [],
                                 group=f"wdc{half}"))
            cv = p.op("vector", lambda E, m=m, half=half: E.tensor_copy(
                p.tensors["l0a_cv0"][:], wst[:, 0:2048]), deps=[dms[-1]], token=True)
            cv2 = p.op("vector", lambda E, m=m, half=half: E.tensor_copy(
                p.tensors["l0a_cv1"][:], wst[:, 2048:4096]), deps=[dms[-1]], token=True)
            st1 = p.dma("gpsimd", bass.AP(ns.wdb_scr.tensor,
                        m * 64 * 16384 + (half * 32) * 16384,
                        [[128, 128], [16384, 16], [1, 128]]),
                        p.tensors["l0a_cv0"][:], deps=[cv], group="wds")
            st2 = p.dma("gpsimd", bass.AP(ns.wdb_scr.tensor,
                        m * 64 * 16384 + (half * 32 + 16) * 16384,
                        [[128, 128], [16384, 16], [1, 128]]),
                        p.tensors["l0a_cv1"][:], deps=[cv2], group="wds")
            last_cv = st2
            cvt_toks += [st1, st2]
    wdb_tok = cvt_toks[-1]

    gsl = [c.scrb[2][:, 0:1024], c.scrb[2][:, 1024:2048]]
    psl = [c.scrb[3][:, 0:1024], c.scrb[3][:, 1024:2048]]
    getg, postg = emit_weight_stream(p, "l0a_gate", 64, 16, c=c, bufset=0,
                                     w_deps=[h2_tok], tag="l0ag",
                                     scale_col=lambda k: p.tensors["l0a_ln2s"][:, k:k + 1],
                                     scale_tok=ns.ctok,
                                     scale_tile=p.tensors["l0a_ln2s"][:])
    getu, postu = emit_weight_stream(p, "l0a_up", 64, 16, c=c, bufset=1,
                                     w_deps=[h2_tok], tag="l0au",
                                     scale_col=lambda k: p.tensors["l0a_ln2s"][:, k:k + 1],
                                     scale_tok=ns.ctok,
                                     scale_tile=p.tensors["l0a_ln2s"][:])
    ev_hist = {}
    pp_st = {}
    for idx in range(64):
        wg, wgt = getg(idx)
        wu, wut = getu(idx)
        for half in range(2):
            hb = 2 * (half % 2)
            gbank, ubank = c.bank2[hb], c.bank2[hb + 1]
            dd = [wgt, wut, h2_tok] + list(ev_hist.get((idx - 1, half), []))
            for kk in range(16):
                for u in range(2):
                    tok_g = p.op("tensor", lambda E, kk=kk, u=u, half=half, gbank=gbank, wg=wg: E.matmul(
                        gbank[:, u * 512:(u + 1) * 512], wg[:, kk * 128:(kk + 1) * 128],
                        h1[kk][:, half * 1024 + u * 512: half * 1024 + (u + 1) * 512],
                        start=(kk == 0), stop=(kk == 15)),
                        deps=dd if (kk == 0 and u == 0) else (), token=(kk == 15 and u == 1))
            for kk in range(16):
                for u in range(2):
                    tok_u = p.op("tensor", lambda E, kk=kk, u=u, half=half, ubank=ubank, wu=wu: E.matmul(
                        ubank[:, u * 512:(u + 1) * 512], wu[:, kk * 128:(kk + 1) * 128],
                        h1[kk][:, half * 1024 + u * 512: half * 1024 + (u + 1) * 512],
                        start=(kk == 0), stop=(kk == 15)),
                        deps=(), token=(kk == 15 and u == 1))
            sg = p.op("scalar", lambda E, half=half, gbank=gbank: E.activation(
                out=gsl[half], in_=gbank[:], func=AF.Silu),
                deps=[tok_g, ev_hist.get((idx - 1, half), [None])[-1]], token=True)
            pe = p.op("vector", lambda E, half=half, ubank=ubank: E.tensor_mul(
                psl[half], gsl[half], ubank[:]),
                deps=[sg, tok_u, pp_st.get((idx - 1, half))], token=True)
            pp_st[(idx, half)] = p.dma("gpsimd",
                                       ns.pp_scr[idx, :, half * 1024:(half + 1) * 1024],
                                       psl[half], deps=[pe], group=f"pps{half}")
            ev_hist[(idx, half)] = [sg, pe]
        postg(idx, tok_g)
        postu(idx, tok_u)
    pp_tok = pp_st[(63, 1)]
    pp_tok0 = pp_st[(63, 0)]
    if stages <= 7:
        return [pp_tok, pp_tok0]

    wdA = p.tensors["l0a_ST"].bitcast(BF16)
    wdB = [p.tensors["l0a_Bfm"], p.tensors["l0a_Cfm"],
           p.tensors["l0a_sl0"], p.tensors["l0a_sl1"]]

    ptg = lambda k: h1[k // 4][:, (k % 4) * 512:(k % 4) * 512 + 512]
    xs1 = [c.scr32[0][:, 0:512], c.scr32[0][:, 512:1024]]
    v1s = [c.scr32[1].bitcast(BF16)[:, 0:512], c.scr32[1].bitcast(BF16)[:, 512:1024]]
    ost = [c.scrb[2].bitcast(F32)[:, 0:512], c.scrb[2].bitcast(F32)[:, 512:1024]]
    out_st = []
    wl_toks = {}
    ptg_toks = {}
    ev_fin = {}
    last_mm = {}
    for tg in range(4):
        pl = None
        for k in range(64):
            pl = p.dma("gpsimd", ptg(k), ns.pp_scr[k, :, tg * 512:(tg + 1) * 512],
                       deps=([pp_tok, pp_tok0, ev_fin.get(tg - 1)] if k == 0 else []),
                       group="ptl")
            ptg_toks[(tg, k)] = pl
        for mp in range(8):
            bp = c.bank2[mp % 2]
            for h_ in range(2):
                for msub in range(2):
                    mm_ = 2 * mp + msub
                    src_ = bass.AP(ns.wdb_scr.tensor,
                                   (mm_ * 64 + h_ * 32) * 16384,
                                   [[128, 128], [16384, 32], [1, 128]])
                    if h_ == 0:
                        dst = wdA[:, msub * 4096:(msub + 1) * 4096]
                        wl_toks[(tg, mp, h_, msub)] = p.dma(
                            "sync", dst, src_,
                            deps=[wdb_tok, last_mm.get((tg, mp - 1, h_),
                                  last_mm.get((tg - 1, 7, h_)))] if msub == 0 else [],
                            group=f"wdl{h_}")
                    else:
                        for t2 in range(2):
                            src2 = bass.AP(ns.wdb_scr.tensor,
                                           (mm_ * 64 + h_ * 32 + t2 * 16) * 16384,
                                           [[128, 128], [16384, 16], [1, 128]])
                            wl_toks[(tg, mp, h_, msub)] = p.dma(
                                "sync", wdB[msub * 2 + t2][:], src2,
                                deps=[wdb_tok, last_mm.get((tg, mp - 1, h_),
                                      last_mm.get((tg - 1, 7, h_)))]
                                if (msub == 0 and t2 == 0) else [],
                                group=f"wdl{h_}")
            mmt = None
            for h_ in range(2):
                for sl in range(64):
                    kl = sl // 2
                    k = h_ * 32 + kl
                    msub = sl % 2
                    if h_ == 0:
                        src = wdA[:, (msub * 32 + kl) * 128:(msub * 32 + kl + 1) * 128]
                    else:
                        s_ = msub * 32 + kl
                        src = wdB[s_ // 16][:, (s_ % 16) * 128:(s_ % 16) * 128 + 128]
                    mmt = p.op("tensor", lambda E, src=src, msub=msub, bp=bp, k=k, sl=sl: E.matmul(
                        bp[:, msub * 512:(msub + 1) * 512], src, ptg(k),
                        start=(k == 0 and sl < 2),
                        stop=(k == 63 and sl >= 62)),
                        deps=[wl_toks[(tg, mp, h_, 0)], wl_toks[(tg, mp, h_, 1)], pl,
                              ev_fin.get((tg, mp - 2))] if sl == 0 else (),
                        token=(sl == 63))
                last_mm[(tg, mp, h_)] = mmt
            evs = []
            for msub in range(2):
                m = 2 * mp + msub
                i = msub
                xt = p.dma("gpsimd", xs1[i], x_h[m * 128:(m + 1) * 128,
                                                 tg * 512:(tg + 1) * 512],
                           deps=[ev_fin.get((tg, mp - 1), mmt)], group=f"xf{i}")
                vt = p.dma("gpsimd", v1s[i],
                           ns.v1_scr[m * 128:(m + 1) * 128, tg * 512:(tg + 1) * 512],
                           deps=[], group=f"vf{i}")
                e1 = p.op("vector", lambda E, i=i, bp=bp, msub=msub: E.tensor_add(
                    ost[i], bp[:, msub * 512:(msub + 1) * 512], v1s[i]),
                    deps=[mmt, vt, out_st[-2] if len(out_st) >= 2 else None], token=True)
                e2 = p.op("vector", lambda E, i=i: E.tensor_add(
                    ost[i], ost[i], xs1[i]), deps=[e1, xt], token=True)
                evs.append(e2)
                out_st.append(p.dma("gpsimd",
                                    xout_h[m * 128:(m + 1) * 128, tg * 512:(tg + 1) * 512],
                                    ost[i], deps=[e2], group=f"of{i}"))
            ev_fin[(tg, mp)] = evs[-1]
        ev_fin[tg] = evs[-1]
    return out_st






def _host_all(inputs):
    h = {}
    h.update(l0attn_host(inputs))
    h.update(l0ffn_host(inputs))
    h1a, blk = l1attn_host(inputs)
    h.update(h1a)
    h.update(l1ffn_host(inputs))
    return h, blk


def _build_fused(host, blk):
    p = P()
    x_h = p.dram("x", [2048, 2048])
    x1_h = p.dram("x1s", [2048, 2048], F32, kind="Internal")
    x2_h = p.dram("x2s", [2048, 2048], F32, kind="Internal")
    x3_h = p.dram("x3s", [2048, 2048], F32, kind="Internal")
    xo_h = p.dram("xo", [2048, 2048], F32, kind="ExternalOutput")
    for k, v in host.items():
        p.dram(k, list(v.shape), F32 if v.dtype == np.float32 else BF16)
    c = common_setup(p)
    p.push_scope()
    t = emit_slot_l0attn(p, c, x_h, x1_h)
    p.pop_scope()
    p.barrier()
    p.push_scope()
    t = emit_slot_l0ffn(p, c, x1_h, x2_h)
    p.pop_scope()
    p.barrier()
    p.push_scope()
    t = emit_slot_l1attn(p, c, x2_h, x3_h, blk)
    p.pop_scope()
    p.barrier()
    p.push_scope()
    t = emit_slot_l1ffn(p, c, x3_h, xo_h)
    p.pop_scope()
    p.op("gpsimd", lambda E: E.engine_nop(), deps=[t[-1], t[-2]])
    return p.build()


def kernel(**inputs):
    host, blk = _host_all(inputs)
    x = np.asarray(inputs["L0_attn_hidden_states"])
    shards = [np.ascontiguousarray(x[b].T) for b in range(8)]
    nc = _build_fused(host, blk)
    in_maps = []
    for b in range(8):
        m = dict(host)
        m["x"] = shards[b]
        in_maps.append(m)
    res = run_bass_kernel_spmd(nc, in_maps, core_ids=list(range(8)))
    out = np.stack([res.results[b]["xo"].T for b in range(8)], axis=0)
    return np.ascontiguousarray(out.astype(np.float32))
